# version 28
# baseline (speedup 1.0000x reference)
import numpy as np
import ml_dtypes
import concourse.bass as bass
import concourse.mybir as mybir
from concourse.bass_utils import run_bass_kernel_spmd

F32 = mybir.dt.float32
BF16 = mybir.dt.bfloat16
I32 = mybir.dt.int32
FP8 = mybir.dt.float8e4
AF = mybir.ActivationFunctionType
ALU = mybir.AluOpType
AX = mybir.AxisListType
BFNP = ml_dtypes.bfloat16

B, S, E = 4, 4096, 1024
H, KVH, D = 16, 4, 64
QL, KVL = 64, 32
HID = 4096
TOK = 2048
NSLOT = 4
SLOT = 512
NKT = S // 128
EPS = 1e-5
KILL = -1.0e30
SCALE = 0.125

JSETS = [[7, 5, 2, 0], [6, 4, 3, 1]]
LPAD = [28, 20, 12, 4]
LOFF = [0, 28, 48, 60]
NL = 64
DPERM = np.concatenate([np.arange(0, 64, 2), np.arange(1, 64, 2)])


class Prog:
    ENGS = ("PE", "ACT", "DVE", "SP", "GP")

    def __init__(self):
        self.items = {e: [] for e in self.ENGS}
        self.cnt = {e: 0 for e in self.ENGS}
        self.dcnt = {}
        self.last_w = {}
        self.group = {}
        self.readers = {}
        self.observed = {e: {} for e in self.ENGS}
        self.fences = {}

    def _want(self, eng, waits, key, val):
        if self.observed[eng].get(key, 0) >= val:
            return
        if waits.get(key, 0) < val:
            waits[key] = val

    def fence(self, tag):
        deps = {}
        for e in self.ENGS:
            if self.cnt[e]:
                deps[("e", e)] = self.cnt[e]
        for s_, v_ in self.dcnt.items():
            deps[("d", s_)] = v_
        self.fences[tag] = deps

    def _emit(self, eng, fn, reads, writes, group, is_dma, fence, sem=None):
        waits = {}
        if fence is not None:
            for k_, v_ in self.fences[fence].items():
                self._want(eng, waits, k_, v_)
        for r in reads:
            for k_, v_ in self.last_w.get(r, {}).items():
                self._want(eng, waits, k_, v_)
        for w in writes:
            same_group = group is not None and self.group.get(w) == group
            if not same_group:
                for k_, v_ in self.last_w.get(w, {}).items():
                    self._want(eng, waits, k_, v_)
            for k_, v_ in self.readers.get(w, {}).items():
                self._want(eng, waits, k_, v_)
        if is_dma:
            prev = self.dcnt.get(sem, 0)
            if prev:
                self._want(eng, waits, ("d", sem), prev)
            self.dcnt[sem] = prev + 16
            me = (("d", sem), self.dcnt[sem])
        else:
            self.cnt[eng] += 1
            me = (("e", eng), self.cnt[eng])
        for r in reads:
            self.readers.setdefault(r, {})[me[0]] = me[1]
        for w in writes:
            same_group = group is not None and self.group.get(w) == group
            if same_group:
                self.last_w.setdefault(w, {})[me[0]] = me[1]
            else:
                self.last_w[w] = {me[0]: me[1]}
            self.readers[w] = {}
            self.group[w] = group
        obs = self.observed[eng]
        for k_, v_ in waits.items():
            if obs.get(k_, 0) < v_:
                obs[k_] = v_
        self.items[eng].append((sorted(waits.items()), fn,
                                sem if is_dma else None))

    def op(self, eng, fn, reads=(), writes=(), group=None, fence=None):
        import sys
        fn._line = sys._getframe(2).f_lineno
        self._emit(eng, fn, reads, writes, group, False, fence)

    def dma(self, eng, fn, reads=(), writes=(), fence=None, sem=None):
        self._emit(eng, fn, reads, writes, None, True, fence, sem=sem)


def build_program():
    nc = bass.Bass("TRN2", target_bir_lowering=False)
    p = Prog()

    def din(name, shape, dt=F32):
        return nc.dram_tensor(name, list(shape), dt, kind="ExternalInput").ap()

    xT_d = din("xT", (E, S))
    xoT_d = din("xoT", (E, TOK))
    wqa_d = din("wqa", (E, QL))
    wqbp_d = din("wqbp", (QL, E))
    wqbpB_d = din("wqbpB", (QL, E))
    wkva_d = din("wkva", (E, KVL))
    wkvbk_d = din("wkvbk", (KVL, 256))
    wkvbkB_d = din("wkvbkB", (KVL, 256))
    wkvbv_d = din("wkvbv", (KVL, 256))
    woutr_d = din("woutr", (128, 8 * E))
    bout_d = din("bout_c", (128, 8))
    b2_d = din("b2_c", (128, 8))
    b1row_d = din("b1row", (1, HID))
    ln1g_d = din("ln1g", (1, E))
    ln1b_d = din("ln1b", (1, E))
    ln2g_d = din("ln2g", (1, HID))
    ln2b_d = din("ln2b", (1, HID))
    w1T_d = din("w1T", (E, HID))
    w2T_d = din("w2T", (HID, E))
    csf_d = din("cs_full", (128, S), BF16)
    scf_d = din("sc_full", (128, S), BF16)
    cso_d = din("cs_own", (128, TOK), BF16)
    sco_d = din("sc_own", (128, TOK), BF16)
    tri_d = din("tri", (128, 128), BF16)
    kill_d = din("killrow", (1, NL))
    idb_d = din("idb", (128, 128), BF16)
    idf_d = din("idf", (128, 128), F32)
    selx_d = din("selx", (128, 64), BF16)
    selxo_d = din("selxo", (128, 16), BF16)
    selr8_d = din("selr8", (8, 256))
    selr4_d = din("selr4", (4, 256))
    selrl_d = din("selrl", (16, 1024))
    halfsel_d = din("halfsel", (128, 2))

    yT_d = nc.dram_tensor("yT", [E, TOK], F32, kind="ExternalOutput").ap()
    tern1_dr = nc.dram_tensor("tern1_scratch", [8 * 128, HID], FP8).ap()
    tern1b = tern1_dr.rearrange("(hb p) x -> hb p x", p=128)
    t1dr = tuple(f"t1d{i}" for i in range(8))

    ctxs = {}

    def sbuf(name, shape, dt):
        c = nc.sbuf_tensor(name, list(shape), dt)
        h = c.__enter__()
        ctxs[name] = c
        return h

    def sfree(*names):
        for n in names:
            ctxs.pop(n).__exit__(None, None, None)

    w2t = sbuf("w2t", (128, 32 * E), FP8)
    ksb = [sbuf(f"Pksb{g}", (128, S), BF16) for g in range(2)]
    vsb = sbuf("Pvsb", (128, NKT * 260), BF16)
    qr = sbuf("Pqr", (128, 8 * SLOT), BF16)
    qlln = sbuf("Pqlln", (64, TOK), BF16)
    ylnown = sbuf("Pylnown", (32, TOK), BF16)
    g1t = sbuf("Pg1t", (128, E), BF16)
    b1t = sbuf("Pb1t", (128, E), BF16)
    g2t = sbuf("Pg2t", (128, HID), BF16)
    b2t = sbuf("Pb2t", (128, HID), BF16)
    k3rhs = sbuf("Pk3rhs", (3, HID), BF16)
    k3lhs = sbuf("Pk3lhs", (3, 256), BF16)
    c2pair = sbuf("Pc2pair", (2, E), BF16)
    n128 = sbuf("Pn128", (2, 512), BF16)
    tri = sbuf("Ptri", (128, 128), BF16)
    idb = sbuf("Pidb", (128, 128), BF16)
    idf = sbuf("Pidf", (128, 128), F32)
    selrl = sbuf("Pselrl", (16, 1024), BF16)
    halfsel = sbuf("Phalfsel", (128, 2), F32)
    bout_s = sbuf("Pbout", (128, 8), F32)
    b2_s = sbuf("Pb2", (128, 8), F32)
    biastab = sbuf("Pbiastab", (128, NL), F32)
    negb = sbuf("Pnegb", (128, 1), F32)
    killrow = sbuf("Pkill", (1, NL), F32)
    ones_c = sbuf("Ponesc", (128, 1), F32)
    ones_cb = sbuf("Ponescb", (128, 1), BF16)
    ones_r = sbuf("Ponesr", (1, 128), F32)
    epscol = sbuf("Pepscol", (128, 1), F32)
    wsc = sbuf("Pwsc", (128, 8), F32)
    small = sbuf("Psmall", (128, 64), F32)
    small2 = sbuf("Psmall2", (2, 2), F32)
    rowtmp = sbuf("Prowtmp", (1, 8), F32)
    bnb = sbuf("Pbnb", (128, 104), F32)
    wqc = sbuf("Pwqc", (128, 8 * QL), BF16)
    wkvc = sbuf("Pwkvc", (128, 8 * KVL), BF16)
    wqb = sbuf("Pwqb", (64, E), BF16)
    wqbB = sbuf("PwqbB", (64, E), BF16)
    wkvbk = sbuf("Pwkvbk", (32, 256), BF16)
    wkvbkB = sbuf("PwkvbkB", (32, 256), BF16)
    wkvbv = sbuf("Pwkvbv", (32, 256), BF16)
    pt = [sbuf(f"Ppt{i}", (128, 1024), BF16) for i in range(2)]
    w1slab = [sbuf(f"Pw1s{i}", (128, 8 * 512), FP8) for i in range(2)]
    rch = pt[0][:, 0:512]
    rt1 = pt[0][:, 512:1024]
    rt2 = pt[1][:, 0:512]

    ysb = sbuf("1ysb", (32, S), BF16)
    qlp = sbuf("1qlp", (64, TOK), BF16)
    yor = sbuf("1yor", (32, TOK), BF16)
    stat = sbuf("1stat", (8, 1024), F32)
    stato = sbuf("1stato", (4, 1024), F32)
    xbf = [sbuf(f"1xbf{i}", (128, 8 * 512), BF16) for i in range(2)]
    x2b = sbuf("1x2b", (128, 8 * 512), BF16)
    csfb = sbuf("1csfb", (128, 512), BF16)
    scfb = sbuf("1scfb", (128, 512), BF16)


    accw = sbuf("0accw", (128, 80), F32)
    wqaf = sbuf("0wqaf", (128, 8 * QL), BF16)
    wkvaf = sbuf("0wkvaf", (128, 8 * KVL), BF16)
    selx = sbuf("0selx", (128, 64), BF16)
    selxo = sbuf("0selxo", (128, 16), BF16)
    selr8 = sbuf("0selr8", (8, 256), F32)
    selr4 = sbuf("0selr4", (4, 256), F32)

    psSl = nc.psum_tensor("psSl", [128, 1024], F32).__enter__()
    psSu = nc.psum_tensor("psSu", [128, 1024], F32).__enter__()
    psO = nc.psum_tensor("psO", [128, 1024], F32).__enter__()
    psG = [nc.psum_tensor(f"psG{i}", [128, 512], F32).__enter__() for i in range(2)]
    psbfv = psSl[:].bitcast(BF16)
    SL0, SL1, SU0, SU1 = "psSl0", "psSl1", "psSu0", "psSu1"
    OL_, OU_, G0, G1 = "psO_l", "psO_u", "psG0", "psG1"
    PSC1 = [(psSl[:, 0:512], SL0), (psSl[:, 512:1024], SL1),
            (psSu[:, 0:512], SU0), (psSu[:, 512:1024], SU1),
            (psO[:, 0:512], OL_), (psO[:, 512:1024], OU_),
            (psG[0][:, 0:512], G0), (psG[1][:, 0:512], G1)]

    sems = {}
    for e in Prog.ENGS:
        sems[("e", e)] = nc.semaphore(f"sem_{e}").__enter__()

    def dsem(name):
        key = ("d", name)
        if key not in sems:
            sems[key] = nc.semaphore(f"ds_{name}").__enter__()
        return sems[key]

    def mm(out_ap, lhsT, rhs, reads, writes, group=None, start=True, stop=True,
           tile_position=None):
        p.op("PE", lambda eng: eng.matmul(out_ap, lhsT, rhs, start=start,
                                          stop=stop, tile_position=tile_position),
             reads=reads, writes=writes, group=group)

    def tr(out_ap, in_ap, ident, reads, writes, group=None):
        p.op("PE", lambda eng: eng.transpose(out_ap, in_ap, ident),
             reads=reads, writes=writes, group=group)

    def act(out_ap, in_ap, func, reads, writes, bias=None, scale=None,
            accum_out=None, fence=None):
        kw = {}
        if bias is not None:
            kw["bias"] = bias
        if scale is not None:
            kw["scale"] = scale
        if accum_out is not None:
            kw["accum_out"] = accum_out
        p.op("ACT", lambda eng: eng.activation(out_ap, in_ap, func, **kw),
             reads=reads, writes=writes, fence=fence)

    def v(meth, *args, reads=(), writes=(), fence=None, **kw):
        p.op("DVE", lambda eng: getattr(eng, meth)(*args, **kw),
             reads=reads, writes=writes, fence=fence)

    def g(meth, *args, reads=(), writes=(), fence=None, **kw):
        p.op("GP", lambda eng: getattr(eng, meth)(*args, **kw),
             reads=reads, writes=writes, fence=fence)

    def dma(out_ap, in_ap, reads, writes, q="SP", fence=None, sem="ld0"):
        p.dma(q, lambda eng: eng.dma_start(out=out_ap, in_=in_ap),
              reads=reads, writes=writes, fence=fence, sem=sem)

    ones_row = ones_r[:]

    for dst, src, tag in [
        (tri[:], tri_d[:], "tri"), (idb[:], idb_d[:], "idb"),
        (idf[:], idf_d[:], "idf"), (halfsel[:], halfsel_d[:], "halfsel"),
        (bout_s[:], bout_d[:], "bout"), (b2_s[:], b2_d[:], "b2"),
        (killrow[:], kill_d[:], "kill"),
        (selr8[:], selr8_d[:], "selr8"), (selr4[:], selr4_d[:], "selr4"),
    ]:
        dma(dst, src, reads=(), writes=(tag,))
    dma(wqaf[:].rearrange("p (c r) -> p c r", r=QL),
        wqa_d.rearrange("(c p) r -> p c r", p=128), reads=(), writes=("wqaf",),
        q="GP", sem="ldg")
    dma(wkvaf[:].rearrange("p (c r) -> p c r", r=KVL),
        wkva_d.rearrange("(c p) r -> p c r", p=128), reads=(), writes=("wkvaf",),
        q="GP", sem="ldg")
    for dst, src, tag in [
        (wqb[:], wqbp_d[:], "wqb"), (wkvbk[:], wkvbk_d[:], "wkvbk"),
        (wqbB[:], wqbpB_d[:], "wqbB"), (wkvbkB[:], wkvbkB_d[:], "wkvbkB"),
        (wkvbv[:], wkvbv_d[:], "wkvbv"), (selx[:], selx_d[:], "selx"),
        (selxo[:], selxo_d[:], "selxo"), (selrl[:], selrl_d[:], "selrl"),
    ]:
        dma(dst, src, reads=(), writes=(tag,), q="GP", sem="ldg")
    dma(k3rhs[0:1, :], b1row_d[:], reads=(), writes=("k3r0",), q="GP", sem="ldg")
    for dst, src_d, dtag in ((g1t, ln1g_d, "g1t"), (b1t, ln1b_d, "b1t"),
                             (g2t, ln2g_d, "g2t"), (b2t, ln2b_d, "b2t")):
        dma(dst[:], src_d.partition_broadcast(128).rearrange("p o n -> p (o n)"),
            reads=(), writes=(dtag,), q="GP", sem="ldg")

    v("memset", ones_c[:], 1.0, writes=("onesc",))
    v("memset", ones_r[:], 1.0, writes=("onesc",))
    v("memset", epscol[:], float(EPS), writes=("epsc",))
    v("memset", ones_cb[:], 1.0, writes=("onescb",))
    v("memset", bnb[:, 102:103], 128.0, writes=("c128",))
    v("memset", n128[:], -128.0, writes=("n128",))
    v("memset", k3lhs[0:3, :], -128.0, writes=("k3l12",))
    v("memset", vsb[:].rearrange("p (t x) -> p t x", x=65)[:, :, 64:65], 1.0,
      writes=("vones",))

    for tag, wtag, wf, rank, dst, bank, btag in (
            ("qa", "wqaf", wqaf, QL, wqc, psG[0], G0),
            ("kv", "wkvaf", wkvaf, KVL, wkvc, psG[1], G1)):
        for c in range(8):
            mm(bank[0:1, 0:rank], ones_cb[:], wf[:, c * rank:(c + 1) * rank],
               reads=("onescb", wtag), writes=(btag,),
               group=f"gcol{tag}", start=(c == 0), stop=(c == 7))
        v("tensor_scalar", small[0:1, 0:rank], bank[0:1, 0:rank], 1.0 / float(E),
          None, op0=ALU.mult, reads=(btag,), writes=(f"cs{tag}",))
        mm(bank[:, 0:rank], ones_row, small[0:1, 0:rank],
           reads=("onesc", f"cs{tag}"), writes=(btag,))
        for c in range(8):
            v("tensor_tensor", dst[:, c * rank:(c + 1) * rank],
              wf[:, c * rank:(c + 1) * rank], bank[:, 0:rank], op=ALU.subtract,
              reads=(wtag, btag), writes=(f"wadj{tag}",))

    wab = [sbuf(f"0wab{i}", (128, HID // 2), F32) for i in range(2)]

    def wabs_gen(src, nchunk, width, acc_off):
        hw_ = width // 2
        for i in range(nchunk):
            for hf in range(2):
                j = 2 * i + hf
                buf = wab[j % 2]
                qw = hw_ // 2
                for ha in range(2):
                    dma(buf[:, ha * qw:(ha + 1) * qw],
                        src.rearrange("(c p) h -> p c h", p=128)
                        [:, i, hf * hw_ + ha * qw:hf * hw_ + (ha + 1) * qw],
                        reads=(), writes=(f"wab{j % 2}{'ab'[ha]}",),
                        sem=f"wab{j % 2}{'ab'[ha]}")
                act(buf[:, 0:hw_], buf[:, 0:hw_], AF.Abs,
                    reads=(f"wab{j % 2}a", f"wab{j % 2}b"),
                    writes=(f"wab{j % 2}a", f"wab{j % 2}b",
                            f"accw{2 * acc_off + j}"),
                    accum_out=accw[:, 2 * acc_off + j:2 * acc_off + j + 1])
                yield

    def _wab_chain():
        yield from wabs_gen(w1T_d, 8, HID, 0)
        yield from wabs_gen(w2T_d, 32, E, 8)
    wab_iter = _wab_chain()

    def wab_drain(k):
        for _ in range(k):
            try:
                next(wab_iter)
            except StopIteration:
                return

    def wsc_block():
        accreads = tuple(f"accw{i}" for i in range(80))
        mm(psG[0][0:1, 0:80], ones_c[:], accw[:, 0:80],
           reads=accreads + ("onesc",), writes=(G0,))
        v("reduce_sum", rowtmp[0:1, 0:1], psG[0][0:1, 0:16], axis=AX.X,
          reads=(G0,), writes=("s1sum",))
        v("reduce_sum", rowtmp[0:1, 1:2], psG[0][0:1, 16:80], axis=AX.X,
          reads=(G0,), writes=("s2sum",))
        v("tensor_scalar", rowtmp[0:1, 2:4], rowtmp[0:1, 0:2],
          1.0 / float(E * HID), 1e-5, op0=ALU.mult, op1=ALU.max,
          reads=("s1sum", "s2sum"), writes=("wmean",))
        v("reciprocal", rowtmp[0:1, 4:6], rowtmp[0:1, 2:4], reads=("wmean",),
          writes=("winv",))
        mm(psG[0][:, 0:4], ones_row, rowtmp[0:1, 2:6],
           reads=("wmean", "winv", "onesc"), writes=(G0,))
        v("tensor_copy", wsc[:, 0:4], psG[0][:, 0:4], reads=(G0,),
          writes=("wsc",))


    def weight_prep_steps():
        def ternarize(srcd, nchunk, width, scol, sink):
            hw_ = width // 2
            for i in range(nchunk):
                for hf in range(2):
                    j = 2 * i + hf
                    buf = wst[j % 2]
                    qw2 = hw_ // 2
                    for ha in range(2):
                        dma(buf[:, ha * qw2:(ha + 1) * qw2],
                            srcd.rearrange("(c p) h -> p c h", p=128)
                            [:, i, hf * hw_ + ha * qw2:hf * hw_ + (ha + 1) * qw2],
                            reads=(), writes=(f"wst{j % 2}{'ab'[ha]}",),
                            sem=f"wst{j % 2}{'ab'[ha]}")
                    for qtr in range(0, hw_, 256):
                        w_ = min(256, hw_ - qtr)
                        htag = f"wst{j % 2}{'ab'[qtr * 2 // hw_]}"
                        v("tensor_scalar", tzq[0][:, 0:w_], buf[:, qtr:qtr + w_],
                          wsc[:, scol + 2:scol + 3], 0.5, op0=ALU.mult,
                          op1=ALU.is_ge, reads=(htag, "wsc"),
                          writes=("tzq0",))
                        v("tensor_scalar", tzq[1][:, 0:w_], buf[:, qtr:qtr + w_],
                          wsc[:, scol + 2:scol + 3], -0.5, op0=ALU.mult,
                          op1=ALU.is_le, reads=(htag, "wsc"),
                          writes=("tzq1",))
                        sink(i, hf * hw_ + qtr, w_)
                        yield

        def w1_sink(i, qtr, w_):
            qi = t1qc[0] % 2
            t1qc[0] += 1
            v("tensor_tensor", t1cq[qi][:, 0:w_], tzq[0][:, 0:w_], tzq[1][:, 0:w_],
              op=ALU.subtract, reads=("tzq0", "tzq1"), writes=(f"t1cq{qi}",))
            hb_ = qtr // 512
            j0 = i * 512 + (qtr % 512)
            dma(tern1b[hb_, :, j0:j0 + w_],
                t1cq[qi][:, 0:w_], reads=(f"t1cq{qi}",), writes=(f"t1d{hb_}",),
                sem=f"ternw{qi}")

        def w2_sink(i, qtr, w_):
            v("tensor_tensor", w2t[:, i * E + qtr:i * E + qtr + w_],
              tzq[0][:, 0:w_], tzq[1][:, 0:w_], op=ALU.subtract,
              reads=("tzq0", "tzq1"), writes=(f"t2_{i}",))

        yield from ternarize(w1T_d, 8, HID, 0, w1_sink)
        yield from ternarize(w2T_d, 32, E, 1, w2_sink)

        for hb in range(8):
            slab = w1slab[hb % 2]
            dma(slab[:, 0:HID], tern1b[hb],
                reads=(f"t1d{hb}",), writes=(f"w1s{hb % 2}",), sem=f"c1s{hb % 2}")
            reg, rtag = PSC1[hb]
            for fc in range(8):
                mm(reg[0:1, 0:512], ones_cb[:],
                   slab[:, fc * 512:(fc + 1) * 512],
                   reads=(f"w1s{hb % 2}", "onescb"), writes=(rtag,),
                   group=f"gc1{hb}", start=(fc == 0), stop=(fc == 7))
        for hb in range(8):
            reg, rtag = PSC1[hb]
            for hq in range(2):
                sg = slice(hq * 256, (hq + 1) * 256)
                v("tensor_copy", tzq[0][0:1, 0:256], reg[0:1, sg],
                  reads=(rtag,), writes=("s1hiseg", "tzq0"))
                v("tensor_tensor", tzq[1][0:1, 0:256], reg[0:1, sg],
                  tzq[0][0:1, 0:256], op=ALU.subtract,
                  reads=(rtag, "s1hiseg"), writes=("s1loseg", "tzq1"))
                dma(k3rhs[1:2, hb * 512 + hq * 256:hb * 512 + (hq + 1) * 256],
                    tzq[0][0:1, 0:256], reads=("s1hiseg", "tzq0"),
                    writes=("k3r1",), q="GP", sem="ldg2")
                dma(k3rhs[2:3, hb * 512 + hq * 256:hb * 512 + (hq + 1) * 256],
                    tzq[1][0:1, 0:256], reads=("s1loseg", "tzq1"),
                    writes=("k3r2",), q="GP", sem="ldg2")

        for ec in range(8):
            for hc in range(32):
                mm(psG[1][0:1, 0:128], ones_cb[:],
                   w2t[:, hc * E + ec * 128:hc * E + (ec + 1) * 128],
                   reads=(f"t2_{hc}", "onescb"), writes=(G1,),
                   group=f"gc2{ec}", start=(hc == 0), stop=(hc == 31))
            v("tensor_copy", c2pair[0:1, ec * 128:(ec + 1) * 128],
              psG[1][0:1, 0:128], reads=(G1,), writes=("c2hi",))
            v("tensor_tensor", tzq[1][0:1, 0:128], psG[1][0:1, 0:128],
              c2pair[0:1, ec * 128:(ec + 1) * 128], op=ALU.subtract,
              reads=(G1, "c2hi"), writes=("c2loseg", "tzq1"))
            dma(c2pair[1:2, ec * 128:(ec + 1) * 128], tzq[1][0:1, 0:128],
                reads=("c2loseg", "tzq1"), writes=("c2lo",), q="GP", sem="ldg2")

    def xpass(src_ap, nblk, wadj, wmat, rank, ysink, selb, selbt, statps,
              statr0, statr1, sel_w, okind):
        for b_ in range(nblk):
            buf = xbf[b_ % 2]
            tag = f"xb{b_ % 2}"
            al = "tza" if b_ % 2 == 0 else "tzb"
            dma(buf[:], src_ap[:, :, b_ * 512:(b_ + 1) * 512],
                reads=(), writes=(tag, al), q="GP", sem=f"xbf{b_ % 2}")
            act(x2b[:], buf[:], AF.Square, reads=(tag,), writes=("x2b", "t1c"))
            for c in range(8):
                mm(psG[0][0:rank, 0:512], wmat[:, c * rank:(c + 1) * rank],
                   buf[:, c * 512:(c + 1) * 512],
                   reads=(tag, wadj), writes=(G0,), group=f"gy{okind}{b_}",
                   start=(c == 0), stop=(c == 7))
            ysink(b_, buf, tag)
            wab_drain(7)
            for c in range(8):
                mm(statps[0:sel_w, 0:512],
                   selb[:, b_ * sel_w:(b_ + 1) * sel_w],
                   x2b[:, c * 512:(c + 1) * 512], reads=("x2b", selbt),
                   writes=(statr0,), group="gstatA" + okind,
                   start=(b_ == 0 and c == 0),
                   stop=(b_ == nblk - 1 and c == 7))
                mm(statps[0:sel_w, 512:1024],
                   selb[:, b_ * sel_w:(b_ + 1) * sel_w],
                   buf[:, c * 512:(c + 1) * 512], reads=(tag, selbt),
                   writes=(statr1,), group="gstatB" + okind,
                   start=(b_ == 0 and c == 0),
                   stop=(b_ == nblk - 1 and c == 7))

    def ysink_full(b_, buf, tag):
        v("tensor_copy", ysb[0:32, b_ * 512:(b_ + 1) * 512], psG[0][0:32, 0:512],
          reads=(G0,), writes=(f"ysb{b_}",))

    xpass(xT_d.rearrange("(c p) t -> p c t", p=128), 8, "wadjkv", wkvc, 32,
          ysink_full, selx, "selx", psO, OL_, OU_, 8, "f")

    def ysink_own(b_, buf, tag):
        v("tensor_copy", qlp[0:64, b_ * 512:(b_ + 1) * 512], psG[0][0:64, 0:512],
          reads=(G0,), writes=(f"qlp{b_}",))
        for c in range(8):
            mm(psG[1][0:32, 0:512], wkvc[:, c * 32:(c + 1) * 32],
               buf[:, c * 512:(c + 1) * 512],
               reads=(tag, "wadjkv"), writes=(G1,), group=f"gyo{b_}",
               start=(c == 0), stop=(c == 7))
        v("tensor_copy", yor[0:32, b_ * 512:(b_ + 1) * 512], psG[1][0:32, 0:512],
          reads=(G1,), writes=(f"yor{b_}",))

    xpass(xoT_d.rearrange("(c p) t -> p c t", p=128), 4, "wadjqa", wqc, 64,
          ysink_own, selxo, "selxo", psSl, SL0, SL1, 4, "o")

    def finstats(pst, ptags, n, dst, dtag):
        v("tensor_scalar", dst[0:n, :], pst[0:n, 0:1024], 1.0 / float(E),
          None, op0=ALU.mult, reads=ptags, writes=(dtag + "m",))
        v("tensor_tensor", dst[0:n, 512:1024], dst[0:n, 512:1024],
          dst[0:n, 512:1024], op=ALU.mult, reads=(dtag + "m",),
          writes=(dtag + "m",))
        v("tensor_tensor", dst[0:n, 0:512], dst[0:n, 0:512],
          dst[0:n, 512:1024], op=ALU.subtract, reads=(dtag + "m",),
          writes=(dtag + "var",))
        act(dst[0:n, 0:512], dst[0:n, 0:512], AF.Ln, bias=epscol[0:n, 0:1],
            scale=1.0, reads=(dtag + "var", "epsc"), writes=(dtag + "ln",))
        act(dst[0:n, 0:512], dst[0:n, 0:512], AF.Exp, scale=-0.5,
            reads=(dtag + "ln",), writes=(dtag + "r",))

    finstats(psO, (OL_, OU_), 8, stat, "sf")
    finstats(psSl, (SL0, SL1), 4, stato, "so")

    for b_ in range(8):
        mm(psG[0][0:32, 0:512], selr8[:, b_ * 32:(b_ + 1) * 32], stat[0:8, 0:512],
           reads=("sfr", "selr8"), writes=(G0,))
        v("tensor_tensor", ysb[0:32, b_ * 512:(b_ + 1) * 512],
          ysb[0:32, b_ * 512:(b_ + 1) * 512], psG[0][0:32, 0:512], op=ALU.mult,
          reads=(f"ysb{b_}", G0), writes=(f"yln{b_}",))
    for b_ in range(4):
        mm(psG[0][0:64, 0:512], selr4[:, b_ * 64:(b_ + 1) * 64], stato[0:4, 0:512],
           reads=("sor", "selr4"), writes=(G0,))
        v("tensor_tensor", qlln[0:64, b_ * 512:(b_ + 1) * 512],
          qlp[0:64, b_ * 512:(b_ + 1) * 512], psG[0][0:64, 0:512], op=ALU.mult,
          reads=(f"qlp{b_}", G0), writes=(f"qlln{b_}",))
        mm(psG[1][0:32, 0:512], selr4[0:4, b_ * 64:b_ * 64 + 32], stato[0:4, 0:512],
           reads=("sor", "selr4"), writes=(G1,))
        v("tensor_tensor", ylnown[0:32, b_ * 512:(b_ + 1) * 512],
          yor[0:32, b_ * 512:(b_ + 1) * 512], psG[1][0:32, 0:512], op=ALU.mult,
          reads=(f"yor{b_}", G1), writes=(f"ylnown{b_}",))

    def rope(dst, dtag, dcol, psA, ptagA, psB, ptagB, t1_ap, t1tag, t2_ap,
             t2tag, width):
        v("tensor_tensor", rt1[:, 0:width], psA, t1_ap, op=ALU.mult,
          reads=(ptagA, t1tag), writes=("rt1", "pt0a"))
        v("tensor_tensor", rt2[:, 0:width], psB, t2_ap, op=ALU.mult,
          reads=(ptagB, t2tag), writes=("rt2", "pt1a"))
        v("tensor_tensor", dst[:, dcol:dcol + width], rt1[:, 0:width],
          rt2[:, 0:width], op=ALU.add, reads=("rt1", "rt2"), writes=(dtag,))

    for b_ in range(8):
        dma(csfb[:], csf_d[:, b_ * 512:(b_ + 1) * 512], reads=(),
            writes=("csfb",), sem="csfb")
        dma(scfb[:], scf_d[:, b_ * 512:(b_ + 1) * 512], reads=(),
            writes=("scfb",), sem="scfb")
        for g_ in range(2):
            mm(psG[0][:, 0:512], wkvbk[:, g_ * 128:(g_ + 1) * 128],
               ysb[0:32, b_ * 512:(b_ + 1) * 512],
               reads=(f"yln{b_}", "wkvbk"), writes=(G0,))
            mm(psG[1][:, 0:512], wkvbkB[:, g_ * 128:(g_ + 1) * 128],
               ysb[0:32, b_ * 512:(b_ + 1) * 512],
               reads=(f"yln{b_}", "wkvbkB"), writes=(G1,))
            rope(ksb[g_], f"ksbt{g_}", b_ * 512, psG[0][:, 0:512], G0,
                 psG[1][:, 0:512], G1, csfb[:], "csfb", scfb[:], "scfb", 512)
        for tt in range(4):
            kt = b_ * 4 + tt
            mm(psSu[:, 0:256], ysb[0:32, kt * 128:(kt + 1) * 128], wkvbv[:],
               reads=(f"yln{b_}", "wkvbv"), writes=(SU0,))
            v("tensor_copy",
              vsb[:].rearrange("p (t g x) -> p t g x", g=4, x=65)[:, kt, :, 0:64],
              psSu[:, 0:256].rearrange("p (g x) -> p g x", x=64),
              reads=(SU0,), writes=("vsbt",))

    v("reduce_max", small[:, 1:2], ksb[0][:], axis=AX.X, apply_absolute_value=True,
      reads=("ksbt0",), writes=("mk0",))
    wab_drain(10 ** 6)
    wsc_block()
    v("reduce_max", small[:, 2:3], ksb[1][:], axis=AX.X, apply_absolute_value=True,
      reads=("ksbt1",), writes=("mk1",))
    v("tensor_tensor", small[:, 1:2], small[:, 1:2], small[:, 2:3], op=ALU.max,
      reads=("mk0", "mk1"), writes=("mk",))

    wp_iter = weight_prep_steps()
    wp_count = [0]

    def wp_drain(k, cap=None):
        for _ in range(k):
            if cap is not None and wp_count[0] >= cap:
                return
            try:
                next(wp_iter)
                wp_count[0] += 1
            except StopIteration:
                return

    tail_pend = [None]

    def tail_drain(k):
        g_t = tail_pend[0]
        if g_t is None:
            return
        for _ in range(k):
            try:
                next(g_t)
            except StopIteration:
                tail_pend[0] = None
                return

    p.fence("p1done")
    sfree("0wab1", "0wab0", "0selr4", "0selr8", "0selxo", "0selx",
          "0wkvaf", "0wqaf", "0accw")
    sfree("1scfb", "1csfb", "1x2b", "1xbf1", "1xbf0", "1stato", "1stat",
          "1yor", "1qlp", "1ysb")
    FN = "p1done"

    x1 = sbuf("2x1", (128, 8 * SLOT), BF16)
    hsb = sbuf("2hsb", (128, 2 * HID), BF16)
    u1b = sbuf("2u1b", (128, E), BF16)
    x1tm = u1b
    sca = sbuf("2sca", (128, 512), F32)
    sda = sbuf("2sda", (128, 512), I32)
    sba = sbuf("2sba", (128, 512), BF16)
    saa = sbuf("2saa", (128, 512), BF16)
    dq1t = sbuf("2dq1t", (128, 2 * E), BF16)
    pts = [pt[0][:], pt[1][:],
           sca[:].bitcast(BF16), sda[:].bitcast(BF16)]
    PTTAG = [("pt0",), ("pt1",), ("pt2", "sca"), ("pt3", "sda")]
    _hf32 = hsb[:].bitcast(F32)
    wst = [_hf32[:, 0:HID // 2], _hf32[:, HID // 2:HID]]
    tzq = [dq1t[:, 0:256], dq1t[:, 256:512]]
    _t1f8 = dq1t[:, 512:768].bitcast(FP8)
    t1cq = [_t1f8[:, 0:256], _t1f8[:, 256:512]]
    t1qc = [0]
    dq2t = sbuf("2dq2t", (128, 32 * 256), BF16)
    otn = dq2t[:, 0:4096]
    olc = dq2t[:, 4096:8192]
    kos = [sbuf(f"2kos{g_}", (128, SLOT), BF16) for g_ in range(2)]
    vos = sbuf("2vos", (128, 4 * 260), BF16)
    cso = sbuf("2cso", (128, SLOT), BF16)
    sco = sbuf("2sco", (128, SLOT), BF16)
    woc = [sbuf(f"2woc{i}", (128, 512), BF16) for i in range(2)]
    xoc = [sbuf(f"2xoc{i}", (128, SLOT), F32) for i in range(2)]
    outst = [sbuf(f"2outst{i}", (128, 256), F32) for i in range(2)]
    scl2t = sbuf("2scl2t", (128, 256), BF16)
    rows2 = sbuf("2rows2", (1, 256), F32)
    lg = sbuf("2lg", (16, 512), BF16)
    linv = sbuf("2linv", (16, 512), BF16)
    olhi0 = sbuf("2olhi", (65, SLOT), BF16)
    olhi = [olhi0, sca[:].bitcast(BF16)[0:65, 0:512]]

    ARN = "dq2t_arena"

    for s_ in range(NSLOT):
        sc0 = s_ * SLOT
        fn = FN if s_ == 0 else None

        dma(cso[:], cso_d[:, sc0:sc0 + SLOT], reads=(), writes=("cso",), fence=fn, sem="cso")
        dma(sco[:], sco_d[:, sc0:sc0 + SLOT], reads=(), writes=("sco",), fence=fn, sem="sco")

        for c in range(8):
            mm(psSl[:, 0:512], wqb[:, c * 128:(c + 1) * 128],
               qlln[:, sc0:sc0 + SLOT], reads=(f"qlln{s_}", "wqb"), writes=(SL0,))
            mm(psSl[:, 512:1024], wqbB[:, c * 128:(c + 1) * 128],
               qlln[:, sc0:sc0 + SLOT], reads=(f"qlln{s_}", "wqbB"), writes=(SL1,))
            rope(qr, "qr", c * 512, psSl[:, 0:512], SL0, psSl[:, 512:1024], SL1,
                 cso[:], "cso", sco[:], "sco", 512)

        for g_ in range(2):
            mm(psSu[:, 0:512], wkvbk[:, g_ * 128:(g_ + 1) * 128],
               ylnown[0:32, sc0:sc0 + SLOT],
               reads=(f"ylnown{s_}", "wkvbk"), writes=(SU0,))
            mm(psSu[:, 512:1024], wkvbkB[:, g_ * 128:(g_ + 1) * 128],
               ylnown[0:32, sc0:sc0 + SLOT],
               reads=(f"ylnown{s_}", "wkvbkB"), writes=(SU1,))
            rope(kos[g_], f"kos{g_}", 0, psSu[:, 0:512], SU0,
                 psSu[:, 512:1024], SU1, cso[:], "cso", sco[:], "sco", 512)
        v("memset", vos[:].rearrange("p (t x) -> p t x", x=65)[:, :, 64:65], 1.0,
          writes=("vos",), fence=fn)
        for tt in range(4):
            mm(psO[:, 0:256], ylnown[0:32, sc0 + tt * 128:sc0 + (tt + 1) * 128],
               wkvbv[:], reads=(f"ylnown{s_}", "wkvbv"), writes=(OL_,))
            v("tensor_copy",
              vos[:].rearrange("p (t g x) -> p t g x", g=4, x=65)[:, tt, :, 0:64],
              psO[:, 0:256].rearrange("p (g x) -> p g x", x=64),
              reads=(OL_,), writes=("vos",))

        v("reduce_max", small[:, 0:1], qr[:], axis=AX.X, apply_absolute_value=True,
          reads=("qr",), writes=("mq",))
        v("tensor_tensor", small[:, 3:4], small[:, 0:1], small[:, 1:2],
          op=ALU.mult, reads=("mq", "mk"), writes=("mqk",))
        pb = psO[:, 512:1024]
        mm(pb[0:2, 0:1], halfsel[:], small[:, 3:4],
           reads=("mqk", "halfsel"), writes=(OU_,))
        v("tensor_copy", small2[0:2, 0:1], pb[0:2, 0:1], reads=(OU_,),
          writes=("bb2",))
        tr(pb[0:1, 0:2], small2[0:2, 0:1], idf[0:2, 0:2], reads=("bb2", "idf"),
           writes=(OU_,))
        v("reduce_max", rowtmp[0:1, 6:7], pb[0:1, 0:2], axis=AX.X,
          reads=(OU_,), writes=("bmax",))
        v("tensor_scalar", rowtmp[0:1, 7:8], rowtmp[0:1, 6:7], -SCALE, None,
          op0=ALU.mult, reads=("bmax",), writes=("negbrow",))
        mm(pb[:, 0:1], ones_row, rowtmp[0:1, 7:8],
           reads=("negbrow", "onesc"), writes=(OU_,))
        v("tensor_copy", negb[:], pb[:, 0:1], reads=(OU_,), writes=("negb",))
        mm(pb[:, 0:NL], ones_row, killrow[:], reads=("kill", "onesc"),
           writes=(OU_,))
        v("tensor_scalar", biastab[:], pb[:, 0:NL], negb[:], None,
          op0=ALU.add, reads=(OU_, "negb"), writes=("biastab",))

        L = LPAD[s_]
        SREG = [(psSl, (SL0,), (SL1,), True),
                (psSu, (SU0,), (SU1,), True)]
        ucnt = [0]

        def s_unit(lhs0, lhs1, rhs_ap, rtags, bias_ap, btag, wexp, ptwr):
            u = ucnt[0] % 2
            pu = ucnt[0] % 4
            ucnt[0] += 1
            reg, tg0, tg1, contig = SREG[u]
            half0 = reg[:, 0:512] if contig else psG[0][:, 0:512]
            half1 = reg[:, 512:1024] if contig else psG[1][:, 0:512]
            mm(half0[:, 0:wexp], lhs0, rhs_ap, reads=rtags, writes=tg0,
               tile_position=(0, 0) if lhs0.shape[0] == 64 and True else None)
            ptbuf = pts[pu]
            if lhs1 is not None:
                mm(half1[:, 0:wexp], lhs1, rhs_ap, reads=rtags, writes=tg1,
                   tile_position=None)
            if contig and lhs1 is not None:
                act(ptbuf[:, 0:1024], reg[:], AF.Exp, bias=bias_ap, scale=SCALE,
                    reads=tg0 + tg1 + (btag,), writes=PTTAG[pu])
            else:
                act(ptbuf[:, 0:wexp], half0[:, 0:wexp], AF.Exp, bias=bias_ap,
                    scale=SCALE, reads=tg0 + (btag,), writes=PTTAG[pu])
                if lhs1 is not None:
                    act(ptbuf[:, 512:512 + wexp], half1[:, 0:wexp], AF.Exp,
                        bias=bias_ap, scale=SCALE, reads=tg1 + (btag,),
                        writes=PTTAG[pu])
            return ptbuf, PTTAG[pu]

        for c in range(8):
            g_ = c // 4
            gl = f"gO{s_}_{c}l"
            gu = f"gO{s_}_{c}u"
            pvq = []

            def pv_flush(keep):
                while len(pvq) > keep:
                    pvq.pop(0)()

            for t0 in range(0, L, 2):
                bias_ap = biastab[:, LOFF[s_] + t0:LOFF[s_] + t0 + 1]
                for hh in range(2):
                    ro = hh * 64
                    tp = (0, 0) if hh == 0 else (64, 0)
                    u = ucnt[0] % 2
                    pu = ucnt[0] % 4
                    ucnt[0] += 1
                    reg, tg0, tg1, contig = SREG[u]
                    half0 = reg[:, 0:512] if contig else psG[0][:, 0:512]
                    half1 = reg[:, 512:1024] if contig else psG[1][:, 0:512]
                    for dt_ in range(2):
                        t = t0 + dt_
                        mm(half0 if dt_ == 0 else half1,
                           ksb[g_][ro:ro + 64, t * 128:(t + 1) * 128],
                           qr[ro:ro + 128 - 64, c * 512:(c + 1) * 512]
                           if False else qr[ro:ro + 64, c * 512:(c + 1) * 512],
                           reads=(f"ksbt{g_}", "qr"),
                           writes=tg0 if dt_ == 0 else tg1,
                           tile_position=tp)
                    ptbuf = pts[pu]
                    if contig:
                        act(ptbuf[:, 0:1024], reg[:], AF.Exp, bias=bias_ap,
                            scale=SCALE, reads=tg0 + tg1 + ("biastab",),
                            writes=PTTAG[pu])
                    else:
                        act(ptbuf[:, 0:512], half0, AF.Exp, bias=bias_ap,
                            scale=SCALE, reads=tg0 + ("biastab",),
                            writes=PTTAG[pu])
                        act(ptbuf[:, 512:1024], half1, AF.Exp, bias=bias_ap,
                            scale=SCALE, reads=tg1 + ("biastab",),
                            writes=PTTAG[pu])
                    og = gl if hh == 0 else gu
                    otag = OL_ if hh == 0 else OU_
                    oc0 = 0 if hh == 0 else 512
                    vg = g_ if hh == 0 else g_ + 2

                    def mkpv(t0=t0, oc0=oc0, vg=vg, og=og, otag=otag,
                             ptbuf=ptbuf, pu=pu):
                        def doit():
                            for dt_ in range(2):
                                t = t0 + dt_
                                mm(psO[0:65, oc0:oc0 + 512],
                                   vsb[:, t * 260 + vg * 65:t * 260 + vg * 65 + 65],
                                   ptbuf[:, dt_ * 512:(dt_ + 1) * 512],
                                   reads=PTTAG[pu] + ("vsbt", "vones"),
                                   writes=(otag,), group=og,
                                   start=(t0 == 0 and dt_ == 0), stop=False)
                        return doit
                    pvq.append(mkpv())
                    pv_flush(2)
                    if s_ > 0:
                        tail_drain(1)
            for kk in range(4):
                w_ = SLOT - kk * 128
                q0 = c * 512 + kk * 128
                for hh in range(2):
                    ro = hh * 64
                    tp = (0, 0) if hh == 0 else (64, 0)
                    u = ucnt[0] % 2
                    pu = ucnt[0] % 4
                    ucnt[0] += 1
                    reg, tg0, tg1, contig = SREG[u]
                    half0 = reg[:, 0:512] if contig else psG[0][:, 0:512]
                    mm(half0[:, 0:w_],
                       kos[g_][ro:ro + 64, kk * 128:(kk + 1) * 128],
                       qr[ro:ro + 64, q0:q0 + w_],
                       reads=(f"kos{g_}", "qr"), writes=tg0, tile_position=tp)
                    ptbuf = pts[pu]
                    act(ptbuf[:, 0:w_], half0[:, 0:w_], AF.Exp, bias=negb[:],
                        scale=SCALE, reads=tg0 + ("negb",), writes=PTTAG[pu])
                    v("tensor_tensor", ptbuf[:, 0:128], ptbuf[:, 0:128], tri[:],
                      op=ALU.mult, reads=PTTAG[pu] + ("tri",), writes=PTTAG[pu])
                    og = gl if hh == 0 else gu
                    otag = OL_ if hh == 0 else OU_
                    oc0 = (0 if hh == 0 else 512) + kk * 128
                    vg = g_ if hh == 0 else g_ + 2

                    def mkpvi(kk=kk, oc0=oc0, hh=hh, vg=vg, og=og, otag=otag,
                              ptbuf=ptbuf, pu=pu, w_=w_):
                        def doit():
                            mm(psO[0:65, oc0:(512 if hh == 0 else 1024)],
                               vos[:, kk * 260 + vg * 65:kk * 260 + vg * 65 + 65],
                               ptbuf[:, 0:w_], reads=PTTAG[pu] + ("vos",),
                               writes=(otag,), group=og, start=False,
                               stop=(kk == 3))
                        return doit
                    pvq.append(mkpvi())
                    pv_flush(2)
                    if s_ > 0:
                        tail_drain(1)
            if s_ > 0:
                tail_drain(10 ** 6)
            arntags = (tuple(f"dq2c{t_}_{k_}" for t_ in range(2) for k_ in range(8))
                       + tuple(f"otn{i}" for i in range(8)))
            pv_flush(0)
            v("tensor_copy", olc[0:65, c * 512:(c + 1) * 512], psO[0:65, 0:512],
              reads=(OL_,), writes=(f"oll{c}", f"olll{c}") + arntags, fence=fn)
            v("tensor_copy", olhi[c % 2][0:65, :], psO[0:65, 512:1024],
              reads=(OU_,), writes=(f"olh{c % 2}",) + (("sca",) if c % 2 else ()))
            st_ = olhi[c % 2]
            dma(lg[c:c + 1, :], olc[64:65, c * 512:(c + 1) * 512],
                reads=(f"olll{c}",), writes=(f"lgl{c}",), q="GP", sem="slg")
            dma(lg[8 + c:9 + c, :], st_[64:65, :], reads=(f"olh{c % 2}", "sca"),
                writes=(f"lgu{c}",), q="GP", sem="slg")
            dma(olc[64:128, c * 512:(c + 1) * 512], st_[0:64, :],
                reads=(f"olh{c % 2}", "sca", f"lgl{c}"),
                writes=(f"olu{c}", f"olll{c}"),
                q="SP", sem=f"ssh{c % 2}")

            if s_ == 0:
                wp_drain(44)

        lreads = (tuple(f"lgl{c}" for c in range(8))
                  + tuple(f"lgu{c}" for c in range(8)))
        v("reciprocal", linv[:], lg[:], reads=lreads, writes=("linv",))
        if s_ == 0:
            wp_drain(1000)

        for c in range(8):
            mm(psG[0][:, 0:512], selrl[:, c * 128:(c + 1) * 128], linv[:],
               reads=("linv", "selrl"), writes=(G0,))
            v("tensor_tensor", otn[:, c * 512:(c + 1) * 512],
              olc[:, c * 512:(c + 1) * 512], psG[0][:, 0:512], op=ALU.mult,
              reads=(f"oll{c}", f"olu{c}", G0),
              writes=(f"otn{c}", f"dq2c0_{c // 2}", f"dq2c1_{c // 2}"))
        for ec in range(8):
            for h_ in range(2):
                dma(woc[h_][:].rearrange("p (f x) -> p f x", x=128),
                    woutr_d.rearrange("p (f e) -> p f e", e=E)
                    [:, h_ * 4:(h_ + 1) * 4, ec * 128:(ec + 1) * 128],
                    reads=(), writes=(f"woc{h_}",), q="GP", fence=fn, sem="woc")
            xb = xoc[ec % 2]
            dma(xb[:], xoT_d.rearrange("(c p) t -> p c t", p=128)
                [:, ec, sc0:sc0 + SLOT],
                reads=(), writes=(f"xoc{ec % 2}",), fence=fn,
                sem=f"xoc{ec % 2}")
            pso, ptag = (psG[1], G1) if ec % 2 == 0 else (psG[0], G0)
            for fc in range(8):
                mm(pso[:, 0:512], woc[fc // 4][:, (fc % 4) * 128:(fc % 4 + 1) * 128],
                   otn[:, fc * 512:(fc + 1) * 512],
                   reads=(f"woc{fc // 4}", f"otn{fc}"), writes=(ptag,),
                   group=f"gat{s_}_{ec}", start=(fc == 0), stop=(fc == 7))
            v("scalar_tensor_tensor", x1[:, ec * 512:(ec + 1) * 512],
              pso[:, 0:512], bout_s[:, ec:ec + 1], xb[:], op0=ALU.add,
              op1=ALU.add, reads=(ptag, "bout", f"xoc{ec % 2}"),
              writes=(f"x1_{ec}",), fence=fn)

        def ffn_A(hs):
            for tt2 in range(2):
                tt = hs * 2 + tt2
                tcol = tt * 128
                po = 1024 * (tt % 2)
                ptg = SL0 if tt % 2 == 0 else SL1
                for fc in range(8):
                    tr(psbfv[:, po + fc * 128:po + (fc + 1) * 128],
                       x1[:, fc * 512 + tcol:fc * 512 + tcol + 128], idb[:],
                       reads=(f"x1_{fc}", "idb"), writes=(ptg,),
                       group=f"gx1t{s_}{tt}")
                v("tensor_copy", x1tm[:], psbfv[:, po:po + 1024], reads=(ptg,),
                  writes=("x1tm",), fence=fn)
                v("bn_stats", small[:, 4:10], x1tm[:, 0:512], reads=("x1tm",),
                  writes=("bn1a",))
                v("bn_stats", small[:, 10:16], x1tm[:, 512:1024],
                  reads=("x1tm",), writes=("bn1b",))
                v("bn_aggr", small[:, 16:18], small[:, 4:16],
                  reads=("bn1a", "bn1b"), writes=("bn1",))
                act(small[:, 18:19], small[:, 17:18], AF.Ln, bias=epscol[:],
                    scale=1.0, reads=("bn1", "epsc"), writes=("r1ln",))
                act(small[:, 18:19], small[:, 18:19], AF.Exp,
                    scale=-0.5, reads=("r1ln",), writes=("r1",))
                v("tensor_scalar", bnb[:, 103:104], small[:, 16:17],
                  small[:, 18:19], None, op0=ALU.mult,
                  reads=("bn1", "r1"), writes=("nmr1",))
                for st in range(2):
                    cu = st * 512
                    act(saa[:], x1tm[:, cu:cu + 512], AF.Copy,
                        scale=small[:, 18:19],
                        reads=("x1tm", "r1"), writes=("saa",),
                        fence=fn)
                    v("scalar_tensor_tensor", sba[:], saa[:], bnb[:, 103:104],
                      g1t[:, cu:cu + 512], op0=ALU.subtract, op1=ALU.mult,
                      reads=("saa", "nmr1", "g1t"), writes=("sba",))
                    v("tensor_tensor", x1tm[:, cu:cu + 512], sba[:],
                      b1t[:, cu:cu + 512], op=ALU.add, reads=("sba", "b1t"),
                      writes=("x1tm",))
                    if st == 0:
                        v("reduce_max", small[:, 19:20], x1tm[:, 0:512],
                          axis=AX.X, apply_absolute_value=True,
                          reads=("x1tm",), writes=("m1",))
                    else:
                        v("reduce_max", small[:, 17:18], x1tm[:, 512:1024],
                          axis=AX.X, apply_absolute_value=True,
                          reads=("x1tm",), writes=("m1t",))
                        v("tensor_tensor", small[:, 19:20], small[:, 19:20],
                          small[:, 17:18], op=ALU.max, reads=("m1", "m1t"),
                          writes=("m1",))
                v("tensor_scalar", small[:, 19:20], small[:, 19:20], 1e-5, None,
                  op0=ALU.max, reads=("m1",), writes=("m1",))
                v("reciprocal", small[:, 20:21], small[:, 19:20], reads=("m1",),
                  writes=("m1inv",))
                v("tensor_scalar", small[:, 21:22], small[:, 20:21], 127.0, None,
                  op0=ALU.mult, reads=("m1inv",), writes=("s1",))
                v("tensor_scalar", small[:, 22 + tt2:23 + tt2], small[:, 19:20],
                  wsc[:, 0:1], 1.0 / 127.0, op0=ALU.mult, op1=ALU.mult,
                  reads=("m1", "wsc"), writes=(f"scl1_{tt2}",))
                v("tensor_scalar", small[:, 24:25], small[:, 20:21], wsc[:, 2:3],
                  127.0, op0=ALU.mult, op1=ALU.mult, reads=("m1inv", "wsc"),
                  writes=("rscl",))
                for st in range(2):
                    cu = st * 512
                    act(sca[:], x1tm[:, cu:cu + 512], AF.Copy,
                        scale=small[:, 21:22], bias=128.0,
                        reads=("x1tm", "s1"), writes=("sca",), fence=fn)
                    v("tensor_copy", sda[:], sca[:], reads=("sca",),
                      writes=("sda",), fence=fn)
                    v("scalar_tensor_tensor", sba[:], sca[:], 0.5, sda[:],
                      op0=ALU.subtract, op1=ALU.is_gt, reads=("sca", "sda"),
                      writes=("sba",), fence=fn)
                    v("tensor_tensor", saa[:], sda[:], sba[:], op=ALU.add,
                      reads=("sda", "sba"), writes=("saa",), fence=fn)
                    po = 1024 * (st % 2)
                    ptg = SL0 if st % 2 == 0 else SL1
                    for fc in range(4):
                        tr(psbfv[:, po + fc * 128:po + (fc + 1) * 128],
                           saa[:, fc * 128:(fc + 1) * 128], idb[:],
                           reads=("saa", "idb"), writes=(ptg,),
                           group=f"gd1{s_}{tt}{st}")
                    v("tensor_copy",
                      dq1t[:, tt2 * 1024 + cu:tt2 * 1024 + cu + 512],
                      psbfv[:, po:po + 512], reads=(ptg,),
                      writes=(f"dq1t{tt2}", "tzq0", "tzq1", "t1cq0", "t1cq1"),
                      fence=fn)
                tr(psG[0][0:1, 0:128], small[:, 24:25], idf[:],
                   reads=("rscl", "idf"), writes=(G0,))
                v("tensor_copy", k3lhs[0:1, tt2 * 128:(tt2 + 1) * 128],
                  psG[0][0:1, 0:128], reads=(G0, "k3l12"),
                  writes=(f"k3l0_{tt2}",))

        def ffn_W1_hb(hs, hb):
            slab = w1slab[hb % 2]
            dma(slab[:], tern1b[hb],
                reads=(f"t1d{hb}",), writes=(f"w1s{hb % 2}",), q="GP",
                sem=f"w1s{hb % 2}")
            for tt2 in range(2):
                ph, ptag = (psG[0], G0) if (hb * 2 + tt2) % 2 == 0 else (psG[1], G1)
                gtag = f"gh{s_}{hs}{hb}{tt2}"
                for fc in range(8):
                    mm(ph[:, 0:512],
                       dq1t[:, tt2 * 1024 + fc * 128:tt2 * 1024 + (fc + 1) * 128],
                       slab[:, fc * 512:(fc + 1) * 512],
                       reads=(f"dq1t{tt2}", f"w1s{hb % 2}"), writes=(ptag,),
                       group=gtag, start=(fc == 0), stop=False)
                mm(ph[:, 0:512], k3lhs[0:3, tt2 * 128:(tt2 + 1) * 128],
                   k3rhs[0:3, hb * 512:(hb + 1) * 512],
                   reads=(f"k3l0_{tt2}", "k3l12", "k3r0", "k3r1", "k3r2"),
                   writes=(ptag,), group=gtag, start=False, stop=True)
                act(hsb[:, tt2 * HID + hb * 512:tt2 * HID + (hb + 1) * 512],
                    ph[:, 0:512], AF.Gelu, scale=small[:, 22 + tt2:23 + tt2],
                    reads=(ptag, f"scl1_{tt2}"),
                    writes=(f"h{tt2}_{hb}", f"u2_{hb}",
                            f"wst{tt2}a", f"wst{tt2}b"))
                v("bn_stats", bnb[:, tt2 * 52 + 6 * hb:tt2 * 52 + 6 * hb + 6],
                  hsb[:, tt2 * HID + hb * 512:tt2 * HID + (hb + 1) * 512],
                  reads=(f"h{tt2}_{hb}",), writes=(f"bn2_{tt2}_{hb}",))

        def ffn_L(hs, tt2):
            hreads = tuple(f"h{tt2}_{i}" for i in range(8))
            b0 = tt2 * 52
            s2c = 30 if tt2 == 0 else 25
            v("bn_aggr", bnb[:, b0 + 48:b0 + 50], bnb[:, b0:b0 + 48],
              reads=tuple(f"bn2_{tt2}_{i}" for i in range(8)),
              writes=(f"bn2s{tt2}",))
            act(small[:, 26:27], bnb[:, b0 + 49:b0 + 50], AF.Ln, bias=epscol[:],
                scale=1.0, reads=(f"bn2s{tt2}", "epsc"), writes=("r2ln",))
            act(small[:, 26:27], small[:, 26:27], AF.Exp,
                scale=-0.5, reads=("r2ln",), writes=("r2",))
            v("tensor_scalar", bnb[:, b0 + 50:b0 + 51], bnb[:, b0 + 48:b0 + 49],
              small[:, 26:27], None, op0=ALU.mult,
              reads=(f"bn2s{tt2}", "r2"), writes=(f"nmr2_{tt2}",))
            yield
            for k_ in range(8):
                col = tt2 * HID + 512 * k_
                act(saa[:], hsb[:, col:col + 512], AF.Copy,
                    scale=small[:, 26:27],
                    reads=(f"h{tt2}_{k_}", "r2"), writes=("saa",))
                v("scalar_tensor_tensor", sba[:], saa[:], bnb[:, b0 + 50:b0 + 51],
                  g2t[:, 512 * k_:512 * (k_ + 1)], op0=ALU.subtract,
                  op1=ALU.mult,
                  reads=("saa", f"nmr2_{tt2}", "g2t"), writes=("sba",))
                v("tensor_tensor", hsb[:, col:col + 512], sba[:],
                  b2t[:, 512 * k_:512 * (k_ + 1)], op=ALU.add,
                  reads=("sba", "b2t"), writes=(f"u2_{k_}", f"h{tt2}_{k_}"))
                if k_ == 0:
                    v("reduce_max", small[:, 27:28], hsb[:, col:col + 512],
                      axis=AX.X, apply_absolute_value=True,
                      reads=(f"u2_{k_}",), writes=("m2",))
                else:
                    v("reduce_max", small[:, 28:29], hsb[:, col:col + 512],
                      axis=AX.X, apply_absolute_value=True,
                      reads=(f"u2_{k_}",), writes=("m2t",))
                    v("tensor_tensor", small[:, 27:28], small[:, 27:28],
                      small[:, 28:29], op=ALU.max, reads=("m2", "m2t"),
                      writes=("m2",))
                yield
            v("tensor_scalar", small[:, 27:28], small[:, 27:28], 1e-5, None,
              op0=ALU.max, reads=("m2",), writes=("m2",))
            v("reciprocal", small[:, 29:30], small[:, 27:28], reads=("m2",),
              writes=("m2inv",))
            v("tensor_scalar", small[:, s2c:s2c + 1], small[:, 29:30], 127.0,
              None, op0=ALU.mult, reads=("m2inv",), writes=(f"s2_{tt2}",))
            v("tensor_scalar", small[:, 31:32], small[:, 27:28], wsc[:, 1:2],
              1.0 / 127.0, op0=ALU.mult, op1=ALU.mult, reads=("m2", "wsc"),
              writes=("scl2",))
            tr(psG[0][0:1, 0:128], small[:, 31:32], idf[:],
               reads=("scl2", "idf"), writes=(G0,))
            v("tensor_copy", rows2[0:1, tt2 * 128:(tt2 + 1) * 128],
              psG[0][0:1, 0:128], reads=(G0,), writes=(f"r2row{tt2}",))

        psgbf = [psG[0][:].bitcast(BF16), psG[1][:].bitcast(BF16)]

        def ffn_Q2(hs, k_, qb=None):
            scb, sct, sdb, sdt = qb if qb is not None else \
                (sca[:], "sca", sda[:], "sda")
            for tt2 in range(2):
                col = tt2 * HID + 512 * k_
                s2c = 30 if tt2 == 0 else 25
                act(scb, hsb[:, col:col + 512], AF.Copy,
                    scale=small[:, s2c:s2c + 1], bias=128.0,
                    reads=(f"u2_{k_}", f"s2_{tt2}"), writes=(sct,))
                v("tensor_copy", sdb, scb, reads=(sct,),
                  writes=(sdt,))
                v("scalar_tensor_tensor", sba[:], scb, 0.5, sdb,
                  op0=ALU.subtract, op1=ALU.is_gt, reads=(sct, sdt),
                  writes=("sba",))
                v("tensor_tensor", saa[:], sdb, sba[:], op=ALU.add,
                  reads=(sdt, "sba"), writes=("saa",))
                if qb is None:
                    pbv = psbfv
                    po = 1024 * (tt2 % 2)
                    ptg = SL0 if tt2 % 2 == 0 else SL1
                else:
                    pbv = psgbf[tt2]
                    po = 0
                    ptg = G0 if tt2 == 0 else G1
                for i in range(4):
                    tr(pbv[:, po + i * 128:po + (i + 1) * 128],
                       saa[:, i * 128:(i + 1) * 128], idb[:],
                       reads=("saa", "idb"), writes=(ptg,),
                       group=f"gd2{s_}{hs}{tt2}{k_}")
                hclo = 4 * k_
                if hclo < 16:
                    ovl = (f"otn{hclo // 2}", f"otn{hclo // 2 + 1}")
                else:
                    cc = (hclo - 16) // 2
                    ovl = (f"oll{cc}", f"olu{cc}", f"olll{cc}",
                           f"oll{cc + 1}", f"olu{cc + 1}", f"olll{cc + 1}")
                v("tensor_copy",
                  dq2t[:].rearrange("p (hc t) -> p hc t", t=256)
                  [:, 4 * k_:4 * (k_ + 1), tt2 * 128:(tt2 + 1) * 128],
                  pbv[:, po:po + 512].rearrange("p (i t) -> p i t", t=128),
                  reads=(ptg,), writes=(f"dq2c{tt2}_{k_}",) + ovl)

        def ffn_W2(hs, s_=s_, sc0=sc0):
            mm(psG[1][:, 0:256], ones_row, rows2[0:1, 0:256],
               reads=("r2row0", "r2row1", "onesc"), writes=(G1,))
            v("tensor_copy", scl2t[:], psG[1][:, 0:256], reads=(G1,),
              writes=("scl2t",))
            yield
            for ec in range(8):
                pso, ptag = (psG[0], G0) if ec % 2 == 0 else (psG[1], G1)
                gtag = f"go{s_}{hs}{ec}"
                for hc in range(32):
                    mm(pso[:, 0:256],
                       w2t[:, hc * E + ec * 128:hc * E + (ec + 1) * 128],
                       dq2t[:, hc * 256:(hc + 1) * 256],
                       reads=(f"t2_{hc}", f"dq2c0_{hc // 4}", f"dq2c1_{hc // 4}"),
                       writes=(ptag,), group=gtag, start=(hc == 0), stop=False)
                mm(pso[:, 0:256], c2pair[0:2, ec * 128:(ec + 1) * 128],
                   n128[0:2, 0:256], reads=("c2hi", "c2lo", "n128"),
                   writes=(ptag,), group=gtag, start=False, stop=True)
                ob = outst[ec % 2]
                v("scalar_tensor_tensor", ob[:], pso[:, 0:256], 1.0, scl2t[:],
                  op0=ALU.mult, op1=ALU.mult, reads=(ptag, "scl2t"),
                  writes=(f"ost{ec % 2}",))
                v("scalar_tensor_tensor", ob[:], ob[:], b2_s[:, ec:ec + 1],
                  x1[:, ec * 512 + hs * 256:ec * 512 + (hs + 1) * 256],
                  op0=ALU.add, op1=ALU.add,
                  reads=(f"ost{ec % 2}", "b2", f"x1_{ec}"),
                  writes=(f"ost{ec % 2}",))
                dma(yT_d.rearrange("(c p) t -> p c t", p=128)
                    [:, ec, sc0 + hs * 256:sc0 + (hs + 1) * 256], ob[:],
                    reads=(f"ost{ec % 2}",), writes=(f"od{s_}{hs}{ec}",),
                    sem=f"sout{ec % 2}")
                yield

        ffn_A(0)
        for hb in range(8):
            ffn_W1_hb(0, hb)
        for _ in ffn_L(0, 0):
            pass
        for _ in ffn_L(0, 1):
            pass
        ffn_A(1)
        for k_ in range(8):
            ffn_Q2(0, k_)
            ffn_W1_hb(1, k_)
        for _ in ffn_W2(0):
            pass

        def ffn_tail(fL=ffn_L, fQ2=ffn_Q2, fW2=ffn_W2):
            qb = (xoc[0][:], "xoc0", xoc[1][:].bitcast(I32), "xoc1")
            yield from fL(1, 0)
            yield from fL(1, 1)
            for k_ in range(8):
                fQ2(1, k_, qb=qb)
                yield
            yield from fW2(1)

        if s_ == 3:
            for _ in ffn_tail():
                pass
        else:
            tail_pend[0] = ffn_tail()

    p.fence("final")
    p.op("SP", lambda eng: eng.nop(), fence="final")

    def replay(engkey):
        def fn(eng):
            for waits, f_, dsemname in p.items[engkey]:
                for (semkey, val) in waits:
                    if semkey[0] == "d":
                        dsem(semkey[1])
                    eng.wait_ge(sems[semkey], val)
                try:
                    inst = f_(eng)
                except Exception as e:
                    raise RuntimeError(f"replay {engkey} line "
                                       f"{getattr(f_, '_line', '?')}: {e}") from e
                if dsemname is not None:
                    inst.then_inc(dsem(dsemname), 16)
                else:
                    inst.then_inc(sems[("e", engkey)], 1)
        return fn

    for _sname in list(p.dcnt.keys()):
        dsem(_sname)
    with nc.allow_low_precision(reason="bf16 intermediates are intentional; "
                                "tolerance calibrated for bf16 matmuls"), \
         nc.Block() as blk:
        blk.tensor(replay("PE"))
        blk.scalar(replay("ACT"))
        blk.vector(replay("DVE"))
        blk.sync(replay("SP"))
        blk.gpsimd(replay("GP"))

    return nc, p


def _host_tables():
    inv = 1.0 / (10000.0 ** (np.arange(0, D, 2, dtype=np.float64) / D))
    t = np.arange(S, dtype=np.float64)
    fr = np.outer(inv, t)
    cos, sin = np.cos(fr), np.sin(fr)
    cs = np.ascontiguousarray(np.concatenate([cos, cos, cos, cos], 0).astype(BFNP))
    sc = np.ascontiguousarray(np.concatenate([-sin, sin, -sin, sin], 0).astype(BFNP))

    tri = np.ascontiguousarray(
        (np.arange(128)[:, None] <= np.arange(128)[None, :]).astype(BFNP))
    idb = np.ascontiguousarray(np.eye(128, dtype=BFNP))
    idf = np.ascontiguousarray(np.eye(128, dtype=np.float32))

    selx = np.zeros((8, 128, 8), np.float32)
    for b_ in range(8):
        selx[b_, :, b_] = 1.0
    selxo = np.zeros((4, 128, 4), np.float32)
    for b_ in range(4):
        selxo[b_, :, b_] = 1.0
    selr8 = np.zeros((8, 8 * 32), np.float32)
    for b_ in range(8):
        selr8[b_, b_ * 32:(b_ + 1) * 32] = 1.0
    selr4 = np.zeros((4, 4 * 64), np.float32)
    for b_ in range(4):
        selr4[b_, b_ * 64:(b_ + 1) * 64] = 1.0
    selrl = np.zeros((16, 8, 128), np.float32)
    for c in range(8):
        selrl[c, c, 0:64] = 1.0
        selrl[c + 8, c, 64:128] = 1.0
    halfsel = np.zeros((128, 2), np.float32)
    halfsel[0:64, 0] = 1.0
    halfsel[64:128, 1] = 1.0

    qperm = np.zeros(E, np.int64)
    qpermB = np.zeros(E, np.int64)
    DPART = np.concatenate([np.arange(1, 64, 2), np.arange(0, 64, 2)])
    for c in range(8):
        for r in range(128):
            h_ = c if r < 64 else c + 8
            qperm[c * 128 + r] = h_ * 64 + DPERM[r % 64]
            qpermB[c * 128 + r] = h_ * 64 + DPART[r % 64]
    kperm = np.zeros(256, np.int64)
    kpermB = np.zeros(256, np.int64)
    for g_ in range(2):
        for r in range(128):
            kvh = g_ if r < 64 else g_ + 2
            kperm[g_ * 128 + r] = kvh * 128 + DPERM[r % 64] * 2
            kpermB[g_ * 128 + r] = kvh * 128 + DPART[r % 64] * 2
    vperm = np.zeros(256, np.int64)
    for vh in range(4):
        for d_ in range(64):
            vperm[vh * 64 + d_] = vh * 128 + d_ * 2 + 1
    wrow = np.concatenate(
        [np.concatenate([np.arange(fc * 64, fc * 64 + 64),
                         np.arange((fc + 8) * 64, (fc + 8) * 64 + 64)])
         for fc in range(8)])
    return dict(
        cs=cs, sc=sc, tri=tri, idb=idb, idf=idf,
        selx=np.ascontiguousarray(selx.transpose(1, 0, 2).reshape(128, 64)
                                  .astype(BFNP)),
        selxo=np.ascontiguousarray(selxo.transpose(1, 0, 2).reshape(128, 16)
                                   .astype(BFNP)),
        selr8=selr8, selr4=selr4,
        selrl=np.ascontiguousarray(selrl.transpose(1, 0, 2).reshape(16, 1024)),
        halfsel=halfsel, qperm=qperm, qpermB=qpermB, kperm=kperm,
        kpermB=kpermB, vperm=vperm, wrow=wrow)


_NC_CACHE = {}


def kernel(L0_attn_x, L0_attn_Wqa, L0_attn_Wqb, L0_attn_Wkva, L0_attn_Wkvb,
           L0_attn_Wout, L0_attn_bout, L0_ffn_ln1_g, L0_ffn_ln1_b, L0_ffn_w1,
           L0_ffn_b1, L0_ffn_ln2_g, L0_ffn_ln2_b, L0_ffn_w2, L0_ffn_b2):
    T = _host_tables()
    x = np.asarray(L0_attn_x, np.float32)

    wqb_f = np.asarray(L0_attn_Wqb, np.float32)
    wqbp = np.ascontiguousarray(wqb_f[:, T["qperm"]])
    wqbpB = np.ascontiguousarray(wqb_f[:, T["qpermB"]])
    wkvb = np.asarray(L0_attn_Wkvb, np.float32)
    wkvbk = np.ascontiguousarray(wkvb[:, T["kperm"]])
    wkvbkB = np.ascontiguousarray(wkvb[:, T["kpermB"]])
    wkvbv = np.ascontiguousarray(wkvb[:, T["vperm"]])
    wout = np.asarray(L0_attn_Wout, np.float32)[T["wrow"]]
    woutr = np.ascontiguousarray(wout.reshape(8, 128, E).transpose(1, 0, 2)
                                 .reshape(128, 8 * E))
    bout_c = np.ascontiguousarray(np.asarray(L0_attn_bout, np.float32)
                                  .reshape(8, 128).T)
    b2_c = np.ascontiguousarray(np.asarray(L0_ffn_b2, np.float32)
                                .reshape(8, 128).T)
    w1T = np.ascontiguousarray(np.asarray(L0_ffn_w1, np.float32).T)
    w2T = np.ascontiguousarray(np.asarray(L0_ffn_w2, np.float32).T)

    shared = {
        "wqa": np.ascontiguousarray(np.asarray(L0_attn_Wqa, np.float32)),
        "wqbp": wqbp,
        "wkva": np.ascontiguousarray(np.asarray(L0_attn_Wkva, np.float32)),
        "wkvbk": wkvbk, "wkvbkB": wkvbkB, "wqbpB": wqbpB,
        "woutr": woutr, "wkvbv": wkvbv,
        "bout_c": bout_c, "b2_c": b2_c,
        "b1row": np.asarray(L0_ffn_b1, np.float32).reshape(1, HID),
        "ln1g": np.asarray(L0_ffn_ln1_g, np.float32).reshape(1, E),
        "ln1b": np.asarray(L0_ffn_ln1_b, np.float32).reshape(1, E),
        "ln2g": np.asarray(L0_ffn_ln2_g, np.float32).reshape(1, HID),
        "ln2b": np.asarray(L0_ffn_ln2_b, np.float32).reshape(1, HID),
        "w1T": w1T, "w2T": w2T,
        "cs_full": T["cs"], "sc_full": T["sc"], "tri": T["tri"],
        "idb": T["idb"], "idf": T["idf"],
        "selx": T["selx"], "selxo": T["selxo"],
        "selr8": T["selr8"], "selr4": T["selr4"],
        "selrl": T["selrl"], "halfsel": T["halfsel"],
    }

    in_maps = []
    for core in range(8):
        b_ = core // 2
        js = JSETS[core % 2]
        cols = np.concatenate([np.arange(512 * j, 512 * j + 512) for j in js])
        kill = np.zeros((1, NL), np.float32)
        for s_, j in enumerate(js):
            kill[0, LOFF[s_] + 4 * j:LOFF[s_] + LPAD[s_]] = KILL
        m = dict(shared)
        m["xT"] = np.ascontiguousarray(x[b_].T)
        m["xoT"] = np.ascontiguousarray(x[b_][cols].T)
        m["cs_own"] = np.ascontiguousarray(T["cs"][:, cols])
        m["sc_own"] = np.ascontiguousarray(T["sc"][:, cols])
        m["killrow"] = kill
        in_maps.append(m)

    if "nc" not in _NC_CACHE:
        _NC_CACHE["nc"] = build_program()[0]
    nc = _NC_CACHE["nc"]
    res = run_bass_kernel_spmd(nc, in_maps, core_ids=list(range(8)))

    out = np.empty((B, S, E), np.float32)
    for core in range(8):
        yv = np.asarray(res.results[core]["yT"])
        b_ = core // 2
        js = JSETS[core % 2]
        for s_, j in enumerate(js):
            out[b_, 512 * j:512 * j + 512, :] = yv[:, 512 * s_:512 * (s_ + 1)].T
    return out


if __name__ == "__main__":
    nc, p = build_program()
    n_inst = sum(1 for _ in nc.all_instructions())
    print("built ok, instructions:", n_inst)
    print("op counts:", {e: p.cnt[e] for e in Prog.ENGS},
          {s: v // 16 for s, v in p.dcnt.items()})


# version 29 (final)
# speedup vs baseline: 1.0076x; 1.0076x over previous
import numpy as np
import ml_dtypes
import concourse.bass as bass
import concourse.mybir as mybir
from concourse.bass_utils import run_bass_kernel_spmd

F32 = mybir.dt.float32
BF16 = mybir.dt.bfloat16
I32 = mybir.dt.int32
FP8 = mybir.dt.float8e4
AF = mybir.ActivationFunctionType
ALU = mybir.AluOpType
AX = mybir.AxisListType
BFNP = ml_dtypes.bfloat16

B, S, E = 4, 4096, 1024
H, KVH, D = 16, 4, 64
QL, KVL = 64, 32
HID = 4096
TOK = 2048
NSLOT = 4
SLOT = 512
NKT = S // 128
EPS = 1e-5
KILL = -1.0e30
SCALE = 0.125

JSETS = [[7, 5, 2, 0], [6, 4, 3, 1]]
LPAD = [28, 20, 12, 4]
LOFF = [0, 28, 48, 60]
NL = 64
DPERM = np.concatenate([np.arange(0, 64, 2), np.arange(1, 64, 2)])


class Prog:
    ENGS = ("PE", "ACT", "DVE", "SP", "GP")

    def __init__(self):
        self.items = {e: [] for e in self.ENGS}
        self.cnt = {e: 0 for e in self.ENGS}
        self.dcnt = {}
        self.last_w = {}
        self.group = {}
        self.readers = {}
        self.observed = {e: {} for e in self.ENGS}
        self.fences = {}

    def _want(self, eng, waits, key, val):
        if self.observed[eng].get(key, 0) >= val:
            return
        if waits.get(key, 0) < val:
            waits[key] = val

    def fence(self, tag):
        deps = {}
        for e in self.ENGS:
            if self.cnt[e]:
                deps[("e", e)] = self.cnt[e]
        for s_, v_ in self.dcnt.items():
            deps[("d", s_)] = v_
        self.fences[tag] = deps

    def _emit(self, eng, fn, reads, writes, group, is_dma, fence, sem=None):
        waits = {}
        if fence is not None:
            for k_, v_ in self.fences[fence].items():
                self._want(eng, waits, k_, v_)
        for r in reads:
            for k_, v_ in self.last_w.get(r, {}).items():
                self._want(eng, waits, k_, v_)
        for w in writes:
            same_group = group is not None and self.group.get(w) == group
            if not same_group:
                for k_, v_ in self.last_w.get(w, {}).items():
                    self._want(eng, waits, k_, v_)
            for k_, v_ in self.readers.get(w, {}).items():
                self._want(eng, waits, k_, v_)
        if is_dma:
            prev = self.dcnt.get(sem, 0)
            if prev:
                self._want(eng, waits, ("d", sem), prev)
            self.dcnt[sem] = prev + 16
            me = (("d", sem), self.dcnt[sem])
        else:
            self.cnt[eng] += 1
            me = (("e", eng), self.cnt[eng])
        for r in reads:
            self.readers.setdefault(r, {})[me[0]] = me[1]
        for w in writes:
            same_group = group is not None and self.group.get(w) == group
            if same_group:
                self.last_w.setdefault(w, {})[me[0]] = me[1]
            else:
                self.last_w[w] = {me[0]: me[1]}
            self.readers[w] = {}
            self.group[w] = group
        obs = self.observed[eng]
        for k_, v_ in waits.items():
            if obs.get(k_, 0) < v_:
                obs[k_] = v_
        self.items[eng].append((sorted(waits.items()), fn,
                                sem if is_dma else None))

    def op(self, eng, fn, reads=(), writes=(), group=None, fence=None):
        import sys
        fn._line = sys._getframe(2).f_lineno
        self._emit(eng, fn, reads, writes, group, False, fence)

    def dma(self, eng, fn, reads=(), writes=(), fence=None, sem=None):
        self._emit(eng, fn, reads, writes, None, True, fence, sem=sem)


def build_program():
    nc = bass.Bass("TRN2", target_bir_lowering=False)
    p = Prog()

    def din(name, shape, dt=F32):
        return nc.dram_tensor(name, list(shape), dt, kind="ExternalInput").ap()

    xT_d = din("xT", (E, S))
    xoT_d = din("xoT", (E, TOK))
    wqa_d = din("wqa", (E, QL))
    wqbp_d = din("wqbp", (QL, E))
    wqbpB_d = din("wqbpB", (QL, E))
    wkva_d = din("wkva", (E, KVL))
    wkvbk_d = din("wkvbk", (KVL, 256))
    wkvbkB_d = din("wkvbkB", (KVL, 256))
    wkvbv_d = din("wkvbv", (KVL, 256))
    woutr_d = din("woutr", (128, 8 * E))
    bout_d = din("bout_c", (128, 8))
    b2_d = din("b2_c", (128, 8))
    b1row_d = din("b1row", (1, HID))
    ln1g_d = din("ln1g", (1, E))
    ln1b_d = din("ln1b", (1, E))
    ln2g_d = din("ln2g", (1, HID))
    ln2b_d = din("ln2b", (1, HID))
    w1T_d = din("w1T", (E, HID))
    w2T_d = din("w2T", (HID, E))
    csf_d = din("cs_full", (128, S), BF16)
    scf_d = din("sc_full", (128, S), BF16)
    cso_d = din("cs_own", (128, TOK), BF16)
    sco_d = din("sc_own", (128, TOK), BF16)
    tri_d = din("tri", (128, 128), BF16)
    kill_d = din("killrow", (1, NL))
    idb_d = din("idb", (128, 128), BF16)
    idf_d = din("idf", (128, 128), F32)
    selx_d = din("selx", (128, 64), BF16)
    selxo_d = din("selxo", (128, 16), BF16)
    selr8_d = din("selr8", (8, 256))
    selr4_d = din("selr4", (4, 256))
    selrl_d = din("selrl", (16, 1024))
    halfsel_d = din("halfsel", (128, 2))

    yT_d = nc.dram_tensor("yT", [E, TOK], F32, kind="ExternalOutput").ap()
    tern1_dr = nc.dram_tensor("tern1_scratch", [8 * 128, HID], FP8).ap()
    tern1b = tern1_dr.rearrange("(hb p) x -> hb p x", p=128)
    t1dr = tuple(f"t1d{i}" for i in range(8))

    ctxs = {}

    def sbuf(name, shape, dt):
        c = nc.sbuf_tensor(name, list(shape), dt)
        h = c.__enter__()
        ctxs[name] = c
        return h

    def sfree(*names):
        for n in names:
            ctxs.pop(n).__exit__(None, None, None)

    w2t = sbuf("w2t", (128, 32 * E), FP8)
    ksb = [sbuf(f"Pksb{g}", (128, S), BF16) for g in range(2)]
    vsb = sbuf("Pvsb", (128, NKT * 260), BF16)
    qr = sbuf("Pqr", (128, 8 * SLOT), BF16)
    qlln = sbuf("Pqlln", (64, TOK), BF16)
    ylnown = sbuf("Pylnown", (32, TOK), BF16)
    g1t = sbuf("Pg1t", (128, E), BF16)
    b1t = sbuf("Pb1t", (128, E), BF16)
    g2t = sbuf("Pg2t", (128, HID), BF16)
    b2t = sbuf("Pb2t", (128, HID), BF16)
    k3rhs = sbuf("Pk3rhs", (3, HID), BF16)
    k3lhs = sbuf("Pk3lhs", (3, 256), BF16)
    c2pair = sbuf("Pc2pair", (2, E), BF16)
    n128 = sbuf("Pn128", (2, 512), BF16)
    tri = sbuf("Ptri", (128, 128), BF16)
    idb = sbuf("Pidb", (128, 128), BF16)
    idf = sbuf("Pidf", (128, 128), F32)
    selrl = sbuf("Pselrl", (16, 1024), BF16)
    halfsel = sbuf("Phalfsel", (128, 2), F32)
    bout_s = sbuf("Pbout", (128, 8), F32)
    b2_s = sbuf("Pb2", (128, 8), F32)
    biastab = sbuf("Pbiastab", (128, NL), F32)
    negb = sbuf("Pnegb", (128, 1), F32)
    killrow = sbuf("Pkill", (1, NL), F32)
    ones_c = sbuf("Ponesc", (128, 1), F32)
    ones_cb = sbuf("Ponescb", (128, 1), BF16)
    ones_r = sbuf("Ponesr", (1, 128), F32)
    epscol = sbuf("Pepscol", (128, 1), F32)
    wsc = sbuf("Pwsc", (128, 8), F32)
    small = sbuf("Psmall", (128, 64), F32)
    small2 = sbuf("Psmall2", (2, 2), F32)
    rowtmp = sbuf("Prowtmp", (1, 8), F32)
    bnb = sbuf("Pbnb", (128, 104), F32)
    wqc = sbuf("Pwqc", (128, 8 * QL), BF16)
    wkvc = sbuf("Pwkvc", (128, 8 * KVL), BF16)
    wqb = sbuf("Pwqb", (64, E), BF16)
    wqbB = sbuf("PwqbB", (64, E), BF16)
    wkvbk = sbuf("Pwkvbk", (32, 256), BF16)
    wkvbkB = sbuf("PwkvbkB", (32, 256), BF16)
    wkvbv = sbuf("Pwkvbv", (32, 256), BF16)
    pt = [sbuf(f"Ppt{i}", (128, 1024), BF16) for i in range(2)]
    w1slab = [sbuf(f"Pw1s{i}", (128, 8 * 512), FP8) for i in range(2)]
    rch = pt[0][:, 0:512]
    rt1 = pt[0][:, 512:1024]
    rt2 = pt[1][:, 0:512]

    ysb = sbuf("1ysb", (32, S), BF16)
    qlp = sbuf("1qlp", (64, TOK), BF16)
    yor = sbuf("1yor", (32, TOK), BF16)
    stat = sbuf("1stat", (8, 1024), F32)
    stato = sbuf("1stato", (4, 1024), F32)
    xbf = [sbuf(f"1xbf{i}", (128, 8 * 512), BF16) for i in range(2)]
    x2b = sbuf("1x2b", (128, 8 * 512), BF16)
    csfb = sbuf("1csfb", (128, 512), BF16)
    scfb = sbuf("1scfb", (128, 512), BF16)


    accw = sbuf("0accw", (128, 80), F32)
    wqaf = sbuf("0wqaf", (128, 8 * QL), BF16)
    wkvaf = sbuf("0wkvaf", (128, 8 * KVL), BF16)
    selx = sbuf("0selx", (128, 64), BF16)
    selxo = sbuf("0selxo", (128, 16), BF16)
    selr8 = sbuf("0selr8", (8, 256), F32)
    selr4 = sbuf("0selr4", (4, 256), F32)

    psSl = nc.psum_tensor("psSl", [128, 1024], F32).__enter__()
    psSu = nc.psum_tensor("psSu", [128, 1024], F32).__enter__()
    psO = nc.psum_tensor("psO", [128, 1024], F32).__enter__()
    psG = [nc.psum_tensor(f"psG{i}", [128, 512], F32).__enter__() for i in range(2)]
    psbfv = psSl[:].bitcast(BF16)
    SL0, SL1, SU0, SU1 = "psSl0", "psSl1", "psSu0", "psSu1"
    OL_, OU_, G0, G1 = "psO_l", "psO_u", "psG0", "psG1"
    PSC1 = [(psSl[:, 0:512], SL0), (psSl[:, 512:1024], SL1),
            (psSu[:, 0:512], SU0), (psSu[:, 512:1024], SU1),
            (psO[:, 0:512], OL_), (psO[:, 512:1024], OU_),
            (psG[0][:, 0:512], G0), (psG[1][:, 0:512], G1)]

    sems = {}
    for e in Prog.ENGS:
        sems[("e", e)] = nc.semaphore(f"sem_{e}").__enter__()

    def dsem(name):
        key = ("d", name)
        if key not in sems:
            sems[key] = nc.semaphore(f"ds_{name}").__enter__()
        return sems[key]

    def mm(out_ap, lhsT, rhs, reads, writes, group=None, start=True, stop=True,
           tile_position=None):
        p.op("PE", lambda eng: eng.matmul(out_ap, lhsT, rhs, start=start,
                                          stop=stop, tile_position=tile_position),
             reads=reads, writes=writes, group=group)

    def tr(out_ap, in_ap, ident, reads, writes, group=None):
        p.op("PE", lambda eng: eng.transpose(out_ap, in_ap, ident),
             reads=reads, writes=writes, group=group)

    def act(out_ap, in_ap, func, reads, writes, bias=None, scale=None,
            accum_out=None, fence=None):
        kw = {}
        if bias is not None:
            kw["bias"] = bias
        if scale is not None:
            kw["scale"] = scale
        if accum_out is not None:
            kw["accum_out"] = accum_out
        p.op("ACT", lambda eng: eng.activation(out_ap, in_ap, func, **kw),
             reads=reads, writes=writes, fence=fence)

    def v(meth, *args, reads=(), writes=(), fence=None, **kw):
        p.op("DVE", lambda eng: getattr(eng, meth)(*args, **kw),
             reads=reads, writes=writes, fence=fence)

    def g(meth, *args, reads=(), writes=(), fence=None, **kw):
        p.op("GP", lambda eng: getattr(eng, meth)(*args, **kw),
             reads=reads, writes=writes, fence=fence)

    def dma(out_ap, in_ap, reads, writes, q="SP", fence=None, sem="ld0"):
        p.dma(q, lambda eng: eng.dma_start(out=out_ap, in_=in_ap),
              reads=reads, writes=writes, fence=fence, sem=sem)

    ones_row = ones_r[:]

    for dst, src, tag in [
        (tri[:], tri_d[:], "tri"), (idb[:], idb_d[:], "idb"),
        (idf[:], idf_d[:], "idf"), (halfsel[:], halfsel_d[:], "halfsel"),
        (bout_s[:], bout_d[:], "bout"), (b2_s[:], b2_d[:], "b2"),
        (killrow[:], kill_d[:], "kill"),
        (selr8[:], selr8_d[:], "selr8"), (selr4[:], selr4_d[:], "selr4"),
    ]:
        dma(dst, src, reads=(), writes=(tag,))
    dma(wqaf[:].rearrange("p (c r) -> p c r", r=QL),
        wqa_d.rearrange("(c p) r -> p c r", p=128), reads=(), writes=("wqaf",),
        q="GP", sem="ldg")
    dma(wkvaf[:].rearrange("p (c r) -> p c r", r=KVL),
        wkva_d.rearrange("(c p) r -> p c r", p=128), reads=(), writes=("wkvaf",),
        q="GP", sem="ldg")
    for dst, src, tag in [
        (wqb[:], wqbp_d[:], "wqb"), (wkvbk[:], wkvbk_d[:], "wkvbk"),
        (wqbB[:], wqbpB_d[:], "wqbB"), (wkvbkB[:], wkvbkB_d[:], "wkvbkB"),
        (wkvbv[:], wkvbv_d[:], "wkvbv"), (selx[:], selx_d[:], "selx"),
        (selxo[:], selxo_d[:], "selxo"), (selrl[:], selrl_d[:], "selrl"),
    ]:
        dma(dst, src, reads=(), writes=(tag,), q="GP", sem="ldg")
    dma(k3rhs[0:1, :], b1row_d[:], reads=(), writes=("k3r0",), q="GP", sem="ldg")
    for dst, src_d, dtag in ((g1t, ln1g_d, "g1t"), (b1t, ln1b_d, "b1t"),
                             (g2t, ln2g_d, "g2t"), (b2t, ln2b_d, "b2t")):
        dma(dst[:], src_d.partition_broadcast(128).rearrange("p o n -> p (o n)"),
            reads=(), writes=(dtag,), q="GP", sem="ldg")

    v("memset", ones_c[:], 1.0, writes=("onesc",))
    v("memset", ones_r[:], 1.0, writes=("onesc",))
    v("memset", epscol[:], float(EPS), writes=("epsc",))
    v("memset", ones_cb[:], 1.0, writes=("onescb",))
    v("memset", bnb[:, 102:103], 128.0, writes=("c128",))
    v("memset", n128[:], -128.0, writes=("n128",))
    v("memset", k3lhs[0:3, :], -128.0, writes=("k3l12",))
    v("memset", vsb[:].rearrange("p (t x) -> p t x", x=65)[:, :, 64:65], 1.0,
      writes=("vones",))

    for tag, wtag, wf, rank, dst, bank, btag in (
            ("qa", "wqaf", wqaf, QL, wqc, psG[0], G0),
            ("kv", "wkvaf", wkvaf, KVL, wkvc, psG[1], G1)):
        for c in range(8):
            mm(bank[0:1, 0:rank], ones_cb[:], wf[:, c * rank:(c + 1) * rank],
               reads=("onescb", wtag), writes=(btag,),
               group=f"gcol{tag}", start=(c == 0), stop=(c == 7))
        v("tensor_scalar", small[0:1, 0:rank], bank[0:1, 0:rank], 1.0 / float(E),
          None, op0=ALU.mult, reads=(btag,), writes=(f"cs{tag}",))
        mm(bank[:, 0:rank], ones_row, small[0:1, 0:rank],
           reads=("onesc", f"cs{tag}"), writes=(btag,))
        for c in range(8):
            v("tensor_tensor", dst[:, c * rank:(c + 1) * rank],
              wf[:, c * rank:(c + 1) * rank], bank[:, 0:rank], op=ALU.subtract,
              reads=(wtag, btag), writes=(f"wadj{tag}",))

    wab = [sbuf(f"0wab{i}", (128, HID // 2), F32) for i in range(2)]

    def wabs_gen(src, nchunk, width, acc_off):
        hw_ = width // 2
        for i in range(nchunk):
            for hf in range(2):
                j = 2 * i + hf
                buf = wab[j % 2]
                qw = hw_ // 2
                for ha in range(2):
                    dma(buf[:, ha * qw:(ha + 1) * qw],
                        src.rearrange("(c p) h -> p c h", p=128)
                        [:, i, hf * hw_ + ha * qw:hf * hw_ + (ha + 1) * qw],
                        reads=(), writes=(f"wab{j % 2}{'ab'[ha]}",),
                        sem=f"wab{j % 2}{'ab'[ha]}")
                act(buf[:, 0:hw_], buf[:, 0:hw_], AF.Abs,
                    reads=(f"wab{j % 2}a", f"wab{j % 2}b"),
                    writes=(f"wab{j % 2}a", f"wab{j % 2}b",
                            f"accw{2 * acc_off + j}"),
                    accum_out=accw[:, 2 * acc_off + j:2 * acc_off + j + 1])
                yield

    def _wab_chain():
        yield from wabs_gen(w1T_d, 8, HID, 0)
        yield from wabs_gen(w2T_d, 32, E, 8)
    wab_iter = _wab_chain()

    def wab_drain(k):
        for _ in range(k):
            try:
                next(wab_iter)
            except StopIteration:
                return

    def wsc_block():
        accreads = tuple(f"accw{i}" for i in range(80))
        mm(psG[0][0:1, 0:80], ones_c[:], accw[:, 0:80],
           reads=accreads + ("onesc",), writes=(G0,))
        v("reduce_sum", rowtmp[0:1, 0:1], psG[0][0:1, 0:16], axis=AX.X,
          reads=(G0,), writes=("s1sum",))
        v("reduce_sum", rowtmp[0:1, 1:2], psG[0][0:1, 16:80], axis=AX.X,
          reads=(G0,), writes=("s2sum",))
        v("tensor_scalar", rowtmp[0:1, 2:4], rowtmp[0:1, 0:2],
          1.0 / float(E * HID), 1e-5, op0=ALU.mult, op1=ALU.max,
          reads=("s1sum", "s2sum"), writes=("wmean",))
        v("reciprocal", rowtmp[0:1, 4:6], rowtmp[0:1, 2:4], reads=("wmean",),
          writes=("winv",))
        mm(psG[0][:, 0:4], ones_row, rowtmp[0:1, 2:6],
           reads=("wmean", "winv", "onesc"), writes=(G0,))
        v("tensor_copy", wsc[:, 0:4], psG[0][:, 0:4], reads=(G0,),
          writes=("wsc",))


    def weight_prep_steps():
        def ternarize(srcd, nchunk, width, scol, sink):
            hw_ = width // 2
            for i in range(nchunk):
                for hf in range(2):
                    j = 2 * i + hf
                    buf = wst[j % 2]
                    qw2 = hw_ // 2
                    for ha in range(2):
                        dma(buf[:, ha * qw2:(ha + 1) * qw2],
                            srcd.rearrange("(c p) h -> p c h", p=128)
                            [:, i, hf * hw_ + ha * qw2:hf * hw_ + (ha + 1) * qw2],
                            reads=(), writes=(f"wst{j % 2}{'ab'[ha]}",),
                            sem=f"wst{j % 2}{'ab'[ha]}")
                    for qtr in range(0, hw_, 256):
                        w_ = min(256, hw_ - qtr)
                        htag = f"wst{j % 2}{'ab'[qtr * 2 // hw_]}"
                        v("tensor_scalar", tzq[0][:, 0:w_], buf[:, qtr:qtr + w_],
                          wsc[:, scol + 2:scol + 3], 0.5, op0=ALU.mult,
                          op1=ALU.is_ge, reads=(htag, "wsc"),
                          writes=("tzq0",))
                        v("tensor_scalar", tzq[1][:, 0:w_], buf[:, qtr:qtr + w_],
                          wsc[:, scol + 2:scol + 3], -0.5, op0=ALU.mult,
                          op1=ALU.is_le, reads=(htag, "wsc"),
                          writes=("tzq1",))
                        sink(i, hf * hw_ + qtr, w_)
                        yield

        def w1_sink(i, qtr, w_):
            qi = t1qc[0] % 2
            t1qc[0] += 1
            v("tensor_tensor", t1cq[qi][:, 0:w_], tzq[0][:, 0:w_], tzq[1][:, 0:w_],
              op=ALU.subtract, reads=("tzq0", "tzq1"), writes=(f"t1cq{qi}",))
            hb_ = qtr // 512
            j0 = i * 512 + (qtr % 512)
            dma(tern1b[hb_, :, j0:j0 + w_],
                t1cq[qi][:, 0:w_], reads=(f"t1cq{qi}",), writes=(f"t1d{hb_}",),
                sem=f"ternw{qi}")

        def w2_sink(i, qtr, w_):
            v("tensor_tensor", w2t[:, i * E + qtr:i * E + qtr + w_],
              tzq[0][:, 0:w_], tzq[1][:, 0:w_], op=ALU.subtract,
              reads=("tzq0", "tzq1"), writes=(f"t2_{i}",))

        yield from ternarize(w1T_d, 8, HID, 0, w1_sink)
        yield from ternarize(w2T_d, 32, E, 1, w2_sink)

        for hb in range(8):
            slab = w1slab[hb % 2]
            dma(slab[:, 0:HID], tern1b[hb],
                reads=(f"t1d{hb}",), writes=(f"w1s{hb % 2}",), sem=f"c1s{hb % 2}")
            reg, rtag = PSC1[hb]
            for fc in range(8):
                mm(reg[0:1, 0:512], ones_cb[:],
                   slab[:, fc * 512:(fc + 1) * 512],
                   reads=(f"w1s{hb % 2}", "onescb"), writes=(rtag,),
                   group=f"gc1{hb}", start=(fc == 0), stop=(fc == 7))
        for hb in range(8):
            reg, rtag = PSC1[hb]
            for hq in range(2):
                sg = slice(hq * 256, (hq + 1) * 256)
                v("tensor_copy", tzq[0][0:1, 0:256], reg[0:1, sg],
                  reads=(rtag,), writes=("s1hiseg", "tzq0"))
                v("tensor_tensor", tzq[1][0:1, 0:256], reg[0:1, sg],
                  tzq[0][0:1, 0:256], op=ALU.subtract,
                  reads=(rtag, "s1hiseg"), writes=("s1loseg", "tzq1"))
                dma(k3rhs[1:2, hb * 512 + hq * 256:hb * 512 + (hq + 1) * 256],
                    tzq[0][0:1, 0:256], reads=("s1hiseg", "tzq0"),
                    writes=("k3r1",), q="GP", sem="ldg2")
                dma(k3rhs[2:3, hb * 512 + hq * 256:hb * 512 + (hq + 1) * 256],
                    tzq[1][0:1, 0:256], reads=("s1loseg", "tzq1"),
                    writes=("k3r2",), q="GP", sem="ldg2")

        for ec in range(8):
            for hc in range(32):
                mm(psG[1][0:1, 0:128], ones_cb[:],
                   w2t[:, hc * E + ec * 128:hc * E + (ec + 1) * 128],
                   reads=(f"t2_{hc}", "onescb"), writes=(G1,),
                   group=f"gc2{ec}", start=(hc == 0), stop=(hc == 31))
            v("tensor_copy", c2pair[0:1, ec * 128:(ec + 1) * 128],
              psG[1][0:1, 0:128], reads=(G1,), writes=("c2hi",))
            v("tensor_tensor", tzq[1][0:1, 0:128], psG[1][0:1, 0:128],
              c2pair[0:1, ec * 128:(ec + 1) * 128], op=ALU.subtract,
              reads=(G1, "c2hi"), writes=("c2loseg", "tzq1"))
            dma(c2pair[1:2, ec * 128:(ec + 1) * 128], tzq[1][0:1, 0:128],
                reads=("c2loseg", "tzq1"), writes=("c2lo",), q="GP", sem="ldg2")

    def xpass(src_ap, nblk, wadj, wmat, rank, ysink, selb, selbt, statps,
              statr0, statr1, sel_w, okind):
        for b_ in range(nblk):
            buf = xbf[b_ % 2]
            tag = f"xb{b_ % 2}"
            al = "tza" if b_ % 2 == 0 else "tzb"
            dma(buf[:], src_ap[:, :, b_ * 512:(b_ + 1) * 512],
                reads=(), writes=(tag, al), q="GP", sem=f"xbf{b_ % 2}")
            act(x2b[:], buf[:], AF.Square, reads=(tag,), writes=("x2b", "t1c"))
            for c in range(8):
                mm(psG[0][0:rank, 0:512], wmat[:, c * rank:(c + 1) * rank],
                   buf[:, c * 512:(c + 1) * 512],
                   reads=(tag, wadj), writes=(G0,), group=f"gy{okind}{b_}",
                   start=(c == 0), stop=(c == 7))
            ysink(b_, buf, tag)
            wab_drain(7)
            for c in range(8):
                mm(statps[0:sel_w, 0:512],
                   selb[:, b_ * sel_w:(b_ + 1) * sel_w],
                   x2b[:, c * 512:(c + 1) * 512], reads=("x2b", selbt),
                   writes=(statr0,), group="gstatA" + okind,
                   start=(b_ == 0 and c == 0),
                   stop=(b_ == nblk - 1 and c == 7))
                mm(statps[0:sel_w, 512:1024],
                   selb[:, b_ * sel_w:(b_ + 1) * sel_w],
                   buf[:, c * 512:(c + 1) * 512], reads=(tag, selbt),
                   writes=(statr1,), group="gstatB" + okind,
                   start=(b_ == 0 and c == 0),
                   stop=(b_ == nblk - 1 and c == 7))

    def ysink_full(b_, buf, tag):
        v("tensor_copy", ysb[0:32, b_ * 512:(b_ + 1) * 512], psG[0][0:32, 0:512],
          reads=(G0,), writes=(f"ysb{b_}",))

    xpass(xT_d.rearrange("(c p) t -> p c t", p=128), 8, "wadjkv", wkvc, 32,
          ysink_full, selx, "selx", psO, OL_, OU_, 8, "f")

    def ysink_own(b_, buf, tag):
        v("tensor_copy", qlp[0:64, b_ * 512:(b_ + 1) * 512], psG[0][0:64, 0:512],
          reads=(G0,), writes=(f"qlp{b_}",))
        for c in range(8):
            mm(psG[1][0:32, 0:512], wkvc[:, c * 32:(c + 1) * 32],
               buf[:, c * 512:(c + 1) * 512],
               reads=(tag, "wadjkv"), writes=(G1,), group=f"gyo{b_}",
               start=(c == 0), stop=(c == 7))
        v("tensor_copy", yor[0:32, b_ * 512:(b_ + 1) * 512], psG[1][0:32, 0:512],
          reads=(G1,), writes=(f"yor{b_}",))

    xpass(xoT_d.rearrange("(c p) t -> p c t", p=128), 4, "wadjqa", wqc, 64,
          ysink_own, selxo, "selxo", psSl, SL0, SL1, 4, "o")

    def finstats(pst, ptags, n, dst, dtag):
        v("tensor_scalar", dst[0:n, :], pst[0:n, 0:1024], 1.0 / float(E),
          None, op0=ALU.mult, reads=ptags, writes=(dtag + "m",))
        v("tensor_tensor", dst[0:n, 512:1024], dst[0:n, 512:1024],
          dst[0:n, 512:1024], op=ALU.mult, reads=(dtag + "m",),
          writes=(dtag + "m",))
        v("tensor_tensor", dst[0:n, 0:512], dst[0:n, 0:512],
          dst[0:n, 512:1024], op=ALU.subtract, reads=(dtag + "m",),
          writes=(dtag + "var",))
        act(dst[0:n, 0:512], dst[0:n, 0:512], AF.Ln, bias=epscol[0:n, 0:1],
            scale=1.0, reads=(dtag + "var", "epsc"), writes=(dtag + "ln",))
        act(dst[0:n, 0:512], dst[0:n, 0:512], AF.Exp, scale=-0.5,
            reads=(dtag + "ln",), writes=(dtag + "r",))

    finstats(psO, (OL_, OU_), 8, stat, "sf")
    finstats(psSl, (SL0, SL1), 4, stato, "so")

    for b_ in range(8):
        mm(psG[0][0:32, 0:512], selr8[:, b_ * 32:(b_ + 1) * 32], stat[0:8, 0:512],
           reads=("sfr", "selr8"), writes=(G0,))
        v("tensor_tensor", ysb[0:32, b_ * 512:(b_ + 1) * 512],
          ysb[0:32, b_ * 512:(b_ + 1) * 512], psG[0][0:32, 0:512], op=ALU.mult,
          reads=(f"ysb{b_}", G0), writes=(f"yln{b_}",))
    for b_ in range(4):
        mm(psG[0][0:64, 0:512], selr4[:, b_ * 64:(b_ + 1) * 64], stato[0:4, 0:512],
           reads=("sor", "selr4"), writes=(G0,))
        v("tensor_tensor", qlln[0:64, b_ * 512:(b_ + 1) * 512],
          qlp[0:64, b_ * 512:(b_ + 1) * 512], psG[0][0:64, 0:512], op=ALU.mult,
          reads=(f"qlp{b_}", G0), writes=(f"qlln{b_}",))
        mm(psG[1][0:32, 0:512], selr4[0:4, b_ * 64:b_ * 64 + 32], stato[0:4, 0:512],
           reads=("sor", "selr4"), writes=(G1,))
        v("tensor_tensor", ylnown[0:32, b_ * 512:(b_ + 1) * 512],
          yor[0:32, b_ * 512:(b_ + 1) * 512], psG[1][0:32, 0:512], op=ALU.mult,
          reads=(f"yor{b_}", G1), writes=(f"ylnown{b_}",))

    def rope(dst, dtag, dcol, psA, ptagA, psB, ptagB, t1_ap, t1tag, t2_ap,
             t2tag, width):
        v("tensor_tensor", rt1[:, 0:width], psA, t1_ap, op=ALU.mult,
          reads=(ptagA, t1tag), writes=("rt1", "pt0a"))
        v("tensor_tensor", rt2[:, 0:width], psB, t2_ap, op=ALU.mult,
          reads=(ptagB, t2tag), writes=("rt2", "pt1a"))
        v("tensor_tensor", dst[:, dcol:dcol + width], rt1[:, 0:width],
          rt2[:, 0:width], op=ALU.add, reads=("rt1", "rt2"), writes=(dtag,))

    for b_ in range(8):
        dma(csfb[:], csf_d[:, b_ * 512:(b_ + 1) * 512], reads=(),
            writes=("csfb",), sem="csfb")
        dma(scfb[:], scf_d[:, b_ * 512:(b_ + 1) * 512], reads=(),
            writes=("scfb",), sem="scfb")
        for g_ in range(2):
            mm(psG[0][:, 0:512], wkvbk[:, g_ * 128:(g_ + 1) * 128],
               ysb[0:32, b_ * 512:(b_ + 1) * 512],
               reads=(f"yln{b_}", "wkvbk"), writes=(G0,))
            mm(psG[1][:, 0:512], wkvbkB[:, g_ * 128:(g_ + 1) * 128],
               ysb[0:32, b_ * 512:(b_ + 1) * 512],
               reads=(f"yln{b_}", "wkvbkB"), writes=(G1,))
            rope(ksb[g_], f"ksbt{g_}", b_ * 512, psG[0][:, 0:512], G0,
                 psG[1][:, 0:512], G1, csfb[:], "csfb", scfb[:], "scfb", 512)
        for tt in range(4):
            kt = b_ * 4 + tt
            mm(psSu[:, 0:256], ysb[0:32, kt * 128:(kt + 1) * 128], wkvbv[:],
               reads=(f"yln{b_}", "wkvbv"), writes=(SU0,))
            v("tensor_copy",
              vsb[:].rearrange("p (t g x) -> p t g x", g=4, x=65)[:, kt, :, 0:64],
              psSu[:, 0:256].rearrange("p (g x) -> p g x", x=64),
              reads=(SU0,), writes=("vsbt",))

    v("reduce_max", small[:, 1:2], ksb[0][:], axis=AX.X, apply_absolute_value=True,
      reads=("ksbt0",), writes=("mk0",))
    wab_drain(10 ** 6)
    wsc_block()
    v("reduce_max", small[:, 2:3], ksb[1][:], axis=AX.X, apply_absolute_value=True,
      reads=("ksbt1",), writes=("mk1",))
    v("tensor_tensor", small[:, 1:2], small[:, 1:2], small[:, 2:3], op=ALU.max,
      reads=("mk0", "mk1"), writes=("mk",))

    wp_iter = weight_prep_steps()
    wp_count = [0]

    def wp_drain(k, cap=None):
        for _ in range(k):
            if cap is not None and wp_count[0] >= cap:
                return
            try:
                next(wp_iter)
                wp_count[0] += 1
            except StopIteration:
                return

    tail_pend = [None]

    def tail_drain(k):
        g_t = tail_pend[0]
        if g_t is None:
            return
        for _ in range(k):
            try:
                next(g_t)
            except StopIteration:
                tail_pend[0] = None
                return

    p.fence("p1done")
    sfree("0wab1", "0wab0", "0selr4", "0selr8", "0selxo", "0selx",
          "0wkvaf", "0wqaf", "0accw")
    sfree("1scfb", "1csfb", "1x2b", "1xbf1", "1xbf0", "1stato", "1stat",
          "1yor", "1qlp", "1ysb")
    FN = "p1done"

    x1 = sbuf("2x1", (128, 8 * SLOT), BF16)
    hsb = sbuf("2hsb", (128, 2 * HID), BF16)
    u1b = sbuf("2u1b", (128, E), BF16)
    x1tm = u1b
    sca = sbuf("2sca", (128, 512), F32)
    sda = sbuf("2sda", (128, 512), I32)
    sba = sbuf("2sba", (128, 512), BF16)
    saa = sbuf("2saa", (128, 512), BF16)
    dq1t = sbuf("2dq1t", (128, 2 * E), BF16)
    pts = [pt[0][:], pt[1][:],
           sca[:].bitcast(BF16), sda[:].bitcast(BF16)]
    PTTAG = [("pt0",), ("pt1",), ("pt2", "sca"), ("pt3", "sda")]
    _hf32 = hsb[:].bitcast(F32)
    wst = [_hf32[:, 0:HID // 2], _hf32[:, HID // 2:HID]]
    tzq = [dq1t[:, 0:256], dq1t[:, 256:512]]
    _t1f8 = dq1t[:, 512:768].bitcast(FP8)
    t1cq = [_t1f8[:, 0:256], _t1f8[:, 256:512]]
    t1qc = [0]
    dq2t = sbuf("2dq2t", (128, 32 * 256), BF16)
    otn = dq2t[:, 0:4096]
    olc = dq2t[:, 4096:8192]
    kos = [sbuf(f"2kos{g_}", (128, SLOT), BF16) for g_ in range(2)]
    vos = sbuf("2vos", (128, 4 * 260), BF16)
    cso = sbuf("2cso", (128, SLOT), BF16)
    sco = sbuf("2sco", (128, SLOT), BF16)
    woc = [sbuf(f"2woc{i}", (128, 512), BF16) for i in range(2)]
    xoc = [sbuf(f"2xoc{i}", (128, SLOT), F32) for i in range(2)]
    outst = [sbuf(f"2outst{i}", (128, 256), F32) for i in range(2)]
    scl2t = sbuf("2scl2t", (128, 256), BF16)
    rows2 = sbuf("2rows2", (1, 256), F32)
    lg = sbuf("2lg", (16, 512), BF16)
    linv = sbuf("2linv", (16, 512), BF16)
    olhi0 = sbuf("2olhi", (65, SLOT), BF16)
    olhi = [olhi0, sca[:].bitcast(BF16)[0:65, 0:512]]

    ARN = "dq2t_arena"

    for s_ in range(NSLOT):
        sc0 = s_ * SLOT
        fn = FN if s_ == 0 else None

        dma(cso[:], cso_d[:, sc0:sc0 + SLOT], reads=(), writes=("cso",), fence=fn, sem="cso")
        dma(sco[:], sco_d[:, sc0:sc0 + SLOT], reads=(), writes=("sco",), fence=fn, sem="sco")

        for c in range(8):
            mm(psSl[:, 0:512], wqb[:, c * 128:(c + 1) * 128],
               qlln[:, sc0:sc0 + SLOT], reads=(f"qlln{s_}", "wqb"), writes=(SL0,))
            mm(psSl[:, 512:1024], wqbB[:, c * 128:(c + 1) * 128],
               qlln[:, sc0:sc0 + SLOT], reads=(f"qlln{s_}", "wqbB"), writes=(SL1,))
            rope(qr, "qr", c * 512, psSl[:, 0:512], SL0, psSl[:, 512:1024], SL1,
                 cso[:], "cso", sco[:], "sco", 512)

        for g_ in range(2):
            mm(psSu[:, 0:512], wkvbk[:, g_ * 128:(g_ + 1) * 128],
               ylnown[0:32, sc0:sc0 + SLOT],
               reads=(f"ylnown{s_}", "wkvbk"), writes=(SU0,))
            mm(psSu[:, 512:1024], wkvbkB[:, g_ * 128:(g_ + 1) * 128],
               ylnown[0:32, sc0:sc0 + SLOT],
               reads=(f"ylnown{s_}", "wkvbkB"), writes=(SU1,))
            rope(kos[g_], f"kos{g_}", 0, psSu[:, 0:512], SU0,
                 psSu[:, 512:1024], SU1, cso[:], "cso", sco[:], "sco", 512)
        v("memset", vos[:].rearrange("p (t x) -> p t x", x=65)[:, :, 64:65], 1.0,
          writes=("vos",), fence=fn)
        for tt in range(4):
            mm(psO[:, 0:256], ylnown[0:32, sc0 + tt * 128:sc0 + (tt + 1) * 128],
               wkvbv[:], reads=(f"ylnown{s_}", "wkvbv"), writes=(OL_,))
            v("tensor_copy",
              vos[:].rearrange("p (t g x) -> p t g x", g=4, x=65)[:, tt, :, 0:64],
              psO[:, 0:256].rearrange("p (g x) -> p g x", x=64),
              reads=(OL_,), writes=("vos",))

        v("reduce_max", small[:, 0:1], qr[:], axis=AX.X, apply_absolute_value=True,
          reads=("qr",), writes=("mq",))
        v("tensor_tensor", small[:, 3:4], small[:, 0:1], small[:, 1:2],
          op=ALU.mult, reads=("mq", "mk"), writes=("mqk",))
        pb = psO[:, 512:1024]
        mm(pb[0:2, 0:1], halfsel[:], small[:, 3:4],
           reads=("mqk", "halfsel"), writes=(OU_,))
        v("tensor_copy", small2[0:2, 0:1], pb[0:2, 0:1], reads=(OU_,),
          writes=("bb2",))
        tr(pb[0:1, 0:2], small2[0:2, 0:1], idf[0:2, 0:2], reads=("bb2", "idf"),
           writes=(OU_,))
        v("reduce_max", rowtmp[0:1, 6:7], pb[0:1, 0:2], axis=AX.X,
          reads=(OU_,), writes=("bmax",))
        v("tensor_scalar", rowtmp[0:1, 7:8], rowtmp[0:1, 6:7], -SCALE, None,
          op0=ALU.mult, reads=("bmax",), writes=("negbrow",))
        mm(pb[:, 0:1], ones_row, rowtmp[0:1, 7:8],
           reads=("negbrow", "onesc"), writes=(OU_,))
        v("tensor_copy", negb[:], pb[:, 0:1], reads=(OU_,), writes=("negb",))
        mm(pb[:, 0:NL], ones_row, killrow[:], reads=("kill", "onesc"),
           writes=(OU_,))
        v("tensor_scalar", biastab[:], pb[:, 0:NL], negb[:], None,
          op0=ALU.add, reads=(OU_, "negb"), writes=("biastab",))

        L = LPAD[s_]
        SREG = [(psSl, (SL0,), (SL1,), True),
                (psSu, (SU0,), (SU1,), True),
                (None, (G0,), (G1,), False)]
        ucnt = [0]

        def s_unit(lhs0, lhs1, rhs_ap, rtags, bias_ap, btag, wexp, ptwr):
            u = ucnt[0] % 3
            pu = ucnt[0] % 4
            ucnt[0] += 1
            reg, tg0, tg1, contig = SREG[u]
            half0 = reg[:, 0:512] if contig else psG[0][:, 0:512]
            half1 = reg[:, 512:1024] if contig else psG[1][:, 0:512]
            mm(half0[:, 0:wexp], lhs0, rhs_ap, reads=rtags, writes=tg0,
               tile_position=(0, 0) if lhs0.shape[0] == 64 and True else None)
            ptbuf = pts[pu]
            if lhs1 is not None:
                mm(half1[:, 0:wexp], lhs1, rhs_ap, reads=rtags, writes=tg1,
                   tile_position=None)
            if contig and lhs1 is not None:
                act(ptbuf[:, 0:1024], reg[:], AF.Exp, bias=bias_ap, scale=SCALE,
                    reads=tg0 + tg1 + (btag,), writes=PTTAG[pu])
            else:
                act(ptbuf[:, 0:wexp], half0[:, 0:wexp], AF.Exp, bias=bias_ap,
                    scale=SCALE, reads=tg0 + (btag,), writes=PTTAG[pu])
                if lhs1 is not None:
                    act(ptbuf[:, 512:512 + wexp], half1[:, 0:wexp], AF.Exp,
                        bias=bias_ap, scale=SCALE, reads=tg1 + (btag,),
                        writes=PTTAG[pu])
            return ptbuf, PTTAG[pu]

        for c in range(8):
            g_ = c // 4
            gl = f"gO{s_}_{c}l"
            gu = f"gO{s_}_{c}u"
            pvq = []

            def pv_flush(keep):
                while len(pvq) > keep:
                    pvq.pop(0)()

            for t0 in range(0, L, 2):
                bias_ap = biastab[:, LOFF[s_] + t0:LOFF[s_] + t0 + 1]
                for hh in range(2):
                    ro = hh * 64
                    tp = (0, 0) if hh == 0 else (64, 0)
                    u = ucnt[0] % 3
                    pu = ucnt[0] % 4
                    ucnt[0] += 1
                    reg, tg0, tg1, contig = SREG[u]
                    half0 = reg[:, 0:512] if contig else psG[0][:, 0:512]
                    half1 = reg[:, 512:1024] if contig else psG[1][:, 0:512]
                    for dt_ in range(2):
                        t = t0 + dt_
                        mm(half0 if dt_ == 0 else half1,
                           ksb[g_][ro:ro + 64, t * 128:(t + 1) * 128],
                           qr[ro:ro + 128 - 64, c * 512:(c + 1) * 512]
                           if False else qr[ro:ro + 64, c * 512:(c + 1) * 512],
                           reads=(f"ksbt{g_}", "qr"),
                           writes=tg0 if dt_ == 0 else tg1,
                           tile_position=tp)
                    ptbuf = pts[pu]
                    if contig:
                        act(ptbuf[:, 0:1024], reg[:], AF.Exp, bias=bias_ap,
                            scale=SCALE, reads=tg0 + tg1 + ("biastab",),
                            writes=PTTAG[pu])
                    else:
                        act(ptbuf[:, 0:512], half0, AF.Exp, bias=bias_ap,
                            scale=SCALE, reads=tg0 + ("biastab",),
                            writes=PTTAG[pu])
                        act(ptbuf[:, 512:1024], half1, AF.Exp, bias=bias_ap,
                            scale=SCALE, reads=tg1 + ("biastab",),
                            writes=PTTAG[pu])
                    og = gl if hh == 0 else gu
                    otag = OL_ if hh == 0 else OU_
                    oc0 = 0 if hh == 0 else 512
                    vg = g_ if hh == 0 else g_ + 2

                    def mkpv(t0=t0, oc0=oc0, vg=vg, og=og, otag=otag,
                             ptbuf=ptbuf, pu=pu):
                        def doit():
                            for dt_ in range(2):
                                t = t0 + dt_
                                mm(psO[0:65, oc0:oc0 + 512],
                                   vsb[:, t * 260 + vg * 65:t * 260 + vg * 65 + 65],
                                   ptbuf[:, dt_ * 512:(dt_ + 1) * 512],
                                   reads=PTTAG[pu] + ("vsbt", "vones"),
                                   writes=(otag,), group=og,
                                   start=(t0 == 0 and dt_ == 0), stop=False)
                        return doit
                    pvq.append(mkpv())
                    pv_flush(2)
                    if s_ > 0:
                        tail_drain(1)
            for kk in range(4):
                w_ = SLOT - kk * 128
                q0 = c * 512 + kk * 128
                for hh in range(2):
                    ro = hh * 64
                    tp = (0, 0) if hh == 0 else (64, 0)
                    u = ucnt[0] % 3
                    pu = ucnt[0] % 4
                    ucnt[0] += 1
                    reg, tg0, tg1, contig = SREG[u]
                    half0 = reg[:, 0:512] if contig else psG[0][:, 0:512]
                    mm(half0[:, 0:w_],
                       kos[g_][ro:ro + 64, kk * 128:(kk + 1) * 128],
                       qr[ro:ro + 64, q0:q0 + w_],
                       reads=(f"kos{g_}", "qr"), writes=tg0, tile_position=tp)
                    ptbuf = pts[pu]
                    act(ptbuf[:, 0:w_], half0[:, 0:w_], AF.Exp, bias=negb[:],
                        scale=SCALE, reads=tg0 + ("negb",), writes=PTTAG[pu])
                    v("tensor_tensor", ptbuf[:, 0:128], ptbuf[:, 0:128], tri[:],
                      op=ALU.mult, reads=PTTAG[pu] + ("tri",), writes=PTTAG[pu])
                    og = gl if hh == 0 else gu
                    otag = OL_ if hh == 0 else OU_
                    oc0 = (0 if hh == 0 else 512) + kk * 128
                    vg = g_ if hh == 0 else g_ + 2

                    def mkpvi(kk=kk, oc0=oc0, hh=hh, vg=vg, og=og, otag=otag,
                              ptbuf=ptbuf, pu=pu, w_=w_):
                        def doit():
                            mm(psO[0:65, oc0:(512 if hh == 0 else 1024)],
                               vos[:, kk * 260 + vg * 65:kk * 260 + vg * 65 + 65],
                               ptbuf[:, 0:w_], reads=PTTAG[pu] + ("vos",),
                               writes=(otag,), group=og, start=False,
                               stop=(kk == 3))
                        return doit
                    pvq.append(mkpvi())
                    pv_flush(2)
                    if s_ > 0:
                        tail_drain(1)
            if s_ > 0:
                tail_drain(10 ** 6)
            arntags = (tuple(f"dq2c{t_}_{k_}" for t_ in range(2) for k_ in range(8))
                       + tuple(f"otn{i}" for i in range(8)))
            pv_flush(0)
            v("tensor_copy", olc[0:65, c * 512:(c + 1) * 512], psO[0:65, 0:512],
              reads=(OL_,), writes=(f"oll{c}", f"olll{c}") + arntags, fence=fn)
            v("tensor_copy", olhi[c % 2][0:65, :], psO[0:65, 512:1024],
              reads=(OU_,), writes=(f"olh{c % 2}",) + (("sca",) if c % 2 else ()))
            st_ = olhi[c % 2]
            dma(lg[c:c + 1, :], olc[64:65, c * 512:(c + 1) * 512],
                reads=(f"olll{c}",), writes=(f"lgl{c}",), q="GP", sem="slg")
            dma(lg[8 + c:9 + c, :], st_[64:65, :], reads=(f"olh{c % 2}", "sca"),
                writes=(f"lgu{c}",), q="GP", sem="slg")
            dma(olc[64:128, c * 512:(c + 1) * 512], st_[0:64, :],
                reads=(f"olh{c % 2}", "sca", f"lgl{c}"),
                writes=(f"olu{c}", f"olll{c}"),
                q="SP", sem=f"ssh{c % 2}")

            if s_ == 0:
                wp_drain(44)

        lreads = (tuple(f"lgl{c}" for c in range(8))
                  + tuple(f"lgu{c}" for c in range(8)))
        v("reciprocal", linv[:], lg[:], reads=lreads, writes=("linv",))
        if s_ == 0:
            wp_drain(1000)

        for c in range(8):
            mm(psG[0][:, 0:512], selrl[:, c * 128:(c + 1) * 128], linv[:],
               reads=("linv", "selrl"), writes=(G0,))
            v("tensor_tensor", otn[:, c * 512:(c + 1) * 512],
              olc[:, c * 512:(c + 1) * 512], psG[0][:, 0:512], op=ALU.mult,
              reads=(f"oll{c}", f"olu{c}", G0),
              writes=(f"otn{c}", f"dq2c0_{c // 2}", f"dq2c1_{c // 2}"))
        for ec in range(8):
            for h_ in range(2):
                dma(woc[h_][:].rearrange("p (f x) -> p f x", x=128),
                    woutr_d.rearrange("p (f e) -> p f e", e=E)
                    [:, h_ * 4:(h_ + 1) * 4, ec * 128:(ec + 1) * 128],
                    reads=(), writes=(f"woc{h_}",), q="GP", fence=fn, sem="woc")
            xb = xoc[ec % 2]
            dma(xb[:], xoT_d.rearrange("(c p) t -> p c t", p=128)
                [:, ec, sc0:sc0 + SLOT],
                reads=(), writes=(f"xoc{ec % 2}",), fence=fn,
                sem=f"xoc{ec % 2}")
            pso, ptag = (psG[1], G1) if ec % 2 == 0 else (psG[0], G0)
            for fc in range(8):
                mm(pso[:, 0:512], woc[fc // 4][:, (fc % 4) * 128:(fc % 4 + 1) * 128],
                   otn[:, fc * 512:(fc + 1) * 512],
                   reads=(f"woc{fc // 4}", f"otn{fc}"), writes=(ptag,),
                   group=f"gat{s_}_{ec}", start=(fc == 0), stop=(fc == 7))
            v("scalar_tensor_tensor", x1[:, ec * 512:(ec + 1) * 512],
              pso[:, 0:512], bout_s[:, ec:ec + 1], xb[:], op0=ALU.add,
              op1=ALU.add, reads=(ptag, "bout", f"xoc{ec % 2}"),
              writes=(f"x1_{ec}",), fence=fn)

        def ffn_A(hs):
            for tt2 in range(2):
                tt = hs * 2 + tt2
                tcol = tt * 128
                po = 1024 * (tt % 2)
                ptg = SL0 if tt % 2 == 0 else SL1
                for fc in range(8):
                    tr(psbfv[:, po + fc * 128:po + (fc + 1) * 128],
                       x1[:, fc * 512 + tcol:fc * 512 + tcol + 128], idb[:],
                       reads=(f"x1_{fc}", "idb"), writes=(ptg,),
                       group=f"gx1t{s_}{tt}")
                v("tensor_copy", x1tm[:], psbfv[:, po:po + 1024], reads=(ptg,),
                  writes=("x1tm",), fence=fn)
                v("bn_stats", small[:, 4:10], x1tm[:, 0:512], reads=("x1tm",),
                  writes=("bn1a",))
                v("bn_stats", small[:, 10:16], x1tm[:, 512:1024],
                  reads=("x1tm",), writes=("bn1b",))
                v("bn_aggr", small[:, 16:18], small[:, 4:16],
                  reads=("bn1a", "bn1b"), writes=("bn1",))
                act(small[:, 18:19], small[:, 17:18], AF.Ln, bias=epscol[:],
                    scale=1.0, reads=("bn1", "epsc"), writes=("r1ln",))
                act(small[:, 18:19], small[:, 18:19], AF.Exp,
                    scale=-0.5, reads=("r1ln",), writes=("r1",))
                v("tensor_scalar", bnb[:, 103:104], small[:, 16:17],
                  small[:, 18:19], None, op0=ALU.mult,
                  reads=("bn1", "r1"), writes=("nmr1",))
                for st in range(2):
                    cu = st * 512
                    act(saa[:], x1tm[:, cu:cu + 512], AF.Copy,
                        scale=small[:, 18:19],
                        reads=("x1tm", "r1"), writes=("saa",),
                        fence=fn)
                    v("scalar_tensor_tensor", sba[:], saa[:], bnb[:, 103:104],
                      g1t[:, cu:cu + 512], op0=ALU.subtract, op1=ALU.mult,
                      reads=("saa", "nmr1", "g1t"), writes=("sba",))
                    v("tensor_tensor", x1tm[:, cu:cu + 512], sba[:],
                      b1t[:, cu:cu + 512], op=ALU.add, reads=("sba", "b1t"),
                      writes=("x1tm",))
                    if st == 0:
                        v("reduce_max", small[:, 19:20], x1tm[:, 0:512],
                          axis=AX.X, apply_absolute_value=True,
                          reads=("x1tm",), writes=("m1",))
                    else:
                        v("reduce_max", small[:, 17:18], x1tm[:, 512:1024],
                          axis=AX.X, apply_absolute_value=True,
                          reads=("x1tm",), writes=("m1t",))
                        v("tensor_tensor", small[:, 19:20], small[:, 19:20],
                          small[:, 17:18], op=ALU.max, reads=("m1", "m1t"),
                          writes=("m1",))
                v("tensor_scalar", small[:, 19:20], small[:, 19:20], 1e-5, None,
                  op0=ALU.max, reads=("m1",), writes=("m1",))
                v("reciprocal", small[:, 20:21], small[:, 19:20], reads=("m1",),
                  writes=("m1inv",))
                v("tensor_scalar", small[:, 21:22], small[:, 20:21], 127.0, None,
                  op0=ALU.mult, reads=("m1inv",), writes=("s1",))
                v("tensor_scalar", small[:, 22 + tt2:23 + tt2], small[:, 19:20],
                  wsc[:, 0:1], 1.0 / 127.0, op0=ALU.mult, op1=ALU.mult,
                  reads=("m1", "wsc"), writes=(f"scl1_{tt2}",))
                v("tensor_scalar", small[:, 24:25], small[:, 20:21], wsc[:, 2:3],
                  127.0, op0=ALU.mult, op1=ALU.mult, reads=("m1inv", "wsc"),
                  writes=("rscl",))
                for st in range(2):
                    cu = st * 512
                    act(sca[:], x1tm[:, cu:cu + 512], AF.Copy,
                        scale=small[:, 21:22], bias=128.0,
                        reads=("x1tm", "s1"), writes=("sca",), fence=fn)
                    v("tensor_copy", sda[:], sca[:], reads=("sca",),
                      writes=("sda",), fence=fn)
                    v("scalar_tensor_tensor", sba[:], sca[:], 0.5, sda[:],
                      op0=ALU.subtract, op1=ALU.is_gt, reads=("sca", "sda"),
                      writes=("sba",), fence=fn)
                    v("tensor_tensor", saa[:], sda[:], sba[:], op=ALU.add,
                      reads=("sda", "sba"), writes=("saa",), fence=fn)
                    po = 1024 * (st % 2)
                    ptg = SL0 if st % 2 == 0 else SL1
                    for fc in range(4):
                        tr(psbfv[:, po + fc * 128:po + (fc + 1) * 128],
                           saa[:, fc * 128:(fc + 1) * 128], idb[:],
                           reads=("saa", "idb"), writes=(ptg,),
                           group=f"gd1{s_}{tt}{st}")
                    v("tensor_copy",
                      dq1t[:, tt2 * 1024 + cu:tt2 * 1024 + cu + 512],
                      psbfv[:, po:po + 512], reads=(ptg,),
                      writes=(f"dq1t{tt2}", "tzq0", "tzq1", "t1cq0", "t1cq1"),
                      fence=fn)
                tr(psG[0][0:1, 0:128], small[:, 24:25], idf[:],
                   reads=("rscl", "idf"), writes=(G0,))
                v("tensor_copy", k3lhs[0:1, tt2 * 128:(tt2 + 1) * 128],
                  psG[0][0:1, 0:128], reads=(G0, "k3l12"),
                  writes=(f"k3l0_{tt2}",))

        def ffn_W1_hb(hs, hb):
            slab = w1slab[hb % 2]
            dma(slab[:], tern1b[hb],
                reads=(f"t1d{hb}",), writes=(f"w1s{hb % 2}",), q="GP",
                sem=f"w1s{hb % 2}")
            for tt2 in range(2):
                ph, ptag = (psG[0], G0) if (hb * 2 + tt2) % 2 == 0 else (psG[1], G1)
                gtag = f"gh{s_}{hs}{hb}{tt2}"
                for fc in range(8):
                    mm(ph[:, 0:512],
                       dq1t[:, tt2 * 1024 + fc * 128:tt2 * 1024 + (fc + 1) * 128],
                       slab[:, fc * 512:(fc + 1) * 512],
                       reads=(f"dq1t{tt2}", f"w1s{hb % 2}"), writes=(ptag,),
                       group=gtag, start=(fc == 0), stop=False)
                mm(ph[:, 0:512], k3lhs[0:3, tt2 * 128:(tt2 + 1) * 128],
                   k3rhs[0:3, hb * 512:(hb + 1) * 512],
                   reads=(f"k3l0_{tt2}", "k3l12", "k3r0", "k3r1", "k3r2"),
                   writes=(ptag,), group=gtag, start=False, stop=True)
                act(hsb[:, tt2 * HID + hb * 512:tt2 * HID + (hb + 1) * 512],
                    ph[:, 0:512], AF.Gelu, scale=small[:, 22 + tt2:23 + tt2],
                    reads=(ptag, f"scl1_{tt2}"),
                    writes=(f"h{tt2}_{hb}", f"u2_{hb}",
                            f"wst{tt2}a", f"wst{tt2}b"))
                v("bn_stats", bnb[:, tt2 * 52 + 6 * hb:tt2 * 52 + 6 * hb + 6],
                  hsb[:, tt2 * HID + hb * 512:tt2 * HID + (hb + 1) * 512],
                  reads=(f"h{tt2}_{hb}",), writes=(f"bn2_{tt2}_{hb}",))

        def ffn_L(hs, tt2):
            hreads = tuple(f"h{tt2}_{i}" for i in range(8))
            b0 = tt2 * 52
            s2c = 30 if tt2 == 0 else 25
            v("bn_aggr", bnb[:, b0 + 48:b0 + 50], bnb[:, b0:b0 + 48],
              reads=tuple(f"bn2_{tt2}_{i}" for i in range(8)),
              writes=(f"bn2s{tt2}",))
            act(small[:, 26:27], bnb[:, b0 + 49:b0 + 50], AF.Ln, bias=epscol[:],
                scale=1.0, reads=(f"bn2s{tt2}", "epsc"), writes=("r2ln",))
            act(small[:, 26:27], small[:, 26:27], AF.Exp,
                scale=-0.5, reads=("r2ln",), writes=("r2",))
            v("tensor_scalar", bnb[:, b0 + 50:b0 + 51], bnb[:, b0 + 48:b0 + 49],
              small[:, 26:27], None, op0=ALU.mult,
              reads=(f"bn2s{tt2}", "r2"), writes=(f"nmr2_{tt2}",))
            yield
            for k_ in range(8):
                col = tt2 * HID + 512 * k_
                act(saa[:], hsb[:, col:col + 512], AF.Copy,
                    scale=small[:, 26:27],
                    reads=(f"h{tt2}_{k_}", "r2"), writes=("saa",))
                v("scalar_tensor_tensor", sba[:], saa[:], bnb[:, b0 + 50:b0 + 51],
                  g2t[:, 512 * k_:512 * (k_ + 1)], op0=ALU.subtract,
                  op1=ALU.mult,
                  reads=("saa", f"nmr2_{tt2}", "g2t"), writes=("sba",))
                v("tensor_tensor", hsb[:, col:col + 512], sba[:],
                  b2t[:, 512 * k_:512 * (k_ + 1)], op=ALU.add,
                  reads=("sba", "b2t"), writes=(f"u2_{k_}", f"h{tt2}_{k_}"))
                if k_ == 0:
                    v("reduce_max", small[:, 27:28], hsb[:, col:col + 512],
                      axis=AX.X, apply_absolute_value=True,
                      reads=(f"u2_{k_}",), writes=("m2",))
                else:
                    v("reduce_max", small[:, 28:29], hsb[:, col:col + 512],
                      axis=AX.X, apply_absolute_value=True,
                      reads=(f"u2_{k_}",), writes=("m2t",))
                    v("tensor_tensor", small[:, 27:28], small[:, 27:28],
                      small[:, 28:29], op=ALU.max, reads=("m2", "m2t"),
                      writes=("m2",))
                yield
            v("tensor_scalar", small[:, 27:28], small[:, 27:28], 1e-5, None,
              op0=ALU.max, reads=("m2",), writes=("m2",))
            v("reciprocal", small[:, 29:30], small[:, 27:28], reads=("m2",),
              writes=("m2inv",))
            v("tensor_scalar", small[:, s2c:s2c + 1], small[:, 29:30], 127.0,
              None, op0=ALU.mult, reads=("m2inv",), writes=(f"s2_{tt2}",))
            v("tensor_scalar", small[:, 31:32], small[:, 27:28], wsc[:, 1:2],
              1.0 / 127.0, op0=ALU.mult, op1=ALU.mult, reads=("m2", "wsc"),
              writes=("scl2",))
            tr(psG[0][0:1, 0:128], small[:, 31:32], idf[:],
               reads=("scl2", "idf"), writes=(G0,))
            v("tensor_copy", rows2[0:1, tt2 * 128:(tt2 + 1) * 128],
              psG[0][0:1, 0:128], reads=(G0,), writes=(f"r2row{tt2}",))

        psgbf = [psG[0][:].bitcast(BF16), psG[1][:].bitcast(BF16)]

        def ffn_Q2(hs, k_, qb=None):
            scb, sct, sdb, sdt = qb if qb is not None else \
                (sca[:], "sca", sda[:], "sda")
            for tt2 in range(2):
                col = tt2 * HID + 512 * k_
                s2c = 30 if tt2 == 0 else 25
                act(scb, hsb[:, col:col + 512], AF.Copy,
                    scale=small[:, s2c:s2c + 1], bias=128.0,
                    reads=(f"u2_{k_}", f"s2_{tt2}"), writes=(sct,))
                v("tensor_copy", sdb, scb, reads=(sct,),
                  writes=(sdt,))
                v("scalar_tensor_tensor", sba[:], scb, 0.5, sdb,
                  op0=ALU.subtract, op1=ALU.is_gt, reads=(sct, sdt),
                  writes=("sba",))
                v("tensor_tensor", saa[:], sdb, sba[:], op=ALU.add,
                  reads=(sdt, "sba"), writes=("saa",))
                if qb is None:
                    pbv = psbfv
                    po = 1024 * (tt2 % 2)
                    ptg = SL0 if tt2 % 2 == 0 else SL1
                else:
                    pbv = psgbf[tt2]
                    po = 0
                    ptg = G0 if tt2 == 0 else G1
                for i in range(4):
                    tr(pbv[:, po + i * 128:po + (i + 1) * 128],
                       saa[:, i * 128:(i + 1) * 128], idb[:],
                       reads=("saa", "idb"), writes=(ptg,),
                       group=f"gd2{s_}{hs}{tt2}{k_}")
                hclo = 4 * k_
                if hclo < 16:
                    ovl = (f"otn{hclo // 2}", f"otn{hclo // 2 + 1}")
                else:
                    cc = (hclo - 16) // 2
                    ovl = (f"oll{cc}", f"olu{cc}", f"olll{cc}",
                           f"oll{cc + 1}", f"olu{cc + 1}", f"olll{cc + 1}")
                v("tensor_copy",
                  dq2t[:].rearrange("p (hc t) -> p hc t", t=256)
                  [:, 4 * k_:4 * (k_ + 1), tt2 * 128:(tt2 + 1) * 128],
                  pbv[:, po:po + 512].rearrange("p (i t) -> p i t", t=128),
                  reads=(ptg,), writes=(f"dq2c{tt2}_{k_}",) + ovl)

        def ffn_W2(hs, s_=s_, sc0=sc0):
            mm(psG[1][:, 0:256], ones_row, rows2[0:1, 0:256],
               reads=("r2row0", "r2row1", "onesc"), writes=(G1,))
            v("tensor_copy", scl2t[:], psG[1][:, 0:256], reads=(G1,),
              writes=("scl2t",))
            yield
            for ec in range(8):
                pso, ptag = (psG[0], G0) if ec % 2 == 0 else (psG[1], G1)
                gtag = f"go{s_}{hs}{ec}"
                for hc in range(32):
                    mm(pso[:, 0:256],
                       w2t[:, hc * E + ec * 128:hc * E + (ec + 1) * 128],
                       dq2t[:, hc * 256:(hc + 1) * 256],
                       reads=(f"t2_{hc}", f"dq2c0_{hc // 4}", f"dq2c1_{hc // 4}"),
                       writes=(ptag,), group=gtag, start=(hc == 0), stop=False)
                mm(pso[:, 0:256], c2pair[0:2, ec * 128:(ec + 1) * 128],
                   n128[0:2, 0:256], reads=("c2hi", "c2lo", "n128"),
                   writes=(ptag,), group=gtag, start=False, stop=True)
                ob = outst[ec % 2]
                v("scalar_tensor_tensor", ob[:], pso[:, 0:256], 1.0, scl2t[:],
                  op0=ALU.mult, op1=ALU.mult, reads=(ptag, "scl2t"),
                  writes=(f"ost{ec % 2}",))
                v("scalar_tensor_tensor", ob[:], ob[:], b2_s[:, ec:ec + 1],
                  x1[:, ec * 512 + hs * 256:ec * 512 + (hs + 1) * 256],
                  op0=ALU.add, op1=ALU.add,
                  reads=(f"ost{ec % 2}", "b2", f"x1_{ec}"),
                  writes=(f"ost{ec % 2}",))
                dma(yT_d.rearrange("(c p) t -> p c t", p=128)
                    [:, ec, sc0 + hs * 256:sc0 + (hs + 1) * 256], ob[:],
                    reads=(f"ost{ec % 2}",), writes=(f"od{s_}{hs}{ec}",),
                    sem=f"sout{ec % 2}")
                yield

        ffn_A(0)
        for hb in range(8):
            ffn_W1_hb(0, hb)
        for _ in ffn_L(0, 0):
            pass
        for _ in ffn_L(0, 1):
            pass
        ffn_A(1)
        for k_ in range(8):
            ffn_Q2(0, k_)
            ffn_W1_hb(1, k_)
        for _ in ffn_W2(0):
            pass

        def ffn_tail(fL=ffn_L, fQ2=ffn_Q2, fW2=ffn_W2):
            qb = (xoc[0][:], "xoc0", xoc[1][:].bitcast(I32), "xoc1")
            yield from fL(1, 0)
            yield from fL(1, 1)
            for k_ in range(8):
                fQ2(1, k_, qb=qb)
                yield
            yield from fW2(1)

        if s_ == 3:
            for _ in ffn_tail():
                pass
        else:
            tail_pend[0] = ffn_tail()

    p.fence("final")
    p.op("SP", lambda eng: eng.nop(), fence="final")

    def replay(engkey):
        def fn(eng):
            for waits, f_, dsemname in p.items[engkey]:
                for (semkey, val) in waits:
                    if semkey[0] == "d":
                        dsem(semkey[1])
                    eng.wait_ge(sems[semkey], val)
                try:
                    inst = f_(eng)
                except Exception as e:
                    raise RuntimeError(f"replay {engkey} line "
                                       f"{getattr(f_, '_line', '?')}: {e}") from e
                if dsemname is not None:
                    inst.then_inc(dsem(dsemname), 16)
                else:
                    inst.then_inc(sems[("e", engkey)], 1)
        return fn

    for _sname in list(p.dcnt.keys()):
        dsem(_sname)
    with nc.allow_low_precision(reason="bf16 intermediates are intentional; "
                                "tolerance calibrated for bf16 matmuls"), \
         nc.Block() as blk:
        blk.tensor(replay("PE"))
        blk.scalar(replay("ACT"))
        blk.vector(replay("DVE"))
        blk.sync(replay("SP"))
        blk.gpsimd(replay("GP"))

    return nc, p


def _host_tables():
    inv = 1.0 / (10000.0 ** (np.arange(0, D, 2, dtype=np.float64) / D))
    t = np.arange(S, dtype=np.float64)
    fr = np.outer(inv, t)
    cos, sin = np.cos(fr), np.sin(fr)
    cs = np.ascontiguousarray(np.concatenate([cos, cos, cos, cos], 0).astype(BFNP))
    sc = np.ascontiguousarray(np.concatenate([-sin, sin, -sin, sin], 0).astype(BFNP))

    tri = np.ascontiguousarray(
        (np.arange(128)[:, None] <= np.arange(128)[None, :]).astype(BFNP))
    idb = np.ascontiguousarray(np.eye(128, dtype=BFNP))
    idf = np.ascontiguousarray(np.eye(128, dtype=np.float32))

    selx = np.zeros((8, 128, 8), np.float32)
    for b_ in range(8):
        selx[b_, :, b_] = 1.0
    selxo = np.zeros((4, 128, 4), np.float32)
    for b_ in range(4):
        selxo[b_, :, b_] = 1.0
    selr8 = np.zeros((8, 8 * 32), np.float32)
    for b_ in range(8):
        selr8[b_, b_ * 32:(b_ + 1) * 32] = 1.0
    selr4 = np.zeros((4, 4 * 64), np.float32)
    for b_ in range(4):
        selr4[b_, b_ * 64:(b_ + 1) * 64] = 1.0
    selrl = np.zeros((16, 8, 128), np.float32)
    for c in range(8):
        selrl[c, c, 0:64] = 1.0
        selrl[c + 8, c, 64:128] = 1.0
    halfsel = np.zeros((128, 2), np.float32)
    halfsel[0:64, 0] = 1.0
    halfsel[64:128, 1] = 1.0

    qperm = np.zeros(E, np.int64)
    qpermB = np.zeros(E, np.int64)
    DPART = np.concatenate([np.arange(1, 64, 2), np.arange(0, 64, 2)])
    for c in range(8):
        for r in range(128):
            h_ = c if r < 64 else c + 8
            qperm[c * 128 + r] = h_ * 64 + DPERM[r % 64]
            qpermB[c * 128 + r] = h_ * 64 + DPART[r % 64]
    kperm = np.zeros(256, np.int64)
    kpermB = np.zeros(256, np.int64)
    for g_ in range(2):
        for r in range(128):
            kvh = g_ if r < 64 else g_ + 2
            kperm[g_ * 128 + r] = kvh * 128 + DPERM[r % 64] * 2
            kpermB[g_ * 128 + r] = kvh * 128 + DPART[r % 64] * 2
    vperm = np.zeros(256, np.int64)
    for vh in range(4):
        for d_ in range(64):
            vperm[vh * 64 + d_] = vh * 128 + d_ * 2 + 1
    wrow = np.concatenate(
        [np.concatenate([np.arange(fc * 64, fc * 64 + 64),
                         np.arange((fc + 8) * 64, (fc + 8) * 64 + 64)])
         for fc in range(8)])
    return dict(
        cs=cs, sc=sc, tri=tri, idb=idb, idf=idf,
        selx=np.ascontiguousarray(selx.transpose(1, 0, 2).reshape(128, 64)
                                  .astype(BFNP)),
        selxo=np.ascontiguousarray(selxo.transpose(1, 0, 2).reshape(128, 16)
                                   .astype(BFNP)),
        selr8=selr8, selr4=selr4,
        selrl=np.ascontiguousarray(selrl.transpose(1, 0, 2).reshape(16, 1024)),
        halfsel=halfsel, qperm=qperm, qpermB=qpermB, kperm=kperm,
        kpermB=kpermB, vperm=vperm, wrow=wrow)


_NC_CACHE = {}


def kernel(L0_attn_x, L0_attn_Wqa, L0_attn_Wqb, L0_attn_Wkva, L0_attn_Wkvb,
           L0_attn_Wout, L0_attn_bout, L0_ffn_ln1_g, L0_ffn_ln1_b, L0_ffn_w1,
           L0_ffn_b1, L0_ffn_ln2_g, L0_ffn_ln2_b, L0_ffn_w2, L0_ffn_b2):
    T = _host_tables()
    x = np.asarray(L0_attn_x, np.float32)

    wqb_f = np.asarray(L0_attn_Wqb, np.float32)
    wqbp = np.ascontiguousarray(wqb_f[:, T["qperm"]])
    wqbpB = np.ascontiguousarray(wqb_f[:, T["qpermB"]])
    wkvb = np.asarray(L0_attn_Wkvb, np.float32)
    wkvbk = np.ascontiguousarray(wkvb[:, T["kperm"]])
    wkvbkB = np.ascontiguousarray(wkvb[:, T["kpermB"]])
    wkvbv = np.ascontiguousarray(wkvb[:, T["vperm"]])
    wout = np.asarray(L0_attn_Wout, np.float32)[T["wrow"]]
    woutr = np.ascontiguousarray(wout.reshape(8, 128, E).transpose(1, 0, 2)
                                 .reshape(128, 8 * E))
    bout_c = np.ascontiguousarray(np.asarray(L0_attn_bout, np.float32)
                                  .reshape(8, 128).T)
    b2_c = np.ascontiguousarray(np.asarray(L0_ffn_b2, np.float32)
                                .reshape(8, 128).T)
    w1T = np.ascontiguousarray(np.asarray(L0_ffn_w1, np.float32).T)
    w2T = np.ascontiguousarray(np.asarray(L0_ffn_w2, np.float32).T)

    shared = {
        "wqa": np.ascontiguousarray(np.asarray(L0_attn_Wqa, np.float32)),
        "wqbp": wqbp,
        "wkva": np.ascontiguousarray(np.asarray(L0_attn_Wkva, np.float32)),
        "wkvbk": wkvbk, "wkvbkB": wkvbkB, "wqbpB": wqbpB,
        "woutr": woutr, "wkvbv": wkvbv,
        "bout_c": bout_c, "b2_c": b2_c,
        "b1row": np.asarray(L0_ffn_b1, np.float32).reshape(1, HID),
        "ln1g": np.asarray(L0_ffn_ln1_g, np.float32).reshape(1, E),
        "ln1b": np.asarray(L0_ffn_ln1_b, np.float32).reshape(1, E),
        "ln2g": np.asarray(L0_ffn_ln2_g, np.float32).reshape(1, HID),
        "ln2b": np.asarray(L0_ffn_ln2_b, np.float32).reshape(1, HID),
        "w1T": w1T, "w2T": w2T,
        "cs_full": T["cs"], "sc_full": T["sc"], "tri": T["tri"],
        "idb": T["idb"], "idf": T["idf"],
        "selx": T["selx"], "selxo": T["selxo"],
        "selr8": T["selr8"], "selr4": T["selr4"],
        "selrl": T["selrl"], "halfsel": T["halfsel"],
    }

    in_maps = []
    for core in range(8):
        b_ = core // 2
        js = JSETS[core % 2]
        cols = np.concatenate([np.arange(512 * j, 512 * j + 512) for j in js])
        kill = np.zeros((1, NL), np.float32)
        for s_, j in enumerate(js):
            kill[0, LOFF[s_] + 4 * j:LOFF[s_] + LPAD[s_]] = KILL
        m = dict(shared)
        m["xT"] = np.ascontiguousarray(x[b_].T)
        m["xoT"] = np.ascontiguousarray(x[b_][cols].T)
        m["cs_own"] = np.ascontiguousarray(T["cs"][:, cols])
        m["sc_own"] = np.ascontiguousarray(T["sc"][:, cols])
        m["killrow"] = kill
        in_maps.append(m)

    if "nc" not in _NC_CACHE:
        _NC_CACHE["nc"] = build_program()[0]
    nc = _NC_CACHE["nc"]
    res = run_bass_kernel_spmd(nc, in_maps, core_ids=list(range(8)))

    out = np.empty((B, S, E), np.float32)
    for core in range(8):
        yv = np.asarray(res.results[core]["yT"])
        b_ = core // 2
        js = JSETS[core % 2]
        for s_, j in enumerate(js):
            out[b_, 512 * j:512 * j + 512, :] = yv[:, 512 * s_:512 * (s_ + 1)].T
    return out


if __name__ == "__main__":
    nc, p = build_program()
    n_inst = sum(1 for _ in nc.all_instructions())
    print("built ok, instructions:", n_inst)
    print("op counts:", {e: p.cnt[e] for e in Prog.ENGS},
          {s: v // 16 for s, v in p.dcnt.items()})


# version 30
# speedup vs baseline: 1.0129x; 1.0052x over previous
import numpy as np
import ml_dtypes
import concourse.bass as bass
import concourse.mybir as mybir
from concourse.bass_utils import run_bass_kernel_spmd

F32 = mybir.dt.float32
BF16 = mybir.dt.bfloat16
I32 = mybir.dt.int32
FP8 = mybir.dt.float8e4
AF = mybir.ActivationFunctionType
ALU = mybir.AluOpType
AX = mybir.AxisListType
BFNP = ml_dtypes.bfloat16

B, S, E = 4, 4096, 1024
H, KVH, D = 16, 4, 64
QL, KVL = 64, 32
HID = 4096
TOK = 2048
NSLOT = 4
SLOT = 512
NKT = S // 128
EPS = 1e-5
KILL = -1.0e30
SCALE = 0.125

JSETS = [[7, 5, 2, 0], [6, 4, 3, 1]]
LPAD = [28, 20, 12, 4]
LOFF = [0, 28, 48, 60]
NL = 64
DPERM = np.concatenate([np.arange(0, 64, 2), np.arange(1, 64, 2)])


class Prog:
    ENGS = ("PE", "ACT", "DVE", "SP", "GP")

    def __init__(self):
        self.items = {e: [] for e in self.ENGS}
        self.cnt = {e: 0 for e in self.ENGS}
        self.dcnt = {}
        self.last_w = {}
        self.group = {}
        self.readers = {}
        self.observed = {e: {} for e in self.ENGS}
        self.fences = {}

    def _want(self, eng, waits, key, val):
        if self.observed[eng].get(key, 0) >= val:
            return
        if waits.get(key, 0) < val:
            waits[key] = val

    def fence(self, tag):
        deps = {}
        for e in self.ENGS:
            if self.cnt[e]:
                deps[("e", e)] = self.cnt[e]
        for s_, v_ in self.dcnt.items():
            deps[("d", s_)] = v_
        self.fences[tag] = deps

    def _emit(self, eng, fn, reads, writes, group, is_dma, fence, sem=None):
        waits = {}
        if fence is not None:
            for k_, v_ in self.fences[fence].items():
                self._want(eng, waits, k_, v_)
        for r in reads:
            for k_, v_ in self.last_w.get(r, {}).items():
                self._want(eng, waits, k_, v_)
        for w in writes:
            same_group = group is not None and self.group.get(w) == group
            if not same_group:
                for k_, v_ in self.last_w.get(w, {}).items():
                    self._want(eng, waits, k_, v_)
            for k_, v_ in self.readers.get(w, {}).items():
                self._want(eng, waits, k_, v_)
        if is_dma:
            prev = self.dcnt.get(sem, 0)
            if prev:
                self._want(eng, waits, ("d", sem), prev)
            self.dcnt[sem] = prev + 16
            me = (("d", sem), self.dcnt[sem])
        else:
            self.cnt[eng] += 1
            me = (("e", eng), self.cnt[eng])
        for r in reads:
            self.readers.setdefault(r, {})[me[0]] = me[1]
        for w in writes:
            same_group = group is not None and self.group.get(w) == group
            if same_group:
                self.last_w.setdefault(w, {})[me[0]] = me[1]
            else:
                self.last_w[w] = {me[0]: me[1]}
            self.readers[w] = {}
            self.group[w] = group
        obs = self.observed[eng]
        for k_, v_ in waits.items():
            if obs.get(k_, 0) < v_:
                obs[k_] = v_
        self.items[eng].append((sorted(waits.items()), fn,
                                sem if is_dma else None))

    def op(self, eng, fn, reads=(), writes=(), group=None, fence=None):
        import sys
        fn._line = sys._getframe(2).f_lineno
        self._emit(eng, fn, reads, writes, group, False, fence)

    def dma(self, eng, fn, reads=(), writes=(), fence=None, sem=None):
        self._emit(eng, fn, reads, writes, None, True, fence, sem=sem)


def build_program():
    nc = bass.Bass("TRN2", target_bir_lowering=False)
    p = Prog()

    def din(name, shape, dt=F32):
        return nc.dram_tensor(name, list(shape), dt, kind="ExternalInput").ap()

    xT_d = din("xT", (E, S))
    xoT_d = din("xoT", (E, TOK))
    wqa_d = din("wqa", (E, QL))
    wqbp_d = din("wqbp", (QL, E))
    wqbpB_d = din("wqbpB", (QL, E))
    wkva_d = din("wkva", (E, KVL))
    wkvbk_d = din("wkvbk", (KVL, 256))
    wkvbkB_d = din("wkvbkB", (KVL, 256))
    wkvbv_d = din("wkvbv", (KVL, 256))
    woutr_d = din("woutr", (128, 8 * E))
    bout_d = din("bout_c", (128, 8))
    b2_d = din("b2_c", (128, 8))
    b1row_d = din("b1row", (1, HID))
    ln1g_d = din("ln1g", (1, E))
    ln1b_d = din("ln1b", (1, E))
    ln2g_d = din("ln2g", (1, HID))
    ln2b_d = din("ln2b", (1, HID))
    w1T_d = din("w1T", (E, HID))
    w2T_d = din("w2T", (HID, E))
    csf_d = din("cs_full", (128, S), BF16)
    scf_d = din("sc_full", (128, S), BF16)
    cso_d = din("cs_own", (128, TOK), BF16)
    sco_d = din("sc_own", (128, TOK), BF16)
    tri_d = din("tri", (128, 128), BF16)
    kill_d = din("killrow", (1, NL))
    idb_d = din("idb", (128, 128), BF16)
    idf_d = din("idf", (128, 128), F32)
    selx_d = din("selx", (128, 64), BF16)
    selxo_d = din("selxo", (128, 16), BF16)
    selr8_d = din("selr8", (8, 256))
    selr4_d = din("selr4", (4, 256))
    selrl_d = din("selrl", (16, 1024))
    halfsel_d = din("halfsel", (128, 2))

    yT_d = nc.dram_tensor("yT", [E, TOK], F32, kind="ExternalOutput").ap()
    tern1_dr = nc.dram_tensor("tern1_scratch", [8 * 128, HID], FP8).ap()
    tern1b = tern1_dr.rearrange("(hb p) x -> hb p x", p=128)
    t1dr = tuple(f"t1d{i}" for i in range(8))

    ctxs = {}

    def sbuf(name, shape, dt):
        c = nc.sbuf_tensor(name, list(shape), dt)
        h = c.__enter__()
        ctxs[name] = c
        return h

    def sfree(*names):
        for n in names:
            ctxs.pop(n).__exit__(None, None, None)

    w2t = sbuf("w2t", (128, 32 * E), FP8)
    ksb = [sbuf(f"Pksb{g}", (128, S), BF16) for g in range(2)]
    vsb = sbuf("Pvsb", (128, NKT * 260), BF16)
    qr = sbuf("Pqr", (128, 8 * SLOT), BF16)
    qlln = sbuf("Pqlln", (64, TOK), BF16)
    ylnown = sbuf("Pylnown", (32, TOK), BF16)
    g1t = sbuf("Pg1t", (128, E), BF16)
    b1t = sbuf("Pb1t", (128, E), BF16)
    g2t = sbuf("Pg2t", (128, HID), BF16)
    b2t = sbuf("Pb2t", (128, HID), BF16)
    k3rhs = sbuf("Pk3rhs", (3, HID), BF16)
    k3lhs = sbuf("Pk3lhs", (3, 256), BF16)
    c2pair = sbuf("Pc2pair", (2, E), BF16)
    n128 = sbuf("Pn128", (2, 512), BF16)
    tri = sbuf("Ptri", (128, 128), BF16)
    idb = sbuf("Pidb", (128, 128), BF16)
    idf = sbuf("Pidf", (128, 128), F32)
    selrl = sbuf("Pselrl", (16, 1024), BF16)
    halfsel = sbuf("Phalfsel", (128, 2), F32)
    bout_s = sbuf("Pbout", (128, 8), F32)
    b2_s = sbuf("Pb2", (128, 8), F32)
    biastab = sbuf("Pbiastab", (128, NL), F32)
    negb = sbuf("Pnegb", (128, 1), F32)
    killrow = sbuf("Pkill", (1, NL), F32)
    ones_c = sbuf("Ponesc", (128, 1), F32)
    ones_cb = sbuf("Ponescb", (128, 1), BF16)
    ones_r = sbuf("Ponesr", (1, 128), F32)
    epscol = sbuf("Pepscol", (128, 1), F32)
    wsc = sbuf("Pwsc", (128, 8), F32)
    small = sbuf("Psmall", (128, 64), F32)
    small2 = sbuf("Psmall2", (2, 2), F32)
    rowtmp = sbuf("Prowtmp", (1, 8), F32)
    bnb = sbuf("Pbnb", (128, 104), F32)
    wqc = sbuf("Pwqc", (128, 8 * QL), BF16)
    wkvc = sbuf("Pwkvc", (128, 8 * KVL), BF16)
    wqb = sbuf("Pwqb", (64, E), BF16)
    wqbB = sbuf("PwqbB", (64, E), BF16)
    wkvbk = sbuf("Pwkvbk", (32, 256), BF16)
    wkvbkB = sbuf("PwkvbkB", (32, 256), BF16)
    wkvbv = sbuf("Pwkvbv", (32, 256), BF16)
    pt = [sbuf(f"Ppt{i}", (128, 1024), BF16) for i in range(2)]
    w1slab = [sbuf(f"Pw1s{i}", (128, 8 * 512), FP8) for i in range(2)]
    rch = pt[0][:, 0:512]
    rt1 = pt[0][:, 512:1024]
    rt2 = pt[1][:, 0:512]

    ysb = sbuf("1ysb", (32, S), BF16)
    qlp = sbuf("1qlp", (64, TOK), BF16)
    yor = sbuf("1yor", (32, TOK), BF16)
    stat = sbuf("1stat", (8, 1024), F32)
    stato = sbuf("1stato", (4, 1024), F32)
    xbf = [sbuf(f"1xbf{i}", (128, 8 * 512), BF16) for i in range(2)]
    x2b = sbuf("1x2b", (128, 8 * 512), BF16)
    csfb = sbuf("1csfb", (128, 512), BF16)
    scfb = sbuf("1scfb", (128, 512), BF16)


    accw = sbuf("0accw", (128, 80), F32)
    wqaf = sbuf("0wqaf", (128, 8 * QL), BF16)
    wkvaf = sbuf("0wkvaf", (128, 8 * KVL), BF16)
    selx = sbuf("0selx", (128, 64), BF16)
    selxo = sbuf("0selxo", (128, 16), BF16)
    selr8 = sbuf("0selr8", (8, 256), F32)
    selr4 = sbuf("0selr4", (4, 256), F32)

    psSl = nc.psum_tensor("psSl", [128, 1024], F32).__enter__()
    psSu = nc.psum_tensor("psSu", [128, 1024], F32).__enter__()
    psO = nc.psum_tensor("psO", [128, 1024], F32).__enter__()
    psG = [nc.psum_tensor(f"psG{i}", [128, 512], F32).__enter__() for i in range(2)]
    psbfv = psSl[:].bitcast(BF16)
    SL0, SL1, SU0, SU1 = "psSl0", "psSl1", "psSu0", "psSu1"
    OL_, OU_, G0, G1 = "psO_l", "psO_u", "psG0", "psG1"
    PSC1 = [(psSl[:, 0:512], SL0), (psSl[:, 512:1024], SL1),
            (psSu[:, 0:512], SU0), (psSu[:, 512:1024], SU1),
            (psO[:, 0:512], OL_), (psO[:, 512:1024], OU_),
            (psG[0][:, 0:512], G0), (psG[1][:, 0:512], G1)]

    sems = {}
    for e in Prog.ENGS:
        sems[("e", e)] = nc.semaphore(f"sem_{e}").__enter__()

    def dsem(name):
        key = ("d", name)
        if key not in sems:
            sems[key] = nc.semaphore(f"ds_{name}").__enter__()
        return sems[key]

    def mm(out_ap, lhsT, rhs, reads, writes, group=None, start=True, stop=True,
           tile_position=None):
        p.op("PE", lambda eng: eng.matmul(out_ap, lhsT, rhs, start=start,
                                          stop=stop, tile_position=tile_position),
             reads=reads, writes=writes, group=group)

    def tr(out_ap, in_ap, ident, reads, writes, group=None):
        p.op("PE", lambda eng: eng.transpose(out_ap, in_ap, ident),
             reads=reads, writes=writes, group=group)

    def act(out_ap, in_ap, func, reads, writes, bias=None, scale=None,
            accum_out=None, fence=None):
        kw = {}
        if bias is not None:
            kw["bias"] = bias
        if scale is not None:
            kw["scale"] = scale
        if accum_out is not None:
            kw["accum_out"] = accum_out
        p.op("ACT", lambda eng: eng.activation(out_ap, in_ap, func, **kw),
             reads=reads, writes=writes, fence=fence)

    def v(meth, *args, reads=(), writes=(), fence=None, **kw):
        p.op("DVE", lambda eng: getattr(eng, meth)(*args, **kw),
             reads=reads, writes=writes, fence=fence)

    def g(meth, *args, reads=(), writes=(), fence=None, **kw):
        p.op("GP", lambda eng: getattr(eng, meth)(*args, **kw),
             reads=reads, writes=writes, fence=fence)

    def dma(out_ap, in_ap, reads, writes, q="SP", fence=None, sem="ld0"):
        p.dma(q, lambda eng: eng.dma_start(out=out_ap, in_=in_ap),
              reads=reads, writes=writes, fence=fence, sem=sem)

    ones_row = ones_r[:]

    for dst, src, tag in [
        (tri[:], tri_d[:], "tri"), (idb[:], idb_d[:], "idb"),
        (idf[:], idf_d[:], "idf"), (halfsel[:], halfsel_d[:], "halfsel"),
        (bout_s[:], bout_d[:], "bout"), (b2_s[:], b2_d[:], "b2"),
        (killrow[:], kill_d[:], "kill"),
        (selr8[:], selr8_d[:], "selr8"), (selr4[:], selr4_d[:], "selr4"),
    ]:
        dma(dst, src, reads=(), writes=(tag,))
    dma(wqaf[:].rearrange("p (c r) -> p c r", r=QL),
        wqa_d.rearrange("(c p) r -> p c r", p=128), reads=(), writes=("wqaf",),
        q="GP", sem="ldg")
    dma(wkvaf[:].rearrange("p (c r) -> p c r", r=KVL),
        wkva_d.rearrange("(c p) r -> p c r", p=128), reads=(), writes=("wkvaf",),
        q="GP", sem="ldg")
    for dst, src, tag in [
        (wqb[:], wqbp_d[:], "wqb"), (wkvbk[:], wkvbk_d[:], "wkvbk"),
        (wqbB[:], wqbpB_d[:], "wqbB"), (wkvbkB[:], wkvbkB_d[:], "wkvbkB"),
        (wkvbv[:], wkvbv_d[:], "wkvbv"), (selx[:], selx_d[:], "selx"),
        (selxo[:], selxo_d[:], "selxo"), (selrl[:], selrl_d[:], "selrl"),
    ]:
        dma(dst, src, reads=(), writes=(tag,), q="GP", sem="ldg")
    dma(k3rhs[0:1, :], b1row_d[:], reads=(), writes=("k3r0",), q="GP", sem="ldg")
    for dst, src_d, dtag in ((g1t, ln1g_d, "g1t"), (b1t, ln1b_d, "b1t"),
                             (g2t, ln2g_d, "g2t"), (b2t, ln2b_d, "b2t")):
        dma(dst[:], src_d.partition_broadcast(128).rearrange("p o n -> p (o n)"),
            reads=(), writes=(dtag,), q="GP", sem="ldg")

    v("memset", ones_c[:], 1.0, writes=("onesc",))
    v("memset", ones_r[:], 1.0, writes=("onesc",))
    v("memset", epscol[:], float(EPS), writes=("epsc",))
    v("memset", ones_cb[:], 1.0, writes=("onescb",))
    v("memset", bnb[:, 102:103], 128.0, writes=("c128",))
    v("memset", n128[:], -128.0, writes=("n128",))
    v("memset", k3lhs[0:3, :], -128.0, writes=("k3l12",))
    v("memset", vsb[:].rearrange("p (t x) -> p t x", x=65)[:, :, 64:65], 1.0,
      writes=("vones",))

    for tag, wtag, wf, rank, dst, bank, btag in (
            ("qa", "wqaf", wqaf, QL, wqc, psG[0], G0),
            ("kv", "wkvaf", wkvaf, KVL, wkvc, psG[1], G1)):
        for c in range(8):
            mm(bank[0:1, 0:rank], ones_cb[:], wf[:, c * rank:(c + 1) * rank],
               reads=("onescb", wtag), writes=(btag,),
               group=f"gcol{tag}", start=(c == 0), stop=(c == 7))
        v("tensor_scalar", small[0:1, 0:rank], bank[0:1, 0:rank], 1.0 / float(E),
          None, op0=ALU.mult, reads=(btag,), writes=(f"cs{tag}",))
        mm(bank[:, 0:rank], ones_row, small[0:1, 0:rank],
           reads=("onesc", f"cs{tag}"), writes=(btag,))
        for c in range(8):
            v("tensor_tensor", dst[:, c * rank:(c + 1) * rank],
              wf[:, c * rank:(c + 1) * rank], bank[:, 0:rank], op=ALU.subtract,
              reads=(wtag, btag), writes=(f"wadj{tag}",))

    wab = [sbuf(f"0wab{i}", (128, HID // 2), F32) for i in range(2)]

    def wabs_gen(src, nchunk, width, acc_off):
        hw_ = width // 2
        for i in range(nchunk):
            for hf in range(2):
                j = 2 * i + hf
                buf = wab[j % 2]
                qw = hw_ // 2
                for ha in range(2):
                    dma(buf[:, ha * qw:(ha + 1) * qw],
                        src.rearrange("(c p) h -> p c h", p=128)
                        [:, i, hf * hw_ + ha * qw:hf * hw_ + (ha + 1) * qw],
                        reads=(), writes=(f"wab{j % 2}{'ab'[ha]}",),
                        sem=f"wab{j % 2}{'ab'[ha]}")
                act(buf[:, 0:hw_], buf[:, 0:hw_], AF.Abs,
                    reads=(f"wab{j % 2}a", f"wab{j % 2}b"),
                    writes=(f"wab{j % 2}a", f"wab{j % 2}b",
                            f"accw{2 * acc_off + j}"),
                    accum_out=accw[:, 2 * acc_off + j:2 * acc_off + j + 1])
                yield

    def _wab_chain():
        yield from wabs_gen(w1T_d, 8, HID, 0)
        yield from wabs_gen(w2T_d, 32, E, 8)
    wab_iter = _wab_chain()

    def wab_drain(k):
        for _ in range(k):
            try:
                next(wab_iter)
            except StopIteration:
                return

    def wsc_block():
        accreads = tuple(f"accw{i}" for i in range(80))
        mm(psG[0][0:1, 0:80], ones_c[:], accw[:, 0:80],
           reads=accreads + ("onesc",), writes=(G0,))
        v("reduce_sum", rowtmp[0:1, 0:1], psG[0][0:1, 0:16], axis=AX.X,
          reads=(G0,), writes=("s1sum",))
        v("reduce_sum", rowtmp[0:1, 1:2], psG[0][0:1, 16:80], axis=AX.X,
          reads=(G0,), writes=("s2sum",))
        v("tensor_scalar", rowtmp[0:1, 2:4], rowtmp[0:1, 0:2],
          1.0 / float(E * HID), 1e-5, op0=ALU.mult, op1=ALU.max,
          reads=("s1sum", "s2sum"), writes=("wmean",))
        v("reciprocal", rowtmp[0:1, 4:6], rowtmp[0:1, 2:4], reads=("wmean",),
          writes=("winv",))
        mm(psG[0][:, 0:4], ones_row, rowtmp[0:1, 2:6],
           reads=("wmean", "winv", "onesc"), writes=(G0,))
        v("tensor_copy", wsc[:, 0:4], psG[0][:, 0:4], reads=(G0,),
          writes=("wsc",))


    def weight_prep_steps():
        def ternarize(srcd, nchunk, width, scol, sink):
            hw_ = width // 2
            for i in range(nchunk):
                for hf in range(2):
                    j = 2 * i + hf
                    buf = wst[j % 2]
                    qw2 = hw_ // 2
                    for ha in range(2):
                        dma(buf[:, ha * qw2:(ha + 1) * qw2],
                            srcd.rearrange("(c p) h -> p c h", p=128)
                            [:, i, hf * hw_ + ha * qw2:hf * hw_ + (ha + 1) * qw2],
                            reads=(), writes=(f"wst{j % 2}{'ab'[ha]}",),
                            sem=f"wst{j % 2}{'ab'[ha]}")
                    for qtr in range(0, hw_, 256):
                        w_ = min(256, hw_ - qtr)
                        htag = f"wst{j % 2}{'ab'[qtr * 2 // hw_]}"
                        v("tensor_scalar", tzq[0][:, 0:w_], buf[:, qtr:qtr + w_],
                          wsc[:, scol + 2:scol + 3], 0.5, op0=ALU.mult,
                          op1=ALU.is_ge, reads=(htag, "wsc"),
                          writes=("tzq0",))
                        v("tensor_scalar", tzq[1][:, 0:w_], buf[:, qtr:qtr + w_],
                          wsc[:, scol + 2:scol + 3], -0.5, op0=ALU.mult,
                          op1=ALU.is_le, reads=(htag, "wsc"),
                          writes=("tzq1",))
                        sink(i, hf * hw_ + qtr, w_)
                        yield

        def w1_sink(i, qtr, w_):
            qi = t1qc[0] % 2
            t1qc[0] += 1
            v("tensor_tensor", t1cq[qi][:, 0:w_], tzq[0][:, 0:w_], tzq[1][:, 0:w_],
              op=ALU.subtract, reads=("tzq0", "tzq1"), writes=(f"t1cq{qi}",))
            hb_ = qtr // 512
            j0 = i * 512 + (qtr % 512)
            dma(tern1b[hb_, :, j0:j0 + w_],
                t1cq[qi][:, 0:w_], reads=(f"t1cq{qi}",), writes=(f"t1d{hb_}",),
                sem=f"ternw{qi}")

        def w2_sink(i, qtr, w_):
            v("tensor_tensor", w2t[:, i * E + qtr:i * E + qtr + w_],
              tzq[0][:, 0:w_], tzq[1][:, 0:w_], op=ALU.subtract,
              reads=("tzq0", "tzq1"), writes=(f"t2_{i}",))

        yield from ternarize(w1T_d, 8, HID, 0, w1_sink)
        yield from ternarize(w2T_d, 32, E, 1, w2_sink)

        for hb in range(8):
            slab = w1slab[hb % 2]
            dma(slab[:, 0:HID], tern1b[hb],
                reads=(f"t1d{hb}",), writes=(f"w1s{hb % 2}",), sem=f"c1s{hb % 2}")
            reg, rtag = PSC1[hb]
            for fc in range(8):
                mm(reg[0:1, 0:512], ones_cb[:],
                   slab[:, fc * 512:(fc + 1) * 512],
                   reads=(f"w1s{hb % 2}", "onescb"), writes=(rtag,),
                   group=f"gc1{hb}", start=(fc == 0), stop=(fc == 7))
        for hb in range(8):
            reg, rtag = PSC1[hb]
            for hq in range(2):
                sg = slice(hq * 256, (hq + 1) * 256)
                v("tensor_copy", tzq[0][0:1, 0:256], reg[0:1, sg],
                  reads=(rtag,), writes=("s1hiseg", "tzq0"))
                v("tensor_tensor", tzq[1][0:1, 0:256], reg[0:1, sg],
                  tzq[0][0:1, 0:256], op=ALU.subtract,
                  reads=(rtag, "s1hiseg"), writes=("s1loseg", "tzq1"))
                dma(k3rhs[1:2, hb * 512 + hq * 256:hb * 512 + (hq + 1) * 256],
                    tzq[0][0:1, 0:256], reads=("s1hiseg", "tzq0"),
                    writes=("k3r1",), q="GP", sem="ldg2")
                dma(k3rhs[2:3, hb * 512 + hq * 256:hb * 512 + (hq + 1) * 256],
                    tzq[1][0:1, 0:256], reads=("s1loseg", "tzq1"),
                    writes=("k3r2",), q="GP", sem="ldg2")

        for ec in range(8):
            for hc in range(32):
                mm(psG[1][0:1, 0:128], ones_cb[:],
                   w2t[:, hc * E + ec * 128:hc * E + (ec + 1) * 128],
                   reads=(f"t2_{hc}", "onescb"), writes=(G1,),
                   group=f"gc2{ec}", start=(hc == 0), stop=(hc == 31))
            v("tensor_copy", c2pair[0:1, ec * 128:(ec + 1) * 128],
              psG[1][0:1, 0:128], reads=(G1,), writes=("c2hi",))
            v("tensor_tensor", tzq[1][0:1, 0:128], psG[1][0:1, 0:128],
              c2pair[0:1, ec * 128:(ec + 1) * 128], op=ALU.subtract,
              reads=(G1, "c2hi"), writes=("c2loseg", "tzq1"))
            dma(c2pair[1:2, ec * 128:(ec + 1) * 128], tzq[1][0:1, 0:128],
                reads=("c2loseg", "tzq1"), writes=("c2lo",), q="GP", sem="ldg2")

    def xpass(src_ap, nblk, wadj, wmat, rank, ysink, selb, selbt, statps,
              statr0, statr1, sel_w, okind):
        for b_ in range(nblk):
            buf = xbf[b_ % 2]
            tag = f"xb{b_ % 2}"
            al = "tza" if b_ % 2 == 0 else "tzb"
            dma(buf[:], src_ap[:, :, b_ * 512:(b_ + 1) * 512],
                reads=(), writes=(tag, al), q="GP", sem=f"xbf{b_ % 2}")
            act(x2b[:], buf[:], AF.Square, reads=(tag,), writes=("x2b", "t1c"))
            for c in range(8):
                mm(psG[0][0:rank, 0:512], wmat[:, c * rank:(c + 1) * rank],
                   buf[:, c * 512:(c + 1) * 512],
                   reads=(tag, wadj), writes=(G0,), group=f"gy{okind}{b_}",
                   start=(c == 0), stop=(c == 7))
            ysink(b_, buf, tag)
            wab_drain(7)
            for c in range(8):
                mm(statps[0:sel_w, 0:512],
                   selb[:, b_ * sel_w:(b_ + 1) * sel_w],
                   x2b[:, c * 512:(c + 1) * 512], reads=("x2b", selbt),
                   writes=(statr0,), group="gstatA" + okind,
                   start=(b_ == 0 and c == 0),
                   stop=(b_ == nblk - 1 and c == 7))
                mm(statps[0:sel_w, 512:1024],
                   selb[:, b_ * sel_w:(b_ + 1) * sel_w],
                   buf[:, c * 512:(c + 1) * 512], reads=(tag, selbt),
                   writes=(statr1,), group="gstatB" + okind,
                   start=(b_ == 0 and c == 0),
                   stop=(b_ == nblk - 1 and c == 7))

    def ysink_full(b_, buf, tag):
        v("tensor_copy", ysb[0:32, b_ * 512:(b_ + 1) * 512], psG[0][0:32, 0:512],
          reads=(G0,), writes=(f"ysb{b_}",))

    xpass(xT_d.rearrange("(c p) t -> p c t", p=128), 8, "wadjkv", wkvc, 32,
          ysink_full, selx, "selx", psO, OL_, OU_, 8, "f")

    def ysink_own(b_, buf, tag):
        v("tensor_copy", qlp[0:64, b_ * 512:(b_ + 1) * 512], psG[0][0:64, 0:512],
          reads=(G0,), writes=(f"qlp{b_}",))
        for c in range(8):
            mm(psG[1][0:32, 0:512], wkvc[:, c * 32:(c + 1) * 32],
               buf[:, c * 512:(c + 1) * 512],
               reads=(tag, "wadjkv"), writes=(G1,), group=f"gyo{b_}",
               start=(c == 0), stop=(c == 7))
        v("tensor_copy", yor[0:32, b_ * 512:(b_ + 1) * 512], psG[1][0:32, 0:512],
          reads=(G1,), writes=(f"yor{b_}",))

    xpass(xoT_d.rearrange("(c p) t -> p c t", p=128), 4, "wadjqa", wqc, 64,
          ysink_own, selxo, "selxo", psSl, SL0, SL1, 4, "o")

    def finstats(pst, ptags, n, dst, dtag):
        v("tensor_scalar", dst[0:n, :], pst[0:n, 0:1024], 1.0 / float(E),
          None, op0=ALU.mult, reads=ptags, writes=(dtag + "m",))
        v("tensor_tensor", dst[0:n, 512:1024], dst[0:n, 512:1024],
          dst[0:n, 512:1024], op=ALU.mult, reads=(dtag + "m",),
          writes=(dtag + "m",))
        v("tensor_tensor", dst[0:n, 0:512], dst[0:n, 0:512],
          dst[0:n, 512:1024], op=ALU.subtract, reads=(dtag + "m",),
          writes=(dtag + "var",))
        act(dst[0:n, 0:512], dst[0:n, 0:512], AF.Ln, bias=epscol[0:n, 0:1],
            scale=1.0, reads=(dtag + "var", "epsc"), writes=(dtag + "ln",))
        act(dst[0:n, 0:512], dst[0:n, 0:512], AF.Exp, scale=-0.5,
            reads=(dtag + "ln",), writes=(dtag + "r",))

    finstats(psO, (OL_, OU_), 8, stat, "sf")
    finstats(psSl, (SL0, SL1), 4, stato, "so")

    for b_ in range(8):
        mm(psG[0][0:32, 0:512], selr8[:, b_ * 32:(b_ + 1) * 32], stat[0:8, 0:512],
           reads=("sfr", "selr8"), writes=(G0,))
        v("tensor_tensor", ysb[0:32, b_ * 512:(b_ + 1) * 512],
          ysb[0:32, b_ * 512:(b_ + 1) * 512], psG[0][0:32, 0:512], op=ALU.mult,
          reads=(f"ysb{b_}", G0), writes=(f"yln{b_}",))
    for b_ in range(4):
        mm(psG[0][0:64, 0:512], selr4[:, b_ * 64:(b_ + 1) * 64], stato[0:4, 0:512],
           reads=("sor", "selr4"), writes=(G0,))
        v("tensor_tensor", qlln[0:64, b_ * 512:(b_ + 1) * 512],
          qlp[0:64, b_ * 512:(b_ + 1) * 512], psG[0][0:64, 0:512], op=ALU.mult,
          reads=(f"qlp{b_}", G0), writes=(f"qlln{b_}",))
        mm(psG[1][0:32, 0:512], selr4[0:4, b_ * 64:b_ * 64 + 32], stato[0:4, 0:512],
           reads=("sor", "selr4"), writes=(G1,))
        v("tensor_tensor", ylnown[0:32, b_ * 512:(b_ + 1) * 512],
          yor[0:32, b_ * 512:(b_ + 1) * 512], psG[1][0:32, 0:512], op=ALU.mult,
          reads=(f"yor{b_}", G1), writes=(f"ylnown{b_}",))

    def rope(dst, dtag, dcol, psA, ptagA, psB, ptagB, t1_ap, t1tag, t2_ap,
             t2tag, width):
        v("tensor_tensor", rt1[:, 0:width], psA, t1_ap, op=ALU.mult,
          reads=(ptagA, t1tag), writes=("rt1", "pt0a"))
        v("tensor_tensor", rt2[:, 0:width], psB, t2_ap, op=ALU.mult,
          reads=(ptagB, t2tag), writes=("rt2", "pt1a"))
        v("tensor_tensor", dst[:, dcol:dcol + width], rt1[:, 0:width],
          rt2[:, 0:width], op=ALU.add, reads=("rt1", "rt2"), writes=(dtag,))

    for b_ in range(8):
        dma(csfb[:], csf_d[:, b_ * 512:(b_ + 1) * 512], reads=(),
            writes=("csfb",), sem="csfb")
        dma(scfb[:], scf_d[:, b_ * 512:(b_ + 1) * 512], reads=(),
            writes=("scfb",), sem="scfb")
        for g_ in range(2):
            mm(psG[0][:, 0:512], wkvbk[:, g_ * 128:(g_ + 1) * 128],
               ysb[0:32, b_ * 512:(b_ + 1) * 512],
               reads=(f"yln{b_}", "wkvbk"), writes=(G0,))
            mm(psG[1][:, 0:512], wkvbkB[:, g_ * 128:(g_ + 1) * 128],
               ysb[0:32, b_ * 512:(b_ + 1) * 512],
               reads=(f"yln{b_}", "wkvbkB"), writes=(G1,))
            rope(ksb[g_], f"ksbt{g_}", b_ * 512, psG[0][:, 0:512], G0,
                 psG[1][:, 0:512], G1, csfb[:], "csfb", scfb[:], "scfb", 512)
        for tt in range(4):
            kt = b_ * 4 + tt
            mm(psSu[:, 0:256], ysb[0:32, kt * 128:(kt + 1) * 128], wkvbv[:],
               reads=(f"yln{b_}", "wkvbv"), writes=(SU0,))
            v("tensor_copy",
              vsb[:].rearrange("p (t g x) -> p t g x", g=4, x=65)[:, kt, :, 0:64],
              psSu[:, 0:256].rearrange("p (g x) -> p g x", x=64),
              reads=(SU0,), writes=("vsbt",))

    v("reduce_max", small[:, 1:2], ksb[0][:], axis=AX.X, apply_absolute_value=True,
      reads=("ksbt0",), writes=("mk0",))
    wab_drain(10 ** 6)
    wsc_block()
    v("reduce_max", small[:, 2:3], ksb[1][:], axis=AX.X, apply_absolute_value=True,
      reads=("ksbt1",), writes=("mk1",))
    v("tensor_tensor", small[:, 1:2], small[:, 1:2], small[:, 2:3], op=ALU.max,
      reads=("mk0", "mk1"), writes=("mk",))

    wp_iter = weight_prep_steps()
    wp_count = [0]

    def wp_drain(k, cap=None):
        for _ in range(k):
            if cap is not None and wp_count[0] >= cap:
                return
            try:
                next(wp_iter)
                wp_count[0] += 1
            except StopIteration:
                return

    tail_pend = [None]

    def tail_drain(k):
        g_t = tail_pend[0]
        if g_t is None:
            return
        for _ in range(k):
            try:
                next(g_t)
            except StopIteration:
                tail_pend[0] = None
                return

    p.fence("p1done")
    sfree("0wab1", "0wab0", "0selr4", "0selr8", "0selxo", "0selx",
          "0wkvaf", "0wqaf", "0accw")
    sfree("1scfb", "1csfb", "1x2b", "1xbf1", "1xbf0", "1stato", "1stat",
          "1yor", "1qlp", "1ysb")
    FN = "p1done"

    x1 = sbuf("2x1", (128, 8 * SLOT), BF16)
    hsb = sbuf("2hsb", (128, 2 * HID), BF16)
    u1b = sbuf("2u1b", (128, E), BF16)
    x1tm = u1b
    sca = sbuf("2sca", (128, 512), F32)
    sda = sbuf("2sda", (128, 512), I32)
    sba = sbuf("2sba", (128, 512), BF16)
    saa = sbuf("2saa", (128, 512), BF16)
    dq1t = sbuf("2dq1t", (128, 2 * E), BF16)
    pts = [pt[0][:], pt[1][:],
           sca[:].bitcast(BF16), sda[:].bitcast(BF16)]
    PTTAG = [("pt0",), ("pt1",), ("pt2", "sca"), ("pt3", "sda")]
    _hf32 = hsb[:].bitcast(F32)
    wst = [_hf32[:, 0:HID // 2], _hf32[:, HID // 2:HID]]
    tzq = [dq1t[:, 0:256], dq1t[:, 256:512]]
    _t1f8 = dq1t[:, 512:768].bitcast(FP8)
    t1cq = [_t1f8[:, 0:256], _t1f8[:, 256:512]]
    t1qc = [0]
    dq2t = sbuf("2dq2t", (128, 32 * 256), BF16)
    otn = dq2t[:, 0:4096]
    olc = dq2t[:, 4096:8192]
    kos = [sbuf(f"2kos{g_}", (128, SLOT), BF16) for g_ in range(2)]
    vos = sbuf("2vos", (128, 4 * 260), BF16)
    cso = sbuf("2cso", (128, SLOT), BF16)
    sco = sbuf("2sco", (128, SLOT), BF16)
    woc = [sbuf(f"2woc{i}", (128, 512), BF16) for i in range(2)]
    xoc = [sbuf(f"2xoc{i}", (128, SLOT), F32) for i in range(2)]
    outst = [sbuf(f"2outst{i}", (128, 256), F32) for i in range(2)]
    scl2t = sbuf("2scl2t", (128, 256), BF16)
    rows2 = sbuf("2rows2", (1, 256), F32)
    lg = sbuf("2lg", (16, 512), BF16)
    linv = sbuf("2linv", (16, 512), BF16)
    olhi0 = sbuf("2olhi", (65, SLOT), BF16)
    olhi = [olhi0, sca[:].bitcast(BF16)[0:65, 0:512]]

    ARN = "dq2t_arena"

    for s_ in range(NSLOT):
        sc0 = s_ * SLOT
        fn = FN if s_ == 0 else None

        dma(cso[:], cso_d[:, sc0:sc0 + SLOT], reads=(), writes=("cso",), fence=fn, sem="cso")
        dma(sco[:], sco_d[:, sc0:sc0 + SLOT], reads=(), writes=("sco",), fence=fn, sem="sco")

        for c in range(8):
            mm(psSl[:, 0:512], wqb[:, c * 128:(c + 1) * 128],
               qlln[:, sc0:sc0 + SLOT], reads=(f"qlln{s_}", "wqb"), writes=(SL0,))
            mm(psSl[:, 512:1024], wqbB[:, c * 128:(c + 1) * 128],
               qlln[:, sc0:sc0 + SLOT], reads=(f"qlln{s_}", "wqbB"), writes=(SL1,))
            rope(qr, "qr", c * 512, psSl[:, 0:512], SL0, psSl[:, 512:1024], SL1,
                 cso[:], "cso", sco[:], "sco", 512)

        for g_ in range(2):
            mm(psSu[:, 0:512], wkvbk[:, g_ * 128:(g_ + 1) * 128],
               ylnown[0:32, sc0:sc0 + SLOT],
               reads=(f"ylnown{s_}", "wkvbk"), writes=(SU0,))
            mm(psSu[:, 512:1024], wkvbkB[:, g_ * 128:(g_ + 1) * 128],
               ylnown[0:32, sc0:sc0 + SLOT],
               reads=(f"ylnown{s_}", "wkvbkB"), writes=(SU1,))
            rope(kos[g_], f"kos{g_}", 0, psSu[:, 0:512], SU0,
                 psSu[:, 512:1024], SU1, cso[:], "cso", sco[:], "sco", 512)
        v("memset", vos[:].rearrange("p (t x) -> p t x", x=65)[:, :, 64:65], 1.0,
          writes=("vos",), fence=fn)
        for tt in range(4):
            mm(psO[:, 0:256], ylnown[0:32, sc0 + tt * 128:sc0 + (tt + 1) * 128],
               wkvbv[:], reads=(f"ylnown{s_}", "wkvbv"), writes=(OL_,))
            v("tensor_copy",
              vos[:].rearrange("p (t g x) -> p t g x", g=4, x=65)[:, tt, :, 0:64],
              psO[:, 0:256].rearrange("p (g x) -> p g x", x=64),
              reads=(OL_,), writes=("vos",))

        v("reduce_max", small[:, 0:1], qr[:], axis=AX.X, apply_absolute_value=True,
          reads=("qr",), writes=("mq",))
        v("tensor_tensor", small[:, 3:4], small[:, 0:1], small[:, 1:2],
          op=ALU.mult, reads=("mq", "mk"), writes=("mqk",))
        pb = psO[:, 512:1024]
        mm(pb[0:2, 0:1], halfsel[:], small[:, 3:4],
           reads=("mqk", "halfsel"), writes=(OU_,))
        v("tensor_copy", small2[0:2, 0:1], pb[0:2, 0:1], reads=(OU_,),
          writes=("bb2",))
        tr(pb[0:1, 0:2], small2[0:2, 0:1], idf[0:2, 0:2], reads=("bb2", "idf"),
           writes=(OU_,))
        v("reduce_max", rowtmp[0:1, 6:7], pb[0:1, 0:2], axis=AX.X,
          reads=(OU_,), writes=("bmax",))
        v("tensor_scalar", rowtmp[0:1, 7:8], rowtmp[0:1, 6:7], -SCALE, None,
          op0=ALU.mult, reads=("bmax",), writes=("negbrow",))
        mm(pb[:, 0:1], ones_row, rowtmp[0:1, 7:8],
           reads=("negbrow", "onesc"), writes=(OU_,))
        v("tensor_copy", negb[:], pb[:, 0:1], reads=(OU_,), writes=("negb",))
        mm(pb[:, 0:NL], ones_row, killrow[:], reads=("kill", "onesc"),
           writes=(OU_,))
        v("tensor_scalar", biastab[:], pb[:, 0:NL], negb[:], None,
          op0=ALU.add, reads=(OU_, "negb"), writes=("biastab",))

        L = LPAD[s_]
        SREG = [(psSl, (SL0,), (SL1,), True),
                (psSu, (SU0,), (SU1,), True),
                (None, (G0,), (G1,), False)]
        ucnt = [0]

        def s_unit(lhs0, lhs1, rhs_ap, rtags, bias_ap, btag, wexp, ptwr):
            u = ucnt[0] % 3
            pu = ucnt[0] % 4
            ucnt[0] += 1
            reg, tg0, tg1, contig = SREG[u]
            half0 = reg[:, 0:512] if contig else psG[0][:, 0:512]
            half1 = reg[:, 512:1024] if contig else psG[1][:, 0:512]
            mm(half0[:, 0:wexp], lhs0, rhs_ap, reads=rtags, writes=tg0,
               tile_position=(0, 0) if lhs0.shape[0] == 64 and True else None)
            ptbuf = pts[pu]
            if lhs1 is not None:
                mm(half1[:, 0:wexp], lhs1, rhs_ap, reads=rtags, writes=tg1,
                   tile_position=None)
            if contig and lhs1 is not None:
                act(ptbuf[:, 0:1024], reg[:], AF.Exp, bias=bias_ap, scale=SCALE,
                    reads=tg0 + tg1 + (btag,), writes=PTTAG[pu])
            else:
                act(ptbuf[:, 0:wexp], half0[:, 0:wexp], AF.Exp, bias=bias_ap,
                    scale=SCALE, reads=tg0 + (btag,), writes=PTTAG[pu])
                if lhs1 is not None:
                    act(ptbuf[:, 512:512 + wexp], half1[:, 0:wexp], AF.Exp,
                        bias=bias_ap, scale=SCALE, reads=tg1 + (btag,),
                        writes=PTTAG[pu])
            return ptbuf, PTTAG[pu]

        for c in range(8):
            g_ = c // 4
            gl = f"gO{s_}_{c}l"
            gu = f"gO{s_}_{c}u"
            pvq = []

            def pv_flush(keep):
                while len(pvq) > keep:
                    pvq.pop(0)()

            for t0 in range(0, L, 2):
                bias_ap = biastab[:, LOFF[s_] + t0:LOFF[s_] + t0 + 1]
                for hh in range(2):
                    ro = hh * 64
                    tp = (0, 0) if hh == 0 else (64, 0)
                    u = ucnt[0] % 3
                    pu = ucnt[0] % 4
                    ucnt[0] += 1
                    reg, tg0, tg1, contig = SREG[u]
                    half0 = reg[:, 0:512] if contig else psG[0][:, 0:512]
                    half1 = reg[:, 512:1024] if contig else psG[1][:, 0:512]
                    for dt_ in range(2):
                        t = t0 + dt_
                        mm(half0 if dt_ == 0 else half1,
                           ksb[g_][ro:ro + 64, t * 128:(t + 1) * 128],
                           qr[ro:ro + 128 - 64, c * 512:(c + 1) * 512]
                           if False else qr[ro:ro + 64, c * 512:(c + 1) * 512],
                           reads=(f"ksbt{g_}", "qr"),
                           writes=tg0 if dt_ == 0 else tg1,
                           tile_position=tp)
                    ptbuf = pts[pu]
                    if contig:
                        act(ptbuf[:, 0:1024], reg[:], AF.Exp, bias=bias_ap,
                            scale=SCALE, reads=tg0 + tg1 + ("biastab",),
                            writes=PTTAG[pu])
                    else:
                        act(ptbuf[:, 0:512], half0, AF.Exp, bias=bias_ap,
                            scale=SCALE, reads=tg0 + ("biastab",),
                            writes=PTTAG[pu])
                        act(ptbuf[:, 512:1024], half1, AF.Exp, bias=bias_ap,
                            scale=SCALE, reads=tg1 + ("biastab",),
                            writes=PTTAG[pu])
                    og = gl if hh == 0 else gu
                    otag = OL_ if hh == 0 else OU_
                    oc0 = 0 if hh == 0 else 512
                    vg = g_ if hh == 0 else g_ + 2

                    def mkpv(t0=t0, oc0=oc0, vg=vg, og=og, otag=otag,
                             ptbuf=ptbuf, pu=pu):
                        def doit():
                            for dt_ in range(2):
                                t = t0 + dt_
                                mm(psO[0:65, oc0:oc0 + 512],
                                   vsb[:, t * 260 + vg * 65:t * 260 + vg * 65 + 65],
                                   ptbuf[:, dt_ * 512:(dt_ + 1) * 512],
                                   reads=PTTAG[pu] + ("vsbt", "vones"),
                                   writes=(otag,), group=og,
                                   start=(t0 == 0 and dt_ == 0), stop=False)
                        return doit
                    pvq.append(mkpv())
                    pv_flush(2)
                    if s_ > 0:
                        tail_drain(2)
            for kk in range(4):
                w_ = SLOT - kk * 128
                q0 = c * 512 + kk * 128
                for hh in range(2):
                    ro = hh * 64
                    tp = (0, 0) if hh == 0 else (64, 0)
                    u = ucnt[0] % 3
                    pu = ucnt[0] % 4
                    ucnt[0] += 1
                    reg, tg0, tg1, contig = SREG[u]
                    half0 = reg[:, 0:512] if contig else psG[0][:, 0:512]
                    mm(half0[:, 0:w_],
                       kos[g_][ro:ro + 64, kk * 128:(kk + 1) * 128],
                       qr[ro:ro + 64, q0:q0 + w_],
                       reads=(f"kos{g_}", "qr"), writes=tg0, tile_position=tp)
                    ptbuf = pts[pu]
                    act(ptbuf[:, 0:w_], half0[:, 0:w_], AF.Exp, bias=negb[:],
                        scale=SCALE, reads=tg0 + ("negb",), writes=PTTAG[pu])
                    v("tensor_tensor", ptbuf[:, 0:128], ptbuf[:, 0:128], tri[:],
                      op=ALU.mult, reads=PTTAG[pu] + ("tri",), writes=PTTAG[pu])
                    og = gl if hh == 0 else gu
                    otag = OL_ if hh == 0 else OU_
                    oc0 = (0 if hh == 0 else 512) + kk * 128
                    vg = g_ if hh == 0 else g_ + 2

                    def mkpvi(kk=kk, oc0=oc0, hh=hh, vg=vg, og=og, otag=otag,
                              ptbuf=ptbuf, pu=pu, w_=w_):
                        def doit():
                            mm(psO[0:65, oc0:(512 if hh == 0 else 1024)],
                               vos[:, kk * 260 + vg * 65:kk * 260 + vg * 65 + 65],
                               ptbuf[:, 0:w_], reads=PTTAG[pu] + ("vos",),
                               writes=(otag,), group=og, start=False,
                               stop=(kk == 3))
                        return doit
                    pvq.append(mkpvi())
                    pv_flush(2)
                    if s_ > 0:
                        tail_drain(2)
            if s_ > 0:
                tail_drain(10 ** 6)
            arntags = (tuple(f"dq2c{t_}_{k_}" for t_ in range(2) for k_ in range(8))
                       + tuple(f"otn{i}" for i in range(8)))
            pv_flush(0)
            v("tensor_copy", olc[0:65, c * 512:(c + 1) * 512], psO[0:65, 0:512],
              reads=(OL_,), writes=(f"oll{c}", f"olll{c}") + arntags, fence=fn)
            v("tensor_copy", olhi[c % 2][0:65, :], psO[0:65, 512:1024],
              reads=(OU_,), writes=(f"olh{c % 2}",) + (("sca",) if c % 2 else ()))
            st_ = olhi[c % 2]
            dma(lg[c:c + 1, :], olc[64:65, c * 512:(c + 1) * 512],
                reads=(f"olll{c}",), writes=(f"lgl{c}",), q="GP", sem="slg")
            dma(lg[8 + c:9 + c, :], st_[64:65, :], reads=(f"olh{c % 2}", "sca"),
                writes=(f"lgu{c}",), q="GP", sem="slg")
            dma(olc[64:128, c * 512:(c + 1) * 512], st_[0:64, :],
                reads=(f"olh{c % 2}", "sca", f"lgl{c}"),
                writes=(f"olu{c}", f"olll{c}"),
                q="SP", sem=f"ssh{c % 2}")

            if s_ == 0:
                wp_drain(44)

        lreads = (tuple(f"lgl{c}" for c in range(8))
                  + tuple(f"lgu{c}" for c in range(8)))
        v("reciprocal", linv[:], lg[:], reads=lreads, writes=("linv",))
        if s_ == 0:
            wp_drain(1000)

        for c in range(8):
            mm(psG[0][:, 0:512], selrl[:, c * 128:(c + 1) * 128], linv[:],
               reads=("linv", "selrl"), writes=(G0,))
            v("tensor_tensor", otn[:, c * 512:(c + 1) * 512],
              olc[:, c * 512:(c + 1) * 512], psG[0][:, 0:512], op=ALU.mult,
              reads=(f"oll{c}", f"olu{c}", G0),
              writes=(f"otn{c}", f"dq2c0_{c // 2}", f"dq2c1_{c // 2}"))
        for ec in range(8):
            for h_ in range(2):
                dma(woc[h_][:].rearrange("p (f x) -> p f x", x=128),
                    woutr_d.rearrange("p (f e) -> p f e", e=E)
                    [:, h_ * 4:(h_ + 1) * 4, ec * 128:(ec + 1) * 128],
                    reads=(), writes=(f"woc{h_}",), q="GP", fence=fn, sem="woc")
            xb = xoc[ec % 2]
            dma(xb[:], xoT_d.rearrange("(c p) t -> p c t", p=128)
                [:, ec, sc0:sc0 + SLOT],
                reads=(), writes=(f"xoc{ec % 2}",), fence=fn,
                sem=f"xoc{ec % 2}")
            pso, ptag = (psG[1], G1) if ec % 2 == 0 else (psG[0], G0)
            for fc in range(8):
                mm(pso[:, 0:512], woc[fc // 4][:, (fc % 4) * 128:(fc % 4 + 1) * 128],
                   otn[:, fc * 512:(fc + 1) * 512],
                   reads=(f"woc{fc // 4}", f"otn{fc}"), writes=(ptag,),
                   group=f"gat{s_}_{ec}", start=(fc == 0), stop=(fc == 7))
            v("scalar_tensor_tensor", x1[:, ec * 512:(ec + 1) * 512],
              pso[:, 0:512], bout_s[:, ec:ec + 1], xb[:], op0=ALU.add,
              op1=ALU.add, reads=(ptag, "bout", f"xoc{ec % 2}"),
              writes=(f"x1_{ec}",), fence=fn)

        def ffn_A(hs):
            for tt2 in range(2):
                tt = hs * 2 + tt2
                tcol = tt * 128
                po = 1024 * (tt % 2)
                ptg = SL0 if tt % 2 == 0 else SL1
                for fc in range(8):
                    tr(psbfv[:, po + fc * 128:po + (fc + 1) * 128],
                       x1[:, fc * 512 + tcol:fc * 512 + tcol + 128], idb[:],
                       reads=(f"x1_{fc}", "idb"), writes=(ptg,),
                       group=f"gx1t{s_}{tt}")
                v("tensor_copy", x1tm[:], psbfv[:, po:po + 1024], reads=(ptg,),
                  writes=("x1tm",), fence=fn)
                v("bn_stats", small[:, 4:10], x1tm[:, 0:512], reads=("x1tm",),
                  writes=("bn1a",))
                v("bn_stats", small[:, 10:16], x1tm[:, 512:1024],
                  reads=("x1tm",), writes=("bn1b",))
                v("bn_aggr", small[:, 16:18], small[:, 4:16],
                  reads=("bn1a", "bn1b"), writes=("bn1",))
                act(small[:, 18:19], small[:, 17:18], AF.Ln, bias=epscol[:],
                    scale=1.0, reads=("bn1", "epsc"), writes=("r1ln",))
                act(small[:, 18:19], small[:, 18:19], AF.Exp,
                    scale=-0.5, reads=("r1ln",), writes=("r1",))
                v("tensor_scalar", bnb[:, 103:104], small[:, 16:17],
                  small[:, 18:19], None, op0=ALU.mult,
                  reads=("bn1", "r1"), writes=("nmr1",))
                for st in range(2):
                    cu = st * 512
                    act(saa[:], x1tm[:, cu:cu + 512], AF.Copy,
                        scale=small[:, 18:19],
                        reads=("x1tm", "r1"), writes=("saa",),
                        fence=fn)
                    v("scalar_tensor_tensor", sba[:], saa[:], bnb[:, 103:104],
                      g1t[:, cu:cu + 512], op0=ALU.subtract, op1=ALU.mult,
                      reads=("saa", "nmr1", "g1t"), writes=("sba",))
                    v("tensor_tensor", x1tm[:, cu:cu + 512], sba[:],
                      b1t[:, cu:cu + 512], op=ALU.add, reads=("sba", "b1t"),
                      writes=("x1tm",))
                    if st == 0:
                        v("reduce_max", small[:, 19:20], x1tm[:, 0:512],
                          axis=AX.X, apply_absolute_value=True,
                          reads=("x1tm",), writes=("m1",))
                    else:
                        v("reduce_max", small[:, 17:18], x1tm[:, 512:1024],
                          axis=AX.X, apply_absolute_value=True,
                          reads=("x1tm",), writes=("m1t",))
                        v("tensor_tensor", small[:, 19:20], small[:, 19:20],
                          small[:, 17:18], op=ALU.max, reads=("m1", "m1t"),
                          writes=("m1",))
                v("tensor_scalar", small[:, 19:20], small[:, 19:20], 1e-5, None,
                  op0=ALU.max, reads=("m1",), writes=("m1",))
                v("reciprocal", small[:, 20:21], small[:, 19:20], reads=("m1",),
                  writes=("m1inv",))
                v("tensor_scalar", small[:, 21:22], small[:, 20:21], 127.0, None,
                  op0=ALU.mult, reads=("m1inv",), writes=("s1",))
                v("tensor_scalar", small[:, 22 + tt2:23 + tt2], small[:, 19:20],
                  wsc[:, 0:1], 1.0 / 127.0, op0=ALU.mult, op1=ALU.mult,
                  reads=("m1", "wsc"), writes=(f"scl1_{tt2}",))
                v("tensor_scalar", small[:, 24:25], small[:, 20:21], wsc[:, 2:3],
                  127.0, op0=ALU.mult, op1=ALU.mult, reads=("m1inv", "wsc"),
                  writes=("rscl",))
                for st in range(2):
                    cu = st * 512
                    act(sca[:], x1tm[:, cu:cu + 512], AF.Copy,
                        scale=small[:, 21:22], bias=128.0,
                        reads=("x1tm", "s1"), writes=("sca",), fence=fn)
                    v("tensor_copy", sda[:], sca[:], reads=("sca",),
                      writes=("sda",), fence=fn)
                    v("scalar_tensor_tensor", sba[:], sca[:], 0.5, sda[:],
                      op0=ALU.subtract, op1=ALU.is_gt, reads=("sca", "sda"),
                      writes=("sba",), fence=fn)
                    v("tensor_tensor", saa[:], sda[:], sba[:], op=ALU.add,
                      reads=("sda", "sba"), writes=("saa",), fence=fn)
                    po = 1024 * (st % 2)
                    ptg = SL0 if st % 2 == 0 else SL1
                    for fc in range(4):
                        tr(psbfv[:, po + fc * 128:po + (fc + 1) * 128],
                           saa[:, fc * 128:(fc + 1) * 128], idb[:],
                           reads=("saa", "idb"), writes=(ptg,),
                           group=f"gd1{s_}{tt}{st}")
                    v("tensor_copy",
                      dq1t[:, tt2 * 1024 + cu:tt2 * 1024 + cu + 512],
                      psbfv[:, po:po + 512], reads=(ptg,),
                      writes=(f"dq1t{tt2}", "tzq0", "tzq1", "t1cq0", "t1cq1"),
                      fence=fn)
                tr(psG[0][0:1, 0:128], small[:, 24:25], idf[:],
                   reads=("rscl", "idf"), writes=(G0,))
                v("tensor_copy", k3lhs[0:1, tt2 * 128:(tt2 + 1) * 128],
                  psG[0][0:1, 0:128], reads=(G0, "k3l12"),
                  writes=(f"k3l0_{tt2}",))

        def ffn_W1_hb(hs, hb):
            slab = w1slab[hb % 2]
            dma(slab[:], tern1b[hb],
                reads=(f"t1d{hb}",), writes=(f"w1s{hb % 2}",), q="GP",
                sem=f"w1s{hb % 2}")
            for tt2 in range(2):
                ph, ptag = (psG[0], G0) if (hb * 2 + tt2) % 2 == 0 else (psG[1], G1)
                gtag = f"gh{s_}{hs}{hb}{tt2}"
                for fc in range(8):
                    mm(ph[:, 0:512],
                       dq1t[:, tt2 * 1024 + fc * 128:tt2 * 1024 + (fc + 1) * 128],
                       slab[:, fc * 512:(fc + 1) * 512],
                       reads=(f"dq1t{tt2}", f"w1s{hb % 2}"), writes=(ptag,),
                       group=gtag, start=(fc == 0), stop=False)
                mm(ph[:, 0:512], k3lhs[0:3, tt2 * 128:(tt2 + 1) * 128],
                   k3rhs[0:3, hb * 512:(hb + 1) * 512],
                   reads=(f"k3l0_{tt2}", "k3l12", "k3r0", "k3r1", "k3r2"),
                   writes=(ptag,), group=gtag, start=False, stop=True)
                act(hsb[:, tt2 * HID + hb * 512:tt2 * HID + (hb + 1) * 512],
                    ph[:, 0:512], AF.Gelu, scale=small[:, 22 + tt2:23 + tt2],
                    reads=(ptag, f"scl1_{tt2}"),
                    writes=(f"h{tt2}_{hb}", f"u2_{hb}",
                            f"wst{tt2}a", f"wst{tt2}b"))
                v("bn_stats", bnb[:, tt2 * 52 + 6 * hb:tt2 * 52 + 6 * hb + 6],
                  hsb[:, tt2 * HID + hb * 512:tt2 * HID + (hb + 1) * 512],
                  reads=(f"h{tt2}_{hb}",), writes=(f"bn2_{tt2}_{hb}",))

        def ffn_L(hs, tt2):
            hreads = tuple(f"h{tt2}_{i}" for i in range(8))
            b0 = tt2 * 52
            s2c = 30 if tt2 == 0 else 25
            v("bn_aggr", bnb[:, b0 + 48:b0 + 50], bnb[:, b0:b0 + 48],
              reads=tuple(f"bn2_{tt2}_{i}" for i in range(8)),
              writes=(f"bn2s{tt2}",))
            act(small[:, 26:27], bnb[:, b0 + 49:b0 + 50], AF.Ln, bias=epscol[:],
                scale=1.0, reads=(f"bn2s{tt2}", "epsc"), writes=("r2ln",))
            act(small[:, 26:27], small[:, 26:27], AF.Exp,
                scale=-0.5, reads=("r2ln",), writes=("r2",))
            v("tensor_scalar", bnb[:, b0 + 50:b0 + 51], bnb[:, b0 + 48:b0 + 49],
              small[:, 26:27], None, op0=ALU.mult,
              reads=(f"bn2s{tt2}", "r2"), writes=(f"nmr2_{tt2}",))
            yield
            for k_ in range(8):
                col = tt2 * HID + 512 * k_
                act(saa[:], hsb[:, col:col + 512], AF.Copy,
                    scale=small[:, 26:27],
                    reads=(f"h{tt2}_{k_}", "r2"), writes=("saa",))
                v("scalar_tensor_tensor", sba[:], saa[:], bnb[:, b0 + 50:b0 + 51],
                  g2t[:, 512 * k_:512 * (k_ + 1)], op0=ALU.subtract,
                  op1=ALU.mult,
                  reads=("saa", f"nmr2_{tt2}", "g2t"), writes=("sba",))
                v("tensor_tensor", hsb[:, col:col + 512], sba[:],
                  b2t[:, 512 * k_:512 * (k_ + 1)], op=ALU.add,
                  reads=("sba", "b2t"), writes=(f"u2_{k_}", f"h{tt2}_{k_}"))
                if k_ == 0:
                    v("reduce_max", small[:, 27:28], hsb[:, col:col + 512],
                      axis=AX.X, apply_absolute_value=True,
                      reads=(f"u2_{k_}",), writes=("m2",))
                else:
                    v("reduce_max", small[:, 28:29], hsb[:, col:col + 512],
                      axis=AX.X, apply_absolute_value=True,
                      reads=(f"u2_{k_}",), writes=("m2t",))
                    v("tensor_tensor", small[:, 27:28], small[:, 27:28],
                      small[:, 28:29], op=ALU.max, reads=("m2", "m2t"),
                      writes=("m2",))
                yield
            v("tensor_scalar", small[:, 27:28], small[:, 27:28], 1e-5, None,
              op0=ALU.max, reads=("m2",), writes=("m2",))
            v("reciprocal", small[:, 29:30], small[:, 27:28], reads=("m2",),
              writes=("m2inv",))
            v("tensor_scalar", small[:, s2c:s2c + 1], small[:, 29:30], 127.0,
              None, op0=ALU.mult, reads=("m2inv",), writes=(f"s2_{tt2}",))
            v("tensor_scalar", small[:, 31:32], small[:, 27:28], wsc[:, 1:2],
              1.0 / 127.0, op0=ALU.mult, op1=ALU.mult, reads=("m2", "wsc"),
              writes=("scl2",))
            tr(psG[0][0:1, 0:128], small[:, 31:32], idf[:],
               reads=("scl2", "idf"), writes=(G0,))
            v("tensor_copy", rows2[0:1, tt2 * 128:(tt2 + 1) * 128],
              psG[0][0:1, 0:128], reads=(G0,), writes=(f"r2row{tt2}",))

        psgbf = [psG[0][:].bitcast(BF16), psG[1][:].bitcast(BF16)]

        def ffn_Q2(hs, k_, qb=None):
            scb, sct, sdb, sdt = qb if qb is not None else \
                (sca[:], "sca", sda[:], "sda")
            for tt2 in range(2):
                col = tt2 * HID + 512 * k_
                s2c = 30 if tt2 == 0 else 25
                act(scb, hsb[:, col:col + 512], AF.Copy,
                    scale=small[:, s2c:s2c + 1], bias=128.0,
                    reads=(f"u2_{k_}", f"s2_{tt2}"), writes=(sct,))
                v("tensor_copy", sdb, scb, reads=(sct,),
                  writes=(sdt,))
                v("scalar_tensor_tensor", sba[:], scb, 0.5, sdb,
                  op0=ALU.subtract, op1=ALU.is_gt, reads=(sct, sdt),
                  writes=("sba",))
                v("tensor_tensor", saa[:], sdb, sba[:], op=ALU.add,
                  reads=(sdt, "sba"), writes=("saa",))
                if qb is None:
                    pbv = psbfv
                    po = 1024 * (tt2 % 2)
                    ptg = SL0 if tt2 % 2 == 0 else SL1
                else:
                    pbv = psgbf[tt2]
                    po = 0
                    ptg = G0 if tt2 == 0 else G1
                for i in range(4):
                    tr(pbv[:, po + i * 128:po + (i + 1) * 128],
                       saa[:, i * 128:(i + 1) * 128], idb[:],
                       reads=("saa", "idb"), writes=(ptg,),
                       group=f"gd2{s_}{hs}{tt2}{k_}")
                hclo = 4 * k_
                if hclo < 16:
                    ovl = (f"otn{hclo // 2}", f"otn{hclo // 2 + 1}")
                else:
                    cc = (hclo - 16) // 2
                    ovl = (f"oll{cc}", f"olu{cc}", f"olll{cc}",
                           f"oll{cc + 1}", f"olu{cc + 1}", f"olll{cc + 1}")
                v("tensor_copy",
                  dq2t[:].rearrange("p (hc t) -> p hc t", t=256)
                  [:, 4 * k_:4 * (k_ + 1), tt2 * 128:(tt2 + 1) * 128],
                  pbv[:, po:po + 512].rearrange("p (i t) -> p i t", t=128),
                  reads=(ptg,), writes=(f"dq2c{tt2}_{k_}",) + ovl)

        def ffn_W2(hs, s_=s_, sc0=sc0):
            mm(psG[1][:, 0:256], ones_row, rows2[0:1, 0:256],
               reads=("r2row0", "r2row1", "onesc"), writes=(G1,))
            v("tensor_copy", scl2t[:], psG[1][:, 0:256], reads=(G1,),
              writes=("scl2t",))
            yield
            for ec in range(8):
                pso, ptag = (psG[0], G0) if ec % 2 == 0 else (psG[1], G1)
                gtag = f"go{s_}{hs}{ec}"
                for hc in range(32):
                    mm(pso[:, 0:256],
                       w2t[:, hc * E + ec * 128:hc * E + (ec + 1) * 128],
                       dq2t[:, hc * 256:(hc + 1) * 256],
                       reads=(f"t2_{hc}", f"dq2c0_{hc // 4}", f"dq2c1_{hc // 4}"),
                       writes=(ptag,), group=gtag, start=(hc == 0), stop=False)
                mm(pso[:, 0:256], c2pair[0:2, ec * 128:(ec + 1) * 128],
                   n128[0:2, 0:256], reads=("c2hi", "c2lo", "n128"),
                   writes=(ptag,), group=gtag, start=False, stop=True)
                ob = outst[ec % 2]
                v("scalar_tensor_tensor", ob[:], pso[:, 0:256], 1.0, scl2t[:],
                  op0=ALU.mult, op1=ALU.mult, reads=(ptag, "scl2t"),
                  writes=(f"ost{ec % 2}",))
                v("scalar_tensor_tensor", ob[:], ob[:], b2_s[:, ec:ec + 1],
                  x1[:, ec * 512 + hs * 256:ec * 512 + (hs + 1) * 256],
                  op0=ALU.add, op1=ALU.add,
                  reads=(f"ost{ec % 2}", "b2", f"x1_{ec}"),
                  writes=(f"ost{ec % 2}",))
                dma(yT_d.rearrange("(c p) t -> p c t", p=128)
                    [:, ec, sc0 + hs * 256:sc0 + (hs + 1) * 256], ob[:],
                    reads=(f"ost{ec % 2}",), writes=(f"od{s_}{hs}{ec}",),
                    sem=f"sout{ec % 2}")
                yield

        ffn_A(0)
        for hb in range(8):
            ffn_W1_hb(0, hb)
        for _ in ffn_L(0, 0):
            pass
        for _ in ffn_L(0, 1):
            pass
        ffn_A(1)
        for k_ in range(8):
            ffn_Q2(0, k_)
            ffn_W1_hb(1, k_)
        for _ in ffn_W2(0):
            pass

        def ffn_tail(fL=ffn_L, fQ2=ffn_Q2, fW2=ffn_W2):
            qb = (xoc[0][:], "xoc0", xoc[1][:].bitcast(I32), "xoc1")
            yield from fL(1, 0)
            yield from fL(1, 1)
            for k_ in range(8):
                fQ2(1, k_, qb=qb)
                yield
            yield from fW2(1)

        if s_ == 3:
            for _ in ffn_tail():
                pass
        else:
            tail_pend[0] = ffn_tail()

    p.fence("final")
    p.op("SP", lambda eng: eng.nop(), fence="final")

    def replay(engkey):
        def fn(eng):
            for waits, f_, dsemname in p.items[engkey]:
                for (semkey, val) in waits:
                    if semkey[0] == "d":
                        dsem(semkey[1])
                    eng.wait_ge(sems[semkey], val)
                try:
                    inst = f_(eng)
                except Exception as e:
                    raise RuntimeError(f"replay {engkey} line "
                                       f"{getattr(f_, '_line', '?')}: {e}") from e
                if dsemname is not None:
                    inst.then_inc(dsem(dsemname), 16)
                else:
                    inst.then_inc(sems[("e", engkey)], 1)
        return fn

    for _sname in list(p.dcnt.keys()):
        dsem(_sname)
    with nc.allow_low_precision(reason="bf16 intermediates are intentional; "
                                "tolerance calibrated for bf16 matmuls"), \
         nc.Block() as blk:
        blk.tensor(replay("PE"))
        blk.scalar(replay("ACT"))
        blk.vector(replay("DVE"))
        blk.sync(replay("SP"))
        blk.gpsimd(replay("GP"))

    return nc, p


def _host_tables():
    inv = 1.0 / (10000.0 ** (np.arange(0, D, 2, dtype=np.float64) / D))
    t = np.arange(S, dtype=np.float64)
    fr = np.outer(inv, t)
    cos, sin = np.cos(fr), np.sin(fr)
    cs = np.ascontiguousarray(np.concatenate([cos, cos, cos, cos], 0).astype(BFNP))
    sc = np.ascontiguousarray(np.concatenate([-sin, sin, -sin, sin], 0).astype(BFNP))

    tri = np.ascontiguousarray(
        (np.arange(128)[:, None] <= np.arange(128)[None, :]).astype(BFNP))
    idb = np.ascontiguousarray(np.eye(128, dtype=BFNP))
    idf = np.ascontiguousarray(np.eye(128, dtype=np.float32))

    selx = np.zeros((8, 128, 8), np.float32)
    for b_ in range(8):
        selx[b_, :, b_] = 1.0
    selxo = np.zeros((4, 128, 4), np.float32)
    for b_ in range(4):
        selxo[b_, :, b_] = 1.0
    selr8 = np.zeros((8, 8 * 32), np.float32)
    for b_ in range(8):
        selr8[b_, b_ * 32:(b_ + 1) * 32] = 1.0
    selr4 = np.zeros((4, 4 * 64), np.float32)
    for b_ in range(4):
        selr4[b_, b_ * 64:(b_ + 1) * 64] = 1.0
    selrl = np.zeros((16, 8, 128), np.float32)
    for c in range(8):
        selrl[c, c, 0:64] = 1.0
        selrl[c + 8, c, 64:128] = 1.0
    halfsel = np.zeros((128, 2), np.float32)
    halfsel[0:64, 0] = 1.0
    halfsel[64:128, 1] = 1.0

    qperm = np.zeros(E, np.int64)
    qpermB = np.zeros(E, np.int64)
    DPART = np.concatenate([np.arange(1, 64, 2), np.arange(0, 64, 2)])
    for c in range(8):
        for r in range(128):
            h_ = c if r < 64 else c + 8
            qperm[c * 128 + r] = h_ * 64 + DPERM[r % 64]
            qpermB[c * 128 + r] = h_ * 64 + DPART[r % 64]
    kperm = np.zeros(256, np.int64)
    kpermB = np.zeros(256, np.int64)
    for g_ in range(2):
        for r in range(128):
            kvh = g_ if r < 64 else g_ + 2
            kperm[g_ * 128 + r] = kvh * 128 + DPERM[r % 64] * 2
            kpermB[g_ * 128 + r] = kvh * 128 + DPART[r % 64] * 2
    vperm = np.zeros(256, np.int64)
    for vh in range(4):
        for d_ in range(64):
            vperm[vh * 64 + d_] = vh * 128 + d_ * 2 + 1
    wrow = np.concatenate(
        [np.concatenate([np.arange(fc * 64, fc * 64 + 64),
                         np.arange((fc + 8) * 64, (fc + 8) * 64 + 64)])
         for fc in range(8)])
    return dict(
        cs=cs, sc=sc, tri=tri, idb=idb, idf=idf,
        selx=np.ascontiguousarray(selx.transpose(1, 0, 2).reshape(128, 64)
                                  .astype(BFNP)),
        selxo=np.ascontiguousarray(selxo.transpose(1, 0, 2).reshape(128, 16)
                                   .astype(BFNP)),
        selr8=selr8, selr4=selr4,
        selrl=np.ascontiguousarray(selrl.transpose(1, 0, 2).reshape(16, 1024)),
        halfsel=halfsel, qperm=qperm, qpermB=qpermB, kperm=kperm,
        kpermB=kpermB, vperm=vperm, wrow=wrow)


_NC_CACHE = {}


def kernel(L0_attn_x, L0_attn_Wqa, L0_attn_Wqb, L0_attn_Wkva, L0_attn_Wkvb,
           L0_attn_Wout, L0_attn_bout, L0_ffn_ln1_g, L0_ffn_ln1_b, L0_ffn_w1,
           L0_ffn_b1, L0_ffn_ln2_g, L0_ffn_ln2_b, L0_ffn_w2, L0_ffn_b2):
    T = _host_tables()
    x = np.asarray(L0_attn_x, np.float32)

    wqb_f = np.asarray(L0_attn_Wqb, np.float32)
    wqbp = np.ascontiguousarray(wqb_f[:, T["qperm"]])
    wqbpB = np.ascontiguousarray(wqb_f[:, T["qpermB"]])
    wkvb = np.asarray(L0_attn_Wkvb, np.float32)
    wkvbk = np.ascontiguousarray(wkvb[:, T["kperm"]])
    wkvbkB = np.ascontiguousarray(wkvb[:, T["kpermB"]])
    wkvbv = np.ascontiguousarray(wkvb[:, T["vperm"]])
    wout = np.asarray(L0_attn_Wout, np.float32)[T["wrow"]]
    woutr = np.ascontiguousarray(wout.reshape(8, 128, E).transpose(1, 0, 2)
                                 .reshape(128, 8 * E))
    bout_c = np.ascontiguousarray(np.asarray(L0_attn_bout, np.float32)
                                  .reshape(8, 128).T)
    b2_c = np.ascontiguousarray(np.asarray(L0_ffn_b2, np.float32)
                                .reshape(8, 128).T)
    w1T = np.ascontiguousarray(np.asarray(L0_ffn_w1, np.float32).T)
    w2T = np.ascontiguousarray(np.asarray(L0_ffn_w2, np.float32).T)

    shared = {
        "wqa": np.ascontiguousarray(np.asarray(L0_attn_Wqa, np.float32)),
        "wqbp": wqbp,
        "wkva": np.ascontiguousarray(np.asarray(L0_attn_Wkva, np.float32)),
        "wkvbk": wkvbk, "wkvbkB": wkvbkB, "wqbpB": wqbpB,
        "woutr": woutr, "wkvbv": wkvbv,
        "bout_c": bout_c, "b2_c": b2_c,
        "b1row": np.asarray(L0_ffn_b1, np.float32).reshape(1, HID),
        "ln1g": np.asarray(L0_ffn_ln1_g, np.float32).reshape(1, E),
        "ln1b": np.asarray(L0_ffn_ln1_b, np.float32).reshape(1, E),
        "ln2g": np.asarray(L0_ffn_ln2_g, np.float32).reshape(1, HID),
        "ln2b": np.asarray(L0_ffn_ln2_b, np.float32).reshape(1, HID),
        "w1T": w1T, "w2T": w2T,
        "cs_full": T["cs"], "sc_full": T["sc"], "tri": T["tri"],
        "idb": T["idb"], "idf": T["idf"],
        "selx": T["selx"], "selxo": T["selxo"],
        "selr8": T["selr8"], "selr4": T["selr4"],
        "selrl": T["selrl"], "halfsel": T["halfsel"],
    }

    in_maps = []
    for core in range(8):
        b_ = core // 2
        js = JSETS[core % 2]
        cols = np.concatenate([np.arange(512 * j, 512 * j + 512) for j in js])
        kill = np.zeros((1, NL), np.float32)
        for s_, j in enumerate(js):
            kill[0, LOFF[s_] + 4 * j:LOFF[s_] + LPAD[s_]] = KILL
        m = dict(shared)
        m["xT"] = np.ascontiguousarray(x[b_].T)
        m["xoT"] = np.ascontiguousarray(x[b_][cols].T)
        m["cs_own"] = np.ascontiguousarray(T["cs"][:, cols])
        m["sc_own"] = np.ascontiguousarray(T["sc"][:, cols])
        m["killrow"] = kill
        in_maps.append(m)

    if "nc" not in _NC_CACHE:
        _NC_CACHE["nc"] = build_program()[0]
    nc = _NC_CACHE["nc"]
    res = run_bass_kernel_spmd(nc, in_maps, core_ids=list(range(8)))

    out = np.empty((B, S, E), np.float32)
    for core in range(8):
        yv = np.asarray(res.results[core]["yT"])
        b_ = core // 2
        js = JSETS[core % 2]
        for s_, j in enumerate(js):
            out[b_, 512 * j:512 * j + 512, :] = yv[:, 512 * s_:512 * (s_ + 1)].T
    return out


if __name__ == "__main__":
    nc, p = build_program()
    n_inst = sum(1 for _ in nc.all_instructions())
    print("built ok, instructions:", n_inst)
    print("op counts:", {e: p.cnt[e] for e in Prog.ENGS},
          {s: v // 16 for s, v in p.dcnt.items()})
